# version 06
# baseline (speedup 1.0000x reference)
import numpy as np
import ml_dtypes
import concourse.bass as bass
import concourse.mybir as mybir
from concourse.bass_utils import run_bass_kernel_spmd
from contextlib import ExitStack

F32 = mybir.dt.float32
BF16 = mybir.dt.bfloat16
AF = mybir.ActivationFunctionType
ALU = mybir.AluOpType
AX = mybir.AxisListType

B, S, D = 4, 8192, 2048
NCH = D // 128
H_FFN = 5632
NH = 16
EH = 128
CHUNK = 128
TT = 512
SH = S // 2
NTI = SH // TT
WARM = TT

f32 = np.float32


class Prog:
    ENGS = ("sync", "gpsimd", "vector", "scalar", "tensor")

    def __init__(self):
        self.ops = {e: [] for e in self.ENGS}
        self.last = {}

    def emit(self, eng, fn):
        self.ops[eng].append(fn)

    def run(self, block):
        ops = self.ops

        @block.sync
        def _(e):
            for f in ops["sync"]:
                f(e)

        @block.gpsimd
        def _(e):
            for f in ops["gpsimd"]:
                f(e)

        @block.vector
        def _(e):
            for f in ops["vector"]:
                f(e)

        @block.scalar
        def _(e):
            for f in ops["scalar"]:
                f(e)

        @block.tensor
        def _(e):
            for f in ops["tensor"]:
                f(e)


class Sem:
    def __init__(self, ctx, nc, name):
        self.h = ctx.enter_context(nc.semaphore(name))
        self.n = 0

    def bump(self, k=1):
        self.n += k
        return self.n


AUTOCHAIN = ("vector", "scalar")


def inc(prog, eng, sem, build, k=1):
    if eng in AUTOCHAIN:
        last = prog.last.get(eng)
        if last is not None:
            wait(prog, eng, last[0], last[1])
    v = sem.bump(k)
    h = sem.h

    def f(e, build=build, h=h, k=k):
        build(e).then_inc(h, k)
    prog.emit(eng, f)
    if eng in AUTOCHAIN:
        prog.last[eng] = (sem, v)
    return v


def wait(prog, eng, sem, v):
    h = sem.h
    prog.emit(eng, lambda e, h=h, v=v: e.wait_ge(h, v))


def run_op(prog, eng, build, prog_sems=None):
    assert eng not in AUTOCHAIN, "vector/scalar ops must go through inc()"
    prog.emit(eng, build)


def ln_tile_emit(prog, sems, xr, xb16, xln, xxsc, t1sc, ps, ones_b, onesf, rows, ncols=TT,
                 off=(0, 512, 1024, 1536)):
    sv, st, sc = sems["v"], sems["t"], sems["c"]
    s1 = ps[0:1, off[0]:off[0] + ncols]
    s2 = ps[0:1, off[1]:off[1] + ncols]
    bcm = ps[:, off[2]:off[2] + ncols]
    bcr = ps[:, off[3]:off[3] + ncols]
    mrow = rows[0:1, 0:ncols]
    msq = rows[0:1, ncols:2 * ncols]
    var = rows[0:1, 2 * ncols:3 * ncols]
    srow = rows[0:1, 3 * ncols:4 * ncols]
    rrow = rows[0:1, 4 * ncols:5 * ncols]
    xx_t = {}

    v_cast = inc(prog, "vector", sv, lambda e: e.tensor_copy(xb16[:, 0:NCH * ncols], xr[:, 0:NCH * ncols]))
    wait(prog, "tensor", sv, v_cast)
    for j in range(NCH):
        run_op(prog, "tensor", lambda e, a=s1, b=ones_b, c=xb16[:, j * ncols:(j + 1) * ncols], j=j:
               e.matmul(a, b[:, 0:1], c, start=(j == 0), stop=(j == NCH - 1)))
    for j in range(NCH):
        xsl = xb16[:, j * ncols:(j + 1) * ncols]
        xxj = xxsc[:, (j % 2) * ncols:((j % 2) + 1) * ncols]
        if j >= 2:
            wait(prog, "vector", st, xx_t[j - 2])
        v_sq = inc(prog, "vector", sv, lambda e, a=xxj, b=xsl: e.tensor_mul(a, b, b))
        wait(prog, "tensor", sv, v_sq)
        xx_t[j] = inc(prog, "tensor", st, lambda e, a=s2, b=ones_b, c=xxj, j=j:
                      e.matmul(a, b[:, 0:1], c, start=(j == 0), stop=(j == NCH - 1)))
    wait(prog, "vector", st, xx_t[NCH - 1])
    inc(prog, "vector", sv, lambda e: e.tensor_scalar(mrow, s1, 1.0 / D, None, ALU.mult))
    inc(prog, "vector", sv, lambda e: e.tensor_mul(msq, mrow, mrow))
    v_var = inc(prog, "vector", sv, lambda e: e.scalar_tensor_tensor(var, s2, 1.0 / D, msq, ALU.mult, ALU.subtract))
    wait(prog, "scalar", sv, v_var)
    c_sqrt = inc(prog, "scalar", sc, lambda e: e.activation(srow, var, AF.Sqrt, bias=np.float32(1e-5), scale=np.float32(1.0)))
    wait(prog, "vector", sc, c_sqrt)
    v_rec = inc(prog, "vector", sv, lambda e: e.reciprocal(rrow, srow))
    wait(prog, "tensor", sv, v_rec)
    run_op(prog, "tensor", lambda e: e.matmul(bcm, onesf[0:1, :], mrow, start=True, stop=True))
    t_bc = inc(prog, "tensor", st, lambda e: e.matmul(bcr, onesf[0:1, :], rrow, start=True, stop=True))
    wait(prog, "vector", st, t_bc)
    v_last = None
    for j in range(NCH):
        inc(prog, "vector", sv, lambda e, a=t1sc[:, 0:ncols], x=xr[:, j * ncols:(j + 1) * ncols], m=bcm:
            e.tensor_tensor(a, x, m, ALU.subtract))
        v_last = inc(prog, "vector", sv, lambda e, o=xln[:, j * ncols:(j + 1) * ncols], a=t1sc[:, 0:ncols], r=bcr:
                     e.tensor_tensor(o, a, r, ALU.mult))
    return v_last


def mm_group_emit(prog, sems, wdram, kdim, mcol0, mg, rhs16, ps, psbank0, wbuf, evac=None,
                  pre_mm_wait=None, ncols=TT):
    sv, st = sems["v"], sems["t"]
    nk = kdim // 128
    assert nk % 4 == 0 and 1 <= mg <= 8
    mcw = mg * 128
    t_last = None
    for kg in range(nk // 4):
        buf = wbuf[kg % 2]
        swp = sems["wp"][kg % 2]
        if sems["_wmm"][kg % 2] is not None:
            wait(prog, "gpsimd", st, sems["_wmm"][kg % 2])
        src = wdram.rearrange("(a p) m -> p a m", p=128)[:, 4 * kg:4 * (kg + 1),
                                                        mcol0:mcol0 + mcw]
        w_ld = inc(prog, "gpsimd", swp, lambda e, b=buf[:, 0:4 * mcw], s=src:
                   e.dma_start(out=b.rearrange("p (a m) -> p a m", a=4), in_=s), 16)
        wait(prog, "tensor", swp, w_ld)
        if pre_mm_wait is not None:
            wait(prog, "tensor", pre_mm_wait[0], pre_mm_wait[1])
            pre_mm_wait = None
        for kl in range(4):
            ka = kg * 4 + kl
            for j in range(mg):
                bld = (lambda e, p=ps[:, (psbank0 + j) * 512:(psbank0 + j) * 512 + ncols],
                       w=buf[:, kl * mcw + j * 128: kl * mcw + (j + 1) * 128],
                       r=rhs16[:, ka * ncols:(ka + 1) * ncols],
                       s=(ka == 0), z=(ka == nk - 1):
                       e.matmul(p, w, r, start=s, stop=z))
                if kl == 3 and j == mg - 1:
                    t_last = inc(prog, "tensor", st, bld)
                    sems["_wmm"][kg % 2] = t_last
                else:
                    run_op(prog, "tensor", bld)
    if evac is not None:
        wait(prog, "vector", st, t_last)
        for j in range(mg):
            evac(j, ps[:, (psbank0 + j) * 512:(psbank0 + j) * 512 + ncols])
    return t_last


def mm_streamed_emit(prog, sems, wdram, kdim, mdim, rhs16, ps, wbuf, evac,
                     pre_mm_wait=None, ncols=TT):
    sv = sems["v"]
    nm = mdim // 128
    assert nm % 4 == 0
    ev_done = [None, None]
    for mq in range(nm // 4):
        par = mq % 2
        def evac4(j, psl, mq=mq):
            evac(mq * 4 + j, psl)
        pw = pre_mm_wait if mq <= 1 else ev_done[par]
        mm_group_emit(prog, sems, wdram, kdim, 512 * mq, 4, rhs16, ps, par * 4, wbuf,
                      evac=evac4, pre_mm_wait=pw, ncols=ncols)
        ev_done[par] = (sv, sv.n)


def build_launchA():
    nc = bass.Bass("TRN2", target_bir_lowering=False)
    NT = NTI + 1
    xin = nc.dram_tensor("A_x", [D, WARM + SH], F32, kind="ExternalInput").ap()
    winT = nc.dram_tensor("A_winT", [D, D], F32, kind="ExternalInput").ap()
    woutT = nc.dram_tensor("A_woutT", [D, D], F32, kind="ExternalInput").ap()
    dcol_d = nc.dram_tensor("A_dcol", [128, NCH], F32, kind="ExternalInput").ap()
    bcol_d = nc.dram_tensor("A_bcol", [128, NCH], F32, kind="ExternalInput").ap()
    onesf_d = nc.dram_tensor("A_onesf", [1, 128], F32, kind="ExternalInput").ap()
    onesb_d = nc.dram_tensor("A_onesb", [128, 1], BF16, kind="ExternalInput").ap()
    xout = nc.dram_tensor("A_out", [D, SH], F32, kind="ExternalOutput").ap()

    with ExitStack() as ctx:
        sb = lambda n, s, d: ctx.enter_context(nc.sbuf_tensor(n, s, d))
        xr = sb("A_xr", [128, NCH * TT], F32)
        xb16 = sb("A_xb16", [128, NCH * TT], BF16)
        xln = sb("A_xln", [128, NCH * TT], BF16)
        q = sb("A_q", [128, NCH * TT], F32)
        stk = sb("A_stk", [128, NCH * TT], F32)
        stb = sb("A_stb", [128, NCH * TT], BF16)
        xxsc = sb("A_xxsc", [128, 2 * TT], BF16)
        t1sc = sb("A_t1sc", [128, TT], F32)
        wbuf = [sb(f"A_wb{i}", [128, 4 * 1024], BF16) for i in range(2)]
        dcol = sb("A_dc", [128, NCH], F32)
        bcol = sb("A_bc", [128, NCH], F32)
        dsig = sb("A_ds", [128, NCH], F32)
        omd = sb("A_om", [128, NCH], F32)
        stcol = sb("A_st", [128, NCH], F32)
        ones_b = sb("A_o1", [128, 1], BF16)
        onesf = sb("A_o2", [1, 128], F32)
        rows = sb("A_rows", [1, 5 * 512], F32)
        ps = ctx.enter_context(nc.psum_tensor("A_ps", [128, 4096], F32))

        sems = {k: Sem(ctx, nc, "A_s" + k) for k in ("x", "w", "v", "t", "c", "o", "k")}
        sems["wp"] = [Sem(ctx, nc, "A_swp0"), Sem(ctx, nc, "A_swp1")]
        sems["_wmm"] = [None, None]
        block = ctx.enter_context(nc.Block())
        prog = Prog()
        sx, sw, sv, st, sc, so = (sems[k] for k in ("x", "w", "v", "t", "c", "o"))

        sk = sems["k"]
        inc(prog, "sync", sk, lambda e: e.dma_start(out=ones_b[:], in_=onesb_d[:]), 16)
        inc(prog, "sync", sk, lambda e: e.dma_start(out=onesf[:], in_=onesf_d[:]), 16)
        inc(prog, "sync", sk, lambda e: e.dma_start(out=dcol[:], in_=dcol_d[:]), 16)
        x0 = inc(prog, "sync", sk, lambda e: e.dma_start(out=bcol[:], in_=bcol_d[:]), 16)
        wait(prog, "scalar", sk, x0)
        wait(prog, "tensor", sk, x0)
        wait(prog, "vector", sk, x0)
        c0 = inc(prog, "scalar", sc, lambda e: e.activation(dsig[:], dcol[:], AF.Sigmoid))
        wait(prog, "vector", sc, c0)
        inc(prog, "vector", sv, lambda e: e.tensor_scalar(omd[:], dsig[:], -1.0, 1.0, ALU.mult, ALU.add))

        v_xr_free = None
        for i in range(NT):
            warm = (i == 0)
            c0_ = i * TT if warm else WARM + (i - 1) * TT
            c1_ = c0_ + TT
            if v_xr_free is not None:
                wait(prog, "sync", sv, v_xr_free)
            src = xin.rearrange("(j p) t -> p j t", p=128)[:, :, c0_:c1_]
            x_ld = inc(prog, "sync", sx, lambda e, s=src:
                       e.dma_start(out=xr.ap().rearrange("p (j t) -> p j t", j=NCH), in_=s), 16)
            wait(prog, "vector", sx, x_ld)
            if not warm:
                wait(prog, "vector", so, so.n)
            v_ln = ln_tile_emit(prog, sems, xr, xb16, xln, xxsc, t1sc, ps, ones_b, onesf, rows)

            v_ev = [None]

            def evac_q(j_abs, psl):
                v_ev[0] = inc(prog, "vector", sv,
                              lambda e, o=q[:, j_abs * TT:(j_abs + 1) * TT], p=psl,
                              s=omd[:, j_abs:j_abs + 1]:
                              e.tensor_scalar(o, p, s, None, ALU.mult))
            mm_streamed_emit(prog, sems, winT, D, D, xln, ps, wbuf, evac_q,
                             pre_mm_wait=(sv, v_ln))
            for j in range(NCH):
                init = 0.0 if warm else stcol[:, j:j + 1]
                dbc = dsig[:, j:j + 1].broadcast_to([128, TT])
                inc(prog, "vector", sv,
                    lambda e, o=stk[:, j * TT:(j + 1) * TT], d=dbc,
                    qq=q[:, j * TT:(j + 1) * TT], ii=init:
                    e.tensor_tensor_scan(o, d, qq, ii, ALU.mult, ALU.add))
            v_scan = inc(prog, "vector", sv, lambda e: e.tensor_copy(stcol[:], stk[:, TT - 1::TT]))
            if warm:
                v_xr_free = v_scan
                continue
            v_stb = inc(prog, "vector", sv, lambda e: e.tensor_copy(stb[:], stk[:]))

            def evac_o(j_abs, psl):
                v_ev[0] = inc(prog, "vector", sv,
                              lambda e, o=q[:, j_abs * TT:(j_abs + 1) * TT], p=psl,
                              s=bcol[:, j_abs:j_abs + 1], x=xr[:, j_abs * TT:(j_abs + 1) * TT]:
                              e.scalar_tensor_tensor(o, p, s, x, ALU.add, ALU.add))
            mm_streamed_emit(prog, sems, woutT, D, D, stb, ps, wbuf, evac_o,
                             pre_mm_wait=(sv, v_stb))
            v_xr_free = v_ev[0]
            wait(prog, "sync", sv, v_xr_free)
            dst = xout.rearrange("(j p) t -> p j t", p=128)[:, :, c0_ - WARM:c1_ - WARM]
            inc(prog, "sync", so, lambda e, d=dst:
                e.dma_start(out=d, in_=q.ap().rearrange("p (j t) -> p j t", j=NCH)), 16)
        wait(prog, "sync", so, so.n)
        prog.run(block)
    return nc


def host_prep_A(inputs):
    x = inputs["L0_attn_x"]
    winT = np.ascontiguousarray(inputs["L0_attn_W_in"].T.astype(f32))
    woutT = np.ascontiguousarray(inputs["L0_attn_W_out"].T.astype(f32))
    dcol = np.ascontiguousarray(inputs["L0_attn_decay"].reshape(NCH, 128).T.astype(f32))
    bcol = np.ascontiguousarray(inputs["L0_attn_bias"].reshape(NCH, 128).T.astype(f32))
    onesf_h = np.ones((1, 128), f32)
    onesb_h = np.ones((128, 1), ml_dtypes.bfloat16)
    in_maps = []
    for c in range(8):
        b, h = c // 2, c % 2
        t0 = h * SH
        xs = np.empty((D, WARM + SH), f32)
        if h == 0:
            xs[:, :WARM] = 0.0
        else:
            xs[:, :WARM] = x[b, t0 - WARM:t0, :].T
        xs[:, WARM:] = x[b, t0:t0 + SH, :].T
        in_maps.append({"A_x": np.ascontiguousarray(xs), "A_winT": winT,
                        "A_woutT": woutT, "A_dcol": dcol, "A_bcol": bcol,
                        "A_onesf": onesf_h, "A_onesb": onesb_h})
    return in_maps


def run_launchA(inputs):
    nc = build_launchA()
    in_maps = host_prep_A(inputs)
    res = run_bass_kernel_spmd(nc, in_maps, core_ids=list(range(8)))
    return [r["A_out"] for r in res.results]


MAGIC = 12582912.0


def quant_weight_emit(prog, sems, wsrc, wdst, kdim, mdim, qbuf, qb2, wqbuf, redc, acc,
                      gcol, ones_c, onesf, ps, grows, qcols=2816):
    sx, sw, sv, st = sems["q"], sems["w"], sems["v"], sems["t"]
    nrow = kdim // 128
    csplits = [(c, min(qcols, mdim - c)) for c in range(0, mdim, qcols)]
    nit = nrow * len(csplits)
    assert nit <= redc.shape[1]
    v_done = sv.n if sv.n > 0 else None
    it = 0
    for r in range(nrow):
        for (c0, cw) in csplits:
            if v_done is not None:
                wait(prog, "sync", sv, v_done)
            x_ld = inc(prog, "sync", sx, lambda e, s=wsrc[r * 128:(r + 1) * 128, c0:c0 + cw], cw=cw:
                       e.dma_start(out=qbuf[:, 0:cw], in_=s), 16)
            wait(prog, "vector", sx, x_ld)
            v_done = inc(prog, "vector", sv,
                         lambda e, cw=cw, it=it: e.reduce_sum(redc[:, it:it + 1], qbuf[:, 0:cw], AX.X,
                                                              apply_absolute_value=True))
            it += 1
    v_done = inc(prog, "vector", sv, lambda e: e.reduce_sum(acc[:], redc[:, 0:nit], AX.X))
    wait(prog, "tensor", sv, v_done)
    psg = ps[0:1, 0:1]
    psb = ps[:, 512:513]
    t_g = inc(prog, "tensor", st, lambda e: e.matmul(psg, ones_c[:, 0:1], acc[:], start=True, stop=True))
    wait(prog, "vector", st, t_g)
    v_pre = inc(prog, "vector", sv, lambda e: e.tensor_scalar(grows[0:1, 0:1], psg, 1.0 / (kdim * mdim), 1e-8,
                                                              ALU.mult, ALU.add))
    wait(prog, "vector", sv, v_pre)
    v_gr = inc(prog, "vector", sv, lambda e: e.reciprocal(grows[0:1, 1:2], grows[0:1, 0:1]))
    wait(prog, "tensor", sv, v_gr)
    t_b = inc(prog, "tensor", st, lambda e: e.matmul(psb, onesf[0:1, :], grows[0:1, 1:2], start=True, stop=True))
    wait(prog, "vector", st, t_b)
    v_gc = inc(prog, "vector", sv, lambda e: e.tensor_copy(gcol[:], psb))
    v_q = sv.n
    st_prev = sems["w"].n if sems["w"].n > 0 else None
    for r in range(nrow):
        for (c0, cw) in csplits:
            if v_q is not None:
                wait(prog, "sync", sv, v_q)
            x_ld = inc(prog, "sync", sx, lambda e, s=wsrc[r * 128:(r + 1) * 128, c0:c0 + cw], cw=cw:
                       e.dma_start(out=qbuf[:, 0:cw], in_=s), 16)
            wait(prog, "vector", sx, x_ld)
            if v_gc is not None:
                wait(prog, "vector", sv, v_gc)
                v_gc = None
            if st_prev is not None:
                wait(prog, "vector", sw, st_prev)
            inc(prog, "vector", sv, lambda e, cw=cw: e.tensor_scalar(qb2[:, 0:cw], qbuf[:, 0:cw],
                                                                     gcol[:, 0:1], 1.49, ALU.mult, ALU.min))
            inc(prog, "vector", sv, lambda e, cw=cw: e.tensor_scalar(qbuf[:, 0:cw], qb2[:, 0:cw],
                                                                     -1.49, MAGIC, ALU.max, ALU.add))
            v_q = inc(prog, "vector", sv, lambda e, cw=cw: e.tensor_scalar(wqbuf[:, 0:cw], qbuf[:, 0:cw],
                                                                           MAGIC, None, ALU.subtract))
            wait(prog, "gpsimd", sv, v_q)
            st_prev = inc(prog, "gpsimd", sw, lambda e, d=wdst[r * 128:(r + 1) * 128, c0:c0 + cw], cw=cw:
                          e.dma_start(out=d, in_=wqbuf[:, 0:cw]), 16)
    return st_prev


def build_launchB(debug=False):
    nc = bass.Bass("TRN2", target_bir_lowering=False)
    x1in = nc.dram_tensor("B_x", [D, SH], F32, kind="ExternalInput").ap()
    w1T = nc.dram_tensor("B_w1T", [D, H_FFN], F32, kind="ExternalInput").ap()
    w2T = nc.dram_tensor("B_w2T", [D, H_FFN], F32, kind="ExternalInput").ap()
    w3T = nc.dram_tensor("B_w3T", [H_FFN, D], F32, kind="ExternalInput").ap()
    if debug:
        dbg_g = nc.dram_tensor("B_dbg_g", [128, 44 * TT], BF16, kind="ExternalOutput").ap()
        dbg_xln = nc.dram_tensor("B_dbg_xln", [128, NCH * TT], BF16, kind="ExternalOutput").ap()
        dbg_w1q = nc.dram_tensor("B_dbg_w1q", [128, H_FFN], BF16, kind="ExternalOutput").ap()
        dbg_x2a = nc.dram_tensor("B_dbg_x2a", [128, TT], F32, kind="ExternalOutput").ap()
        dbg_ssc = nc.dram_tensor("B_dbg_ssc", [128, 4 * TT], F32, kind="ExternalOutput").ap()
        dbg_g0 = nc.dram_tensor("B_dbg_g0", [128, 4 * TT], BF16, kind="ExternalOutput").ap()
    onesf_d = nc.dram_tensor("B_onesf", [1, 128], F32, kind="ExternalInput").ap()
    onesb_d = nc.dram_tensor("B_onesb", [128, 1], BF16, kind="ExternalInput").ap()
    onesc_d = nc.dram_tensor("B_onesc", [128, 1], F32, kind="ExternalInput").ap()
    xout = nc.dram_tensor("B_out", [D, SH], F32, kind="ExternalOutput").ap()
    w1q = nc.dram_tensor("B_w1q", [D, H_FFN], BF16).ap()
    w2q = nc.dram_tensor("B_w2q", [D, H_FFN], BF16).ap()
    w3q = nc.dram_tensor("B_w3q", [H_FFN, D], BF16).ap()
    NHC = H_FFN // 128

    with ExitStack() as ctx:
        sb = lambda n, s, d: ctx.enter_context(nc.sbuf_tensor(n, s, d))
        xr = sb("B_xr", [128, NCH * TT], F32)
        xb16 = sb("B_xb16", [128, NCH * TT], BF16)
        xln = sb("B_xln", [128, NCH * TT], BF16)
        gact = sb("B_g", [128, NHC * TT], BF16)
        xo = sb("B_xo", [128, NCH * TT], F32)
        qbuf = sb("B_qb", [128, 2816], F32)
        qb2 = sb("B_qb2", [128, 2816], F32)
        wqbuf = sb("B_wqb", [128, 2816], BF16)
        ssc = sb("B_ssc", [128, 4 * TT], F32)
        xxsc = sb("B_xxsc", [128, 2 * TT], BF16)
        t1sc = sb("B_t1sc", [128, TT], F32)
        wbuf = [sb(f"B_wb{i}", [128, 4 * 1024], BF16) for i in range(2)]
        redc = sb("B_redc", [128, 96], F32)
        acc = sb("B_acc", [128, 1], F32)
        gcol = sb("B_gcol", [128, 1], F32)
        ones_c = sb("B_oc", [128, 1], F32)
        ones_b = sb("B_o1", [128, 1], BF16)
        onesf = sb("B_o2", [1, 128], F32)
        grows = sb("B_grows", [1, 4], F32)
        rows = sb("B_rows", [1, 5 * 512], F32)
        ps = ctx.enter_context(nc.psum_tensor("B_ps", [128, 4096], F32))

        sems = {k: Sem(ctx, nc, "B_s" + k) for k in ("x", "w", "v", "t", "c", "o", "k", "q")}
        sems["wp"] = [Sem(ctx, nc, "B_swp0"), Sem(ctx, nc, "B_swp1")]
        sems["_wmm"] = [None, None]
        block = ctx.enter_context(nc.Block())
        prog = Prog()
        sx, sw, sv, st, sc, so = (sems[k] for k in ("x", "w", "v", "t", "c", "o"))

        sk = sems["k"]
        inc(prog, "sync", sk, lambda e: e.dma_start(out=ones_b[:], in_=onesb_d[:]), 16)
        inc(prog, "sync", sk, lambda e: e.dma_start(out=onesf[:], in_=onesf_d[:]), 16)
        v_c = inc(prog, "sync", sk, lambda e: e.dma_start(out=ones_c[:], in_=onesc_d[:]), 16)
        wait(prog, "tensor", sk, v_c)
        wait(prog, "vector", sk, v_c)

        quant_weight_emit(prog, sems, w1T, w1q, D, H_FFN, qbuf, qb2, wqbuf, redc, acc,
                          gcol, ones_c, onesf, ps, grows)
        quant_weight_emit(prog, sems, w2T, w2q, D, H_FFN, qbuf, qb2, wqbuf, redc, acc,
                          gcol, ones_c, onesf, ps, grows)
        w_qdone = quant_weight_emit(prog, sems, w3T, w3q, H_FFN, D, qbuf, qb2, wqbuf, redc, acc,
                                    gcol, ones_c, onesf, ps, grows)
        wait(prog, "gpsimd", sw, w_qdone)

        v_xr_free = None
        for i in range(NTI):
            c0_, c1_ = i * TT, (i + 1) * TT
            if v_xr_free is not None:
                wait(prog, "sync", sv, v_xr_free)
            src = x1in.rearrange("(j p) t -> p j t", p=128)[:, :, c0_:c1_]
            x_ld = inc(prog, "sync", sx, lambda e, s=src:
                       e.dma_start(out=xr.ap().rearrange("p (j t) -> p j t", j=NCH), in_=s), 16)
            wait(prog, "vector", sx, x_ld)
            if i > 0:
                wait(prog, "vector", so, so.n)
            v_ln = ln_tile_emit(prog, sems, xr, xb16, xln, xxsc, t1sc, ps, ones_b, onesf, rows)

            pre = (sv, v_ln)
            for gi in range(NHC // 4):
                t1 = mm_group_emit(prog, sems, w1q, D, gi * 512, 4, xln, ps, 0, wbuf,
                                   pre_mm_wait=pre)
                t2 = mm_group_emit(prog, sems, w2q, D, gi * 512, 4, xln, ps, 4, wbuf)
                wait(prog, "scalar", st, t1)
                wait(prog, "scalar", sv, sv.n)
                c_sig = None
                for j in range(4):
                    c_sig = inc(prog, "scalar", sems["c"],
                                lambda e, o=ssc[:, j * TT:(j + 1) * TT], p=ps[:, j * 512:j * 512 + TT]:
                                e.activation(o, p, AF.Sigmoid))
                wait(prog, "vector", st, t2)
                wait(prog, "vector", sems["c"], c_sig)
                v_h = None
                for j in range(4):
                    hh = gi * 4 + j
                    v_h = inc(prog, "vector", sv,
                              lambda e, o=gact[:, hh * TT:(hh + 1) * TT],
                              a=ssc[:, j * TT:(j + 1) * TT], p=ps[:, (4 + j) * 512:(4 + j) * 512 + TT]:
                              e.tensor_tensor(o, a, p, ALU.mult))
                if debug and i == 0 and gi == 0:
                    v_dbg = inc(prog, "vector", sv, lambda e:
                                e.tensor_copy(t1sc[:], ps[:, 4 * 512:4 * 512 + TT]))
                    wait(prog, "sync", sv, v_dbg)
                    inc(prog, "sync", so, lambda e: e.dma_start(out=dbg_x2a[:], in_=t1sc[:]), 16)
                    inc(prog, "sync", so, lambda e: e.dma_start(out=dbg_ssc[:], in_=ssc[:]), 16)
                    inc(prog, "sync", so, lambda e: e.dma_start(out=dbg_g0[:], in_=gact[:, 0:4 * TT]), 16)
                pre = (sv, sv.n)

            v_ev = [None]

            def evac_o(j_abs, psl):
                v_ev[0] = inc(prog, "vector", sv,
                              lambda e, o=xo[:, j_abs * TT:(j_abs + 1) * TT], p=psl,
                              x=xr[:, j_abs * TT:(j_abs + 1) * TT]:
                              e.tensor_tensor(o, x, p, ALU.add))
            mm_streamed_emit(prog, sems, w3q, H_FFN, D, gact, ps, wbuf, evac_o,
                             pre_mm_wait=pre)
            v_xr_free = v_ev[0]
            wait(prog, "sync", sv, v_xr_free)
            dst = xout.rearrange("(j p) t -> p j t", p=128)[:, :, c0_:c1_]
            inc(prog, "sync", so, lambda e, d=dst:
                e.dma_start(out=d, in_=xo.ap().rearrange("p (j t) -> p j t", j=NCH)), 16)
            if debug and i == 0:
                inc(prog, "sync", so, lambda e: e.dma_start(out=dbg_g[:], in_=gact[:]), 16)
                inc(prog, "sync", so, lambda e: e.dma_start(out=dbg_xln[:], in_=xln[:]), 16)
                inc(prog, "sync", so, lambda e: e.dma_start(out=dbg_w1q[:], in_=w2q[0:128, :]), 16)
        wait(prog, "sync", so, so.n)
        prog.run(block)
    return nc


def host_prep_B(inputs, x1_shards):
    w1Th = np.ascontiguousarray(inputs["L0_ffn_w1"].T.astype(f32))
    w2Th = np.ascontiguousarray(inputs["L0_ffn_w2"].T.astype(f32))
    w3Th = np.ascontiguousarray(inputs["L0_ffn_w3"].T.astype(f32))
    onesf_h = np.ones((1, 128), f32)
    onesb_h = np.ones((128, 1), ml_dtypes.bfloat16)
    onesc_h = np.ones((128, 1), f32)
    return [{"B_x": np.ascontiguousarray(x1_shards[c]), "B_w1T": w1Th,
             "B_w2T": w2Th, "B_w3T": w3Th, "B_onesf": onesf_h,
             "B_onesb": onesb_h, "B_onesc": onesc_h} for c in range(8)]


def run_launchB(inputs, x1_shards):
    nc = build_launchB()
    in_maps = host_prep_B(inputs, x1_shards)
    res = run_bass_kernel_spmd(nc, in_maps, core_ids=list(range(8)))
    return [r["B_out"] for r in res.results]


def _slopes16():
    start = 2.0 ** (-2.0 ** (-(np.log2(16) - 3)))
    return np.array([start ** (i + 1) for i in range(16)], dtype=np.float32)


def build_launchC(debug=False, nck=None, no_o1=False, no_o2=False):
    nc = bass.Bass("TRN2", target_bir_lowering=False)
    NCKE = nck if nck is not None else (S // CHUNK)
    if debug:
        dbg = {nm: nc.dram_tensor("C_dbg_" + nm, sh, dt, kind="ExternalOutput").ap()
               for nm, sh, dt in (("xln", [128, 2048], BF16), ("qplo", [128, 1024], BF16),
                                  ("qphi", [128, 1024], BF16), ("kplo", [128, 1024], BF16),
                                  ("vt", [128, 1024], BF16), ("smask", [128, 128], BF16),
                                  ("wq", [128, 3072], BF16), ("cos", [128, 1024], F32),
                                  ("st16", [128, 2048], BF16), ("kdlo", [128, 1024], BF16),
                                  ("obuf0", [128, 1024], BF16), ("qqlo", [128, 1024], BF16),
                                  ("st16b", [128, 2048], BF16), ("ops", [128, 128], F32),
                                  ("stf1", [128, 2048], F32))}
    NCK = S // CHUNK
    x2in = nc.dram_tensor("C_x", [D, S], F32, kind="ExternalInput").ap()
    wqkvT = nc.dram_tensor("C_wqkvT", [D, 3072], F32, kind="ExternalInput").ap()
    theta_d = nc.dram_tensor("C_theta", [1, 1024], F32, kind="ExternalInput").ap()
    pos_d = nc.dram_tensor("C_pos", [1, S], F32, kind="ExternalInput").ap()
    maskT_d = nc.dram_tensor("C_maskT", [128, 1024], F32, kind="ExternalInput").ap()
    qdbc_d = nc.dram_tensor("C_qdbc", [128, 1024], F32, kind="ExternalInput").ap()
    kdbc_d = nc.dram_tensor("C_kdbc", [128, 1024], F32, kind="ExternalInput").ap()
    ident_d = nc.dram_tensor("C_ident", [128, 128], BF16, kind="ExternalInput").ap()
    onesf_d = nc.dram_tensor("C_onesf", [1, 128], F32, kind="ExternalInput").ap()
    onesb_d = nc.dram_tensor("C_onesb", [128, 1], BF16, kind="ExternalInput").ap()
    bdcol_d = nc.dram_tensor("C_bdcol", [128, 8], F32, kind="ExternalInput").ap()
    zf32_d = nc.dram_tensor("C_zf32", [128, 2048], F32, kind="ExternalInput").ap()
    ofm = nc.dram_tensor("C_o", [1024, S], BF16, kind="ExternalOutput").ap()

    with ExitStack() as ctx:
        sb = lambda n, s, d: ctx.enter_context(nc.sbuf_tensor(n, s, d))
        wq16 = sb("C_wq16", [128, NCH * 3072], BF16)
        xr = sb("C_xr", [128, NCH * CHUNK], F32)
        xln = sb("C_xln", [128, NCH * CHUNK], BF16)
        xxsc = sb("C_xxsc", [128, 2 * CHUNK], BF16)
        t1sc = sb("C_t1sc", [128, CHUNK], F32)
        rows = sb("C_rows", [1, 5 * CHUNK], F32)
        cosf = sb("C_cosf", [128, 1024], F32)
        sinf = sb("C_sinf", [128, 1024], F32)
        trg = sb("C_trg", [128, 7 * 512], F32)
        cosq = sb("C_cosq", [128, 1024], F32)
        sinq = sb("C_sinq", [128, 1024], F32)
        cosk = sb("C_cosk", [128, 1024], F32)
        sink = sb("C_sink", [128, 1024], F32)
        qp_lo = sb("C_qplo", [128, 1024], BF16)
        qp_hi = sb("C_qphi", [128, 1024], BF16)
        qq_lo = sb("C_qqlo", [128, 1024], BF16)
        qq_hi = sb("C_qqhi", [128, 1024], BF16)
        kp_lo = sb("C_kplo", [128, 1024], BF16)
        kp_hi = sb("C_kphi", [128, 1024], BF16)
        kd_lo = sb("C_kdlo", [128, 1024], BF16)
        kd_hi = sb("C_kdhi", [128, 1024], BF16)
        vt = sb("C_vt", [128, 1024], BF16)
        stf = [sb(f"C_stf{i}", [128, 8 * 256], F32) for i in range(2)]
        st16 = sb("C_st16", [128, 8 * 256], BF16)
        smask = sb("C_smask", [128, 128], BF16)
        ktm = sb("C_ktm", [128, 256], BF16)
        obuf = sb("C_obuf", [128, 1024], BF16)
        theta = sb("C_th", [1, 1024], F32)
        pos = sb("C_pos_s", [1, CHUNK], F32)
        maskT = sb("C_maskT_s", [128, 1024], F32)
        qdbc = sb("C_qdbc_s", [128, 1024], F32)
        kdbc = sb("C_kdbc_s", [128, 1024], F32)
        idb = sb("C_idb", [128, 128], BF16)
        onesf = sb("C_onesf_s", [1, 128], F32)
        ones_b = sb("C_onesb_s", [128, 1], BF16)
        bdcol = sb("C_bdcol_s", [128, 8], F32)
        ps = ctx.enter_context(nc.psum_tensor("C_ps", [128, 3584], F32))
        psB = ctx.enter_context(nc.psum_tensor("C_psB", [128, 256], BF16))

        LNOFF = (3072, 3200, 3328, 3456)

        sems = {k: Sem(ctx, nc, "C_s" + k) for k in ("x", "w", "v", "t", "c", "o", "k", "p")}
        sems["wp"] = [Sem(ctx, nc, "C_swp0"), Sem(ctx, nc, "C_swp1")]
        sems["_wmm"] = [None, None]
        block = ctx.enter_context(nc.Block())
        prog = Prog()
        sx, sw, sv, st, sc, so = (sems[k] for k in ("x", "w", "v", "t", "c", "o"))

        sk = sems["k"]
        sp = sems["p"]
        for (dst, src) in ((theta, theta_d), (maskT, maskT_d),
                           (qdbc, qdbc_d), (kdbc, kdbc_d), (idb, ident_d),
                           (onesf, onesf_d), (ones_b, onesb_d), (bdcol, bdcol_d)):
            inc(prog, "sync", sk, lambda e, d=dst, s=src: e.dma_start(out=d[:], in_=s[:]), 16)
        inc(prog, "sync", sk, lambda e: e.dma_start(out=stf[0][:], in_=zf32_d[:]), 16)
        ld_consts = sk.n
        for kc in range(NCH):
            inc(prog, "gpsimd", sw, lambda e, kc=kc:
                e.dma_start(out=wq16[:, kc * 3072:(kc + 1) * 3072],
                            in_=wqkvT[kc * 128:(kc + 1) * 128, :]), 16)
        inc(prog, "gpsimd", sw, lambda e: e.dma_start(out=st16[:], in_=zf32_d[:]), 16)
        ld_w = sw.n
        wait(prog, "tensor", sk, ld_consts)
        wait(prog, "vector", sk, ld_consts)
        wait(prog, "scalar", sk, ld_consts)
        wait(prog, "tensor", sw, ld_w)
        wait(prog, "vector", sw, ld_w)

        v_xr_free = None
        v_phase_c_last = None
        v_state_upd = {i: None for i in range(8)}
        c_oevac_last = None
        c_ktm_last = None
        v_smask_last = None
        so_last = 0
        for c in range(NCKE):
            cur, nxt = c % 2, (c + 1) % 2
            if v_xr_free is not None:
                wait(prog, "sync", sv, v_xr_free)
            src = x2in.rearrange("(j p) t -> p j t", p=128)[:, :, c * CHUNK:(c + 1) * CHUNK]
            x_ld = inc(prog, "sync", sx, lambda e, s=src:
                       e.dma_start(out=xr.ap().rearrange("p (j t) -> p j t", j=NCH), in_=s), 16)
            if sems.get("_thguard") is not None:
                wait(prog, "sync", st, sems["_tlast"])
            pos_ld = inc(prog, "sync", sp, lambda e, c=c:
                         e.dma_start(out=pos[:], in_=pos_d[0:1, c * CHUNK:(c + 1) * CHUNK]), 16)
            wait(prog, "vector", sx, x_ld)
            v_ln = ln_tile_emit(prog, sems, xr, xln, xln, xxsc, t1sc, ps, ones_b, onesf,
                                rows, ncols=CHUNK, off=LNOFF)
            v_xr_free = v_ln

            for hh in range(2):
                if hh == 0 and c > 0:
                    wait(prog, "tensor", sc, c_v)
                if hh == 0:
                    wait(prog, "tensor", sp, pos_ld)
                if sems.get("_thguard") is not None:
                    wait(prog, "tensor", sv, sems["_thguard"])
                t_th = None
                for i in range(4):
                    gi = hh * 4 + i
                    t_th = inc(prog, "tensor", st, lambda e, gi=gi, i=i:
                               e.matmul(ps[:, 2048 + i * 128:2048 + (i + 1) * 128],
                                        theta[0:1, gi * 128:(gi + 1) * 128],
                                        pos[0:1, :],
                                        start=True, stop=True))
                sems["_tlast"] = t_th
                wait(prog, "vector", st, t_th)
                wait(prog, "vector", sems["c"], sems["c"].n)
                thp = ps[:, 2048:2560]
                PI2I = float(1.0 / (2 * np.pi))
                PI2 = float(2 * np.pi)
                rsl = trg[:, (3 + 2 * hh) * 512:(4 + 2 * hh) * 512]
                r2sl = trg[:, (4 + 2 * hh) * 512:(5 + 2 * hh) * 512]
                inc(prog, "vector", sv, lambda e: e.tensor_scalar(trg[:, 0:512], thp, PI2I, None, ALU.mult))
                inc(prog, "vector", sv, lambda e: e.tensor_scalar(trg[:, 512:1024], trg[:, 0:512], MAGIC, MAGIC, ALU.add, ALU.subtract))
                inc(prog, "vector", sv, lambda e, rsl=rsl: e.scalar_tensor_tensor(rsl, trg[:, 512:1024], -PI2, thp, ALU.mult, ALU.add))
                inc(prog, "vector", sv, lambda e: e.tensor_scalar(trg[:, 512:1024], trg[:, 0:512], 0.25, MAGIC, ALU.add, ALU.add))
                inc(prog, "vector", sv, lambda e: e.tensor_scalar(trg[:, 1024:1536], trg[:, 512:1024], MAGIC, None, ALU.subtract))
                v_trg = inc(prog, "vector", sv, lambda e, r2sl=r2sl: e.scalar_tensor_tensor(r2sl, trg[:, 1024:1536], -PI2, thp, ALU.mult, ALU.add))
                sems["_thguard"] = v_trg
                PIC = float(np.nextafter(np.float32(np.pi), np.float32(0.0)))
                inc(prog, "vector", sv, lambda e, rsl=rsl: e.tensor_scalar(trg[:, 0:512], rsl, PIC, -PIC, ALU.min, ALU.max))
                inc(prog, "vector", sv, lambda e, r2sl=r2sl: e.tensor_scalar(trg[:, 1024:1536], r2sl, float(np.pi / 2), PIC, ALU.add, ALU.min))
                v_trg = inc(prog, "vector", sv, lambda e: e.tensor_scalar(trg[:, 512:1024], trg[:, 1024:1536], -PIC, None, ALU.max))
                wait(prog, "scalar", sv, v_trg)
                inc(prog, "scalar", sc, lambda e, hh=hh: e.activation(sinf[:, hh * 512:(hh + 1) * 512], trg[:, 0:512], AF.Sin))
                c_trig = inc(prog, "scalar", sc, lambda e, hh=hh: e.activation(cosf[:, hh * 512:(hh + 1) * 512], trg[:, 512:1024], AF.Sin))
            wait(prog, "vector", sc, c_trig)
            inc(prog, "vector", sv, lambda e: e.tensor_tensor(cosq[:], cosf[:], qdbc[:], ALU.mult))
            inc(prog, "vector", sv, lambda e: e.tensor_tensor(sinq[:], sinf[:], qdbc[:], ALU.mult))
            inc(prog, "vector", sv, lambda e: e.tensor_tensor(cosk[:], cosf[:], kdbc[:], ALU.mult))
            v_trig_all = inc(prog, "vector", sv, lambda e: e.tensor_tensor(sink[:], sinf[:], kdbc[:], ALU.mult))

            wait(prog, "tensor", sv, v_trig_all)
            if c_oevac_last is not None:
                wait(prog, "tensor", sc, c_oevac_last)
            for i in range(8):
                for kc in range(NCH):
                    run_op(prog, "tensor", lambda e, kc=kc, i=i:
                           e.matmul(ps[:, i * 128:(i + 1) * 128],
                                    wq16[:, kc * 3072 + i * 128: kc * 3072 + (i + 1) * 128],
                                    xln[:, kc * CHUNK:(kc + 1) * CHUNK],
                                    start=(kc == 0), stop=(kc == NCH - 1)))
            for i in range(8):
                for kc in range(NCH):
                    run_op(prog, "tensor", lambda e, kc=kc, i=i:
                           e.matmul(ps[:, 1024 + i * 128:1024 + (i + 1) * 128],
                                    wq16[:, kc * 3072 + 1024 + i * 128: kc * 3072 + 1024 + (i + 1) * 128],
                                    xln[:, kc * CHUNK:(kc + 1) * CHUNK],
                                    start=(kc == 0), stop=(kc == NCH - 1)))
            for i in range(8):
                for kc in range(NCH):
                    is_last = (kc == NCH - 1) and (i == 7)
                    bld = (lambda e, kc=kc, i=i:
                           e.matmul(ps[:, 2048 + i * 128:2048 + (i + 1) * 128],
                                    xln[:, kc * CHUNK:(kc + 1) * CHUNK],
                                    wq16[:, kc * 3072 + 2048 + i * 128: kc * 3072 + 2048 + (i + 1) * 128],
                                    start=(kc == 0), stop=(kc == NCH - 1)))
                    if is_last:
                        t_qkv = inc(prog, "tensor", st, bld)
                    else:
                        run_op(prog, "tensor", bld)

            wait(prog, "vector", st, t_qkv)
            qps = ps[:, 0:1024]
            kps = ps[:, 1024:2048]
            inc(prog, "vector", sv, lambda e: e.tensor_tensor(qp_lo[:], qps, cosf[:], ALU.mult))
            inc(prog, "vector", sv, lambda e: e.tensor_tensor(qp_hi[:], qps, sinf[:], ALU.mult))
            inc(prog, "vector", sv, lambda e: e.tensor_tensor(qq_lo[:], qps, cosq[:], ALU.mult))
            inc(prog, "vector", sv, lambda e: e.tensor_tensor(qq_hi[:], qps, sinq[:], ALU.mult))
            inc(prog, "vector", sv, lambda e: e.tensor_tensor(kp_lo[:], kps, cosf[:], ALU.mult))
            inc(prog, "vector", sv, lambda e: e.tensor_tensor(kp_hi[:], kps, sinf[:], ALU.mult))
            inc(prog, "vector", sv, lambda e: e.tensor_tensor(kd_lo[:], kps, cosk[:], ALU.mult))
            inc(prog, "vector", sv, lambda e: e.tensor_tensor(kd_hi[:], kps, sink[:], ALU.mult))
            v_st16 = sv.n
            wait(prog, "scalar", st, t_qkv)
            c_v = inc(prog, "scalar", sc, lambda e: e.activation(vt[:], ps[:, 2048:3072], AF.Copy))
            v_phase_c_last = v_st16
            if debug and c == 2:
                wait(prog, "sync", sv, v_st16)
                inc(prog, "sync", so, lambda e: e.dma_start(out=dbg["st16"][:], in_=st16[:]), 16)
                wait(prog, "vector", so, so.n)
            if debug and c == 0:
                wait(prog, "sync", sv, v_phase_c_last)
                inc(prog, "sync", so, lambda e: e.dma_start(out=dbg["kdlo"][:], in_=kd_lo[:]), 16)
                wait(prog, "vector", so, so.n)

            for i in range(8):
                if v_smask_last is not None:
                    wait(prog, "tensor", sv, v_smask_last)
                wait(prog, "tensor", sv, v_phase_c_last)
                wait(prog, "tensor", sc, c_v)
                run_op(prog, "tensor", lambda e, i=i:
                       e.matmul(ps[:, 0:128], kp_lo[:, i * 128:(i + 1) * 128],
                                qp_lo[:, i * 128:(i + 1) * 128], start=True, stop=False))
                t_s = inc(prog, "tensor", st, lambda e, i=i:
                          e.matmul(ps[:, 0:128], kp_hi[:, i * 128:(i + 1) * 128],
                                   qp_hi[:, i * 128:(i + 1) * 128], start=False, stop=True))
                wait(prog, "vector", st, t_s)
                v_smask_last = inc(prog, "vector", sv, lambda e, i=i:
                                   e.tensor_tensor(smask[:], ps[:, 0:128], maskT[:, i * 128:(i + 1) * 128], ALU.mult))
                if debug and c == 0 and i == 0:
                    for nm, srcb in (("xln", xln), ("qplo", qp_lo), ("qphi", qp_hi),
                                     ("kplo", kp_lo), ("vt", vt), ("smask", smask),
                                     ("wq", None), ("cos", cosf)):
                        if nm == "wq":
                            inc(prog, "sync", so, lambda e, d=dbg["wq"]:
                                e.dma_start(out=d[:], in_=wq16[:, 0:3072]), 16)
                            continue
                        wait(prog, "sync", sv, v_smask_last)
                        inc(prog, "sync", so, lambda e, d=dbg[nm], s=srcb: e.dma_start(out=d[:], in_=s[:]), 16)
                    wait(prog, "vector", so, so.n)
                if c_oevac_last is not None:
                    wait(prog, "tensor", sc, c_oevac_last)
                wait(prog, "tensor", sv, v_smask_last)
                o_ops = []
                if not no_o1:
                    o_ops.append((vt[:, i * 128:(i + 1) * 128], smask[:]))
                if not no_o2:
                    v_c16 = inc(prog, "vector", sv, lambda e, i=i, cur=cur:
                                e.tensor_copy(st16[:, i * 256:(i + 1) * 256],
                                              stf[cur][:, i * 256:(i + 1) * 256]))
                    wait(prog, "tensor", sv, v_c16)
                    o_ops.append((st16[:, i * 256:i * 256 + 128], qq_lo[:, i * 128:(i + 1) * 128]))
                    o_ops.append((st16[:, i * 256 + 128:i * 256 + 256], qq_hi[:, i * 128:(i + 1) * 128]))
                t_o = None
                for pi, (lh, rh) in enumerate(o_ops):
                    bld = (lambda e, lh=lh, rh=rh, s=(pi == 0), z=(pi == len(o_ops) - 1):
                           e.matmul(ps[:, 512:640], lh, rh, start=s, stop=z))
                    if pi == len(o_ops) - 1:
                        t_o = inc(prog, "tensor", st, bld)
                    else:
                        run_op(prog, "tensor", bld)
                wait(prog, "scalar", st, t_o)
                if debug and c == 1 and i == 0:
                    wait(prog, "sync", st, t_o)
                    inc(prog, "sync", so, lambda e: e.dma_start(out=dbg["qqlo"][:], in_=qq_lo[:]), 16)
                    inc(prog, "sync", so, lambda e: e.dma_start(out=dbg["st16b"][:], in_=st16[:]), 16)
                    wait(prog, "scalar", so, so.n)
                c_oevac_last = inc(prog, "scalar", sc, lambda e, i=i:
                                   e.activation(obuf[:, i * 128:(i + 1) * 128], ps[:, 512:640], AF.Copy))
                if c_ktm_last is not None:
                    wait(prog, "tensor", sc, c_ktm_last)
                run_op(prog, "tensor", lambda e, i=i:
                       e.transpose(psB[:, 0:128], kd_lo[:, i * 128:(i + 1) * 128], idb[:]))
                t_tp = inc(prog, "tensor", st, lambda e, i=i:
                           e.transpose(psB[:, 128:256], kd_hi[:, i * 128:(i + 1) * 128], idb[:]))
                wait(prog, "scalar", st, t_tp)
                c_ktm_last = inc(prog, "scalar", sc, lambda e: e.activation(ktm[:], psB[:], AF.Copy))
                wait(prog, "tensor", sv, sv.n)
                wait(prog, "tensor", sc, c_ktm_last)
                run_op(prog, "tensor", lambda e, i=i:
                       e.matmul(ps[:, 1024:1152], ktm[:, 0:128], vt[:, i * 128:(i + 1) * 128],
                                start=True, stop=True))
                t_c2 = inc(prog, "tensor", st, lambda e, i=i:
                           e.matmul(ps[:, 1536:1664], ktm[:, 128:256], vt[:, i * 128:(i + 1) * 128],
                                    start=True, stop=True))
                wait(prog, "vector", st, t_c2)
                inc(prog, "vector", sv, lambda e, i=i, cur=cur, nxt=nxt:
                    e.scalar_tensor_tensor(stf[nxt][:, i * 256:i * 256 + 128],
                                           stf[cur][:, i * 256:i * 256 + 128], bdcol[:, i:i + 1],
                                           ps[:, 1024:1152], ALU.mult, ALU.add))
                v_state_upd[i] = inc(prog, "vector", sv, lambda e, i=i, cur=cur, nxt=nxt:
                                     e.scalar_tensor_tensor(stf[nxt][:, i * 256 + 128:i * 256 + 256],
                                                            stf[cur][:, i * 256 + 128:i * 256 + 256], bdcol[:, i:i + 1],
                                                            ps[:, 1536:1664], ALU.mult, ALU.add))
            if debug and c == 0:
                wait(prog, "sync", sc, c_oevac_last)
                inc(prog, "sync", so, lambda e: e.dma_start(out=dbg["obuf0"][:], in_=obuf[:]), 16)
                wait(prog, "sync", sv, v_state_upd[7])
                inc(prog, "sync", so, lambda e, nxt=nxt: e.dma_start(out=dbg["stf1"][:], in_=stf[nxt][:]), 16)
                wait(prog, "vector", so, so.n)
            wait(prog, "sync", sc, c_oevac_last)
            if so_last:
                pass
            dst = ofm.rearrange("(i p) t -> p i t", p=128)[:, :, c * CHUNK:(c + 1) * CHUNK]
            so_last = inc(prog, "sync", so, lambda e, d=dst:
                          e.dma_start(out=d, in_=obuf.ap().rearrange("p (i t) -> p i t", t=CHUNK)), 16)
            wait(prog, "scalar", so, so_last)
        wait(prog, "sync", so, so.n)
        prog.run(block)
    return nc


def host_prep_C(inputs, x2_shards):
    w_qkv = inputs["L1_attn_w_qkv"].astype(f32)
    theta = inputs["L1_attn_theta"].astype(f32).reshape(NH, EH)
    slopes = _slopes16()
    idx = np.arange(CHUNK, dtype=np.float64)
    diff = idx[:, None] - idx[None, :]
    in_maps = []
    onesf_h = np.ones((1, 128), f32)
    onesb_h = np.ones((128, 1), ml_dtypes.bfloat16)
    ident_h = np.eye(128, dtype=ml_dtypes.bfloat16)
    pos_h = np.arange(S, dtype=f32).reshape(1, S)
    for core in range(8):
        b, g = core // 2, core % 2
        hsl = slice(g * 8, (g + 1) * 8)
        wg = np.concatenate([w_qkv[g * 1024:(g + 1) * 1024],
                             w_qkv[2048 + g * 1024:2048 + (g + 1) * 1024],
                             w_qkv[4096 + g * 1024:4096 + (g + 1) * 1024]], axis=0)
        wqkvT_h = np.ascontiguousarray(wg.T)
        theta_h = np.ascontiguousarray(theta[hsl].reshape(1, 1024))
        maskT_h = np.empty((128, 1024), f32)
        qdbc_h = np.empty((128, 1024), f32)
        kdbc_h = np.empty((128, 1024), f32)
        for i in range(8):
            s = float(slopes[g * 8 + i])
            m = np.where(diff >= 0, np.exp(np.float32(-s) * diff.astype(f32)), 0.0).astype(f32)
            maskT_h[:, i * 128:(i + 1) * 128] = m.T
            qdbc_h[:, i * 128:(i + 1) * 128] = np.exp(np.float32(-s) * (idx + 1.0).astype(f32))[None, :]
            kdbc_h[:, i * 128:(i + 1) * 128] = np.exp(np.float32(-s) * (CHUNK - 1.0 - idx).astype(f32))[None, :]
        x2full = np.concatenate([x2_shards[2 * b], x2_shards[2 * b + 1]], axis=1)
        bdcol_h = np.broadcast_to(
            np.exp(-slopes[g * 8:(g + 1) * 8].astype(f32) * np.float32(CHUNK))[None, :],
            (128, 8)).astype(f32)
        in_maps.append({"C_zf32": np.zeros((128, 2048), f32),
                        "C_x": np.ascontiguousarray(x2full), "C_wqkvT": wqkvT_h,
                        "C_theta": theta_h, "C_pos": pos_h, "C_maskT": maskT_h,
                        "C_qdbc": qdbc_h, "C_kdbc": kdbc_h, "C_ident": ident_h,
                        "C_onesf": onesf_h, "C_onesb": onesb_h,
                        "C_bdcol": np.ascontiguousarray(bdcol_h)})
    return in_maps


def run_launchC(inputs, x2_shards):
    nc = build_launchC()
    in_maps = host_prep_C(inputs, x2_shards)
    res = run_bass_kernel_spmd(nc, in_maps, core_ids=list(range(8)))
    return [r["C_o"] for r in res.results]


def build_launchD():
    nc = bass.Bass("TRN2", target_bir_lowering=False)
    x2in = nc.dram_tensor("D_x", [D, SH], F32, kind="ExternalInput").ap()
    oin = nc.dram_tensor("D_o", [D, SH], BF16, kind="ExternalInput").ap()
    wprojT = nc.dram_tensor("D_wprojT", [D, D], F32, kind="ExternalInput").ap()
    wr_d = nc.dram_tensor("D_wr", [D, 8], F32, kind="ExternalInput").ap()
    wg_d = nc.dram_tensor("D_wg", [8, D, D], F32, kind="ExternalInput").ap()
    wu_d = nc.dram_tensor("D_wu", [8, D, D], F32, kind="ExternalInput").ap()
    wd_d = nc.dram_tensor("D_wd", [8, D, D], F32, kind="ExternalInput").ap()
    oh8_d = nc.dram_tensor("D_oh8", [8, 8 * 128], F32, kind="ExternalInput").ap()
    ident_d = nc.dram_tensor("D_ident", [128, 128], F32, kind="ExternalInput").ap()
    onesf_d = nc.dram_tensor("D_onesf", [1, 128], F32, kind="ExternalInput").ap()
    onesb_d = nc.dram_tensor("D_onesb", [128, 1], BF16, kind="ExternalInput").ap()
    xout = nc.dram_tensor("D_out", [D, SH], F32, kind="ExternalOutput").ap()

    with ExitStack() as ctx:
        sb = lambda n, s, d: ctx.enter_context(nc.sbuf_tensor(n, s, d))
        xr = sb("D_xr", [128, NCH * TT], F32)
        ob = sb("D_ob", [128, NCH * TT], BF16)
        x3r = sb("D_x3r", [128, NCH * TT], F32)
        xln = sb("D_xln", [128, NCH * TT], BF16)
        hbuf = sb("D_h", [128, NCH * TT], BF16)
        ssc = sb("D_ssc", [128, 4 * TT], F32)
        xxsc = sb("D_xxsc", [128, 2 * TT], BF16)
        t1sc = sb("D_t1sc", [128, TT], F32)
        wbuf = [sb(f"D_wb{i}", [128, 4 * 1024], BF16) for i in range(2)]
        wr16 = sb("D_wr16", [128, NCH * 8], BF16)
        lgt = sb("D_lgt", [128, 14 * 8], F32)
        wgt8 = sb("D_wgt8", [8, 4 * 128], F32)
        wbc16 = sb("D_wbc", [128, TT], BF16)
        idb = sb("D_idb", [128, 128], F32)
        oh8 = sb("D_oh8_s", [8, 8 * 128], F32)
        onesf = sb("D_onesf_s", [1, 128], F32)
        ones_b = sb("D_onesb_s", [128, 1], BF16)
        rows = sb("D_rows", [1, 5 * 512], F32)
        ps = ctx.enter_context(nc.psum_tensor("D_ps", [128, 4096], F32))

        sems = {k: Sem(ctx, nc, "D_s" + k) for k in ("x", "w", "v", "t", "c", "o", "k")}
        sems["wp"] = [Sem(ctx, nc, "D_swp0"), Sem(ctx, nc, "D_swp1")]
        sems["_wmm"] = [None, None]
        block = ctx.enter_context(nc.Block())
        prog = Prog()
        sx, sw, sv, st, sc, so = (sems[k] for k in ("x", "w", "v", "t", "c", "o"))

        sk = sems["k"]
        for (dst, src) in ((idb, ident_d), (oh8, oh8_d), (onesf, onesf_d), (ones_b, onesb_d)):
            inc(prog, "sync", sk, lambda e, d=dst, s=src: e.dma_start(out=d[:], in_=s[:]), 16)
        inc(prog, "gpsimd", sw, lambda e:
            e.dma_start(out=wr16.ap().rearrange("p (k e) -> p k e", e=8),
                        in_=wr_d.rearrange("(k p) e -> p k e", p=128)), 16)
        ld_consts = sk.n
        ld_wr = sw.n
        wait(prog, "tensor", sk, ld_consts)
        wait(prog, "vector", sk, ld_consts)
        wait(prog, "scalar", sk, ld_consts)
        wait(prog, "tensor", sw, ld_wr)

        v_tile_free = None
        for i in range(NTI):
            c0_, c1_ = i * TT, (i + 1) * TT
            if v_tile_free is not None:
                wait(prog, "sync", sv, v_tile_free)
            srcx = x2in.rearrange("(j p) t -> p j t", p=128)[:, :, c0_:c1_]
            x_ld = inc(prog, "sync", sx, lambda e, s=srcx:
                       e.dma_start(out=xr.ap().rearrange("p (j t) -> p j t", j=NCH), in_=s), 16)
            srco = oin.rearrange("(j p) t -> p j t", p=128)[:, :, c0_:c1_]
            o_ld = inc(prog, "sync", sx, lambda e, s=srco:
                       e.dma_start(out=ob.ap().rearrange("p (j t) -> p j t", j=NCH), in_=s), 16)
            wait(prog, "vector", sx, o_ld)
            if i > 0:
                wait(prog, "vector", so, so.n)

            v_ev = [None]

            def evac_x3(j_abs, psl):
                v_ev[0] = inc(prog, "vector", sv,
                              lambda e, o=x3r[:, j_abs * TT:(j_abs + 1) * TT], p=psl,
                              x=xr[:, j_abs * TT:(j_abs + 1) * TT]:
                              e.tensor_tensor(o, x, p, ALU.add))
            wait(prog, "tensor", sx, o_ld)
            mm_streamed_emit(prog, sems, wprojT, D, D, ob, ps, wbuf, evac_x3,
                             pre_mm_wait=(sv, sv.n))
            v_ln = ln_tile_emit(prog, sems, x3r, xln, xln, xxsc, t1sc, ps, ones_b, onesf, rows)

            wait(prog, "tensor", sv, v_ln)
            for tsub in range(4):
                wait(prog, "tensor", sv, sv.n)
                wait(prog, "tensor", sems["c"], sems["c"].n)
                wait(prog, "vector", st, st.n)
                lgp = ps[0:128, 2048 + tsub * 8: 2056 + tsub * 8]
                t_lg = None
                for kc in range(NCH):
                    t_lg = inc(prog, "tensor", st, lambda e, kc=kc, tsub=tsub, lgp=lgp:
                               e.matmul(lgp, xln[:, kc * TT + tsub * 128: kc * TT + (tsub + 1) * 128],
                                        wr16[:, kc * 8:(kc + 1) * 8],
                                        start=(kc == 0), stop=(kc == NCH - 1)))
                wait(prog, "vector", st, t_lg)
                L = lambda k: lgt[:, k * 8:(k + 1) * 8]
                C1 = lambda k: lgt[:, 96 + k:96 + k + 1]
                inc(prog, "vector", sv, lambda e: e.tensor_copy(L(0), lgp))
                inc(prog, "vector", sv, lambda e: e.reduce_max(C1(0), L(0), AX.X))
                v_sm = inc(prog, "vector", sv, lambda e: e.tensor_scalar(L(1), L(0), C1(0), None, ALU.subtract))
                wait(prog, "scalar", sv, v_sm)
                c_ex = inc(prog, "scalar", sc, lambda e: e.activation(L(2), L(1), AF.Exp))
                wait(prog, "vector", sc, c_ex)
                inc(prog, "vector", sv, lambda e: e.reduce_sum(C1(1), L(2), AX.X))
                inc(prog, "vector", sv, lambda e: e.reciprocal(C1(2), C1(1)))
                inc(prog, "vector", sv, lambda e: e.tensor_scalar(L(3), L(2), C1(2), None, ALU.mult))
                inc(prog, "vector", sv, lambda e: e.reduce_max(C1(3), L(3), AX.X))
                inc(prog, "vector", sv, lambda e: e.tensor_scalar(L(4), L(3), C1(3), None, ALU.is_ge))
                inc(prog, "vector", sv, lambda e: e.tensor_scalar(L(5), L(4), -1.0, 1.0, ALU.mult, ALU.add))
                inc(prog, "vector", sv, lambda e: e.tensor_tensor(L(6), L(3), L(5), ALU.mult))
                inc(prog, "vector", sv, lambda e: e.reduce_max(C1(4), L(6), AX.X))
                inc(prog, "vector", sv, lambda e: e.tensor_scalar(L(7), L(3), C1(4), None, ALU.is_ge))
                inc(prog, "vector", sv, lambda e: e.tensor_tensor(L(8), L(3), L(7), ALU.mult))
                inc(prog, "vector", sv, lambda e: e.tensor_tensor(C1(5), C1(3), C1(4), ALU.add))
                inc(prog, "vector", sv, lambda e: e.reciprocal(C1(6), C1(5)))
                v_wn = inc(prog, "vector", sv, lambda e: e.tensor_scalar(L(9), L(8), C1(6), None, ALU.mult))
                wait(prog, "tensor", sv, v_wn)
                wnp = ps[:, 2176:2304]
                t_tr = inc(prog, "tensor", st, lambda e: e.transpose(wnp[0:8, 0:128], L(9), idb[:]))
                wait(prog, "scalar", st, t_tr)
                c_w = inc(prog, "scalar", sc, lambda e, tsub=tsub:
                          e.activation(wgt8[0:8, tsub * 128:(tsub + 1) * 128], wnp[0:8, 0:128], AF.Copy))
            wait(prog, "vector", sems["c"], sems["c"].n)

            for ex in range(8):
                wait(prog, "tensor", sems["c"], sems["c"].n)
                wait(prog, "tensor", sv, sv.n)
                wbp = ps[:, 2560:2560 + TT]
                t_wb = inc(prog, "tensor", st, lambda e, ex=ex:
                           e.matmul(wbp, oh8[0:8, ex * 128:(ex + 1) * 128], wgt8[0:8, :], start=True, stop=True))
                wait(prog, "scalar", st, t_wb)
                c_wb = inc(prog, "scalar", sc, lambda e: e.activation(wbc16[:], wbp, AF.Copy))
                wait(prog, "vector", sems["c"], c_wb)
                pre = (sv, sv.n)
                for gi in range(4):
                    t1 = mm_group_emit(prog, sems, wg_d[ex], D, gi * 512, 4, xln, ps, 0, wbuf,
                                       pre_mm_wait=pre)
                    t2 = mm_group_emit(prog, sems, wu_d[ex], D, gi * 512, 4, xln, ps, 4, wbuf,
                                       pre_mm_wait=(sems["c"], c_wb) if gi == 0 else None)
                    wait(prog, "scalar", st, t1)
                    wait(prog, "scalar", sv, sv.n)
                    c_sig = None
                    for j in range(4):
                        c_sig = inc(prog, "scalar", sems["c"],
                                    lambda e, o=ssc[:, j * TT:(j + 1) * TT], p=ps[:, j * 512:j * 512 + TT]:
                                    e.activation(o, p, AF.Silu))
                    wait(prog, "vector", st, t2)
                    wait(prog, "vector", sems["c"], c_sig)
                    v_h = None
                    for j in range(4):
                        hh = gi * 4 + j
                        inc(prog, "vector", sv,
                            lambda e, o=t1sc[:], a=ssc[:, j * TT:(j + 1) * TT],
                            p=ps[:, (4 + j) * 512:(4 + j) * 512 + TT]:
                            e.tensor_tensor(o, a, p, ALU.mult))
                        v_h = inc(prog, "vector", sv,
                                  lambda e, o=hbuf[:, hh * TT:(hh + 1) * TT], a=t1sc[:]:
                                  e.tensor_tensor(o, a, wbc16[:], ALU.mult))
                    pre = (sv, v_h)
                src_y = x3r if ex == 0 else (xr if ex % 2 == 1 else x3r)
                dst_y = xr if ex % 2 == 0 else x3r
                v_ev2 = [None]

                def evac_y(j_abs, psl, src_y=src_y, dst_y=dst_y):
                    v_ev2[0] = inc(prog, "vector", sv,
                                   lambda e, o=dst_y[:, j_abs * TT:(j_abs + 1) * TT], p=psl,
                                   x=src_y[:, j_abs * TT:(j_abs + 1) * TT]:
                                   e.tensor_tensor(o, x, p, ALU.add))
                mm_streamed_emit(prog, sems, wd_d[ex], D, D, hbuf, ps, wbuf, evac_y,
                                 pre_mm_wait=pre)
            v_tile_free = v_ev2[0]
            wait(prog, "sync", sv, v_tile_free)
            dst = xout.rearrange("(j p) t -> p j t", p=128)[:, :, c0_:c1_]
            inc(prog, "sync", so, lambda e, d=dst:
                e.dma_start(out=d, in_=x3r.ap().rearrange("p (j t) -> p j t", j=NCH)), 16)
        wait(prog, "sync", so, so.n)
        prog.run(block)
    return nc


def host_prep_D(inputs, x2_shards, o_cores):
    wprojT_h = np.ascontiguousarray(inputs["L1_attn_w_proj"].T.astype(f32))
    wr_h = np.ascontiguousarray(inputs["L1_ffn_Wr"].astype(f32))
    wg_h = np.ascontiguousarray(inputs["L1_ffn_Wg"].astype(f32))
    wu_h = np.ascontiguousarray(inputs["L1_ffn_Wu"].astype(f32))
    wd_h = np.ascontiguousarray(inputs["L1_ffn_Wd"].astype(f32))
    oh8_h = np.zeros((8, 8 * 128), f32)
    for ex in range(8):
        oh8_h[ex, ex * 128:(ex + 1) * 128] = 1.0
    ident_h = np.eye(128, dtype=f32)
    onesf_h = np.ones((1, 128), f32)
    onesb_h = np.ones((128, 1), ml_dtypes.bfloat16)
    in_maps = []
    for core in range(8):
        b, h = core // 2, core % 2
        ofull = np.concatenate([o_cores[2 * b], o_cores[2 * b + 1]], axis=0)
        osh = np.ascontiguousarray(ofull[:, h * SH:(h + 1) * SH])
        in_maps.append({"D_x": np.ascontiguousarray(x2_shards[core]), "D_o": osh,
                        "D_wprojT": wprojT_h, "D_wr": wr_h, "D_wg": wg_h,
                        "D_wu": wu_h, "D_wd": wd_h, "D_oh8": oh8_h,
                        "D_ident": ident_h, "D_onesf": onesf_h, "D_onesb": onesb_h})
    return in_maps


def run_launchD(inputs, x2_shards, o_cores):
    nc = build_launchD()
    in_maps = host_prep_D(inputs, x2_shards, o_cores)
    res = run_bass_kernel_spmd(nc, in_maps, core_ids=list(range(8)))
    return [r["D_out"] for r in res.results]


def build_fused(debug=False):
    nc = bass.Bass("TRN2", target_bir_lowering=False)
    if debug:
        dbg_x2 = nc.dram_tensor("F_dbg_x2", [D, SH], F32, kind="ExternalOutput").ap()
        dbg_cc = nc.dram_tensor("F_dbg_cc", [2 * D, SH], BF16, kind="ExternalOutput").ap()
        dbg_oc = nc.dram_tensor("F_dbg_oc", [2048, S], BF16, kind="ExternalOutput").ap()
        dbg_ci = nc.dram_tensor("F_dbg_ci", [D, SH], BF16, kind="ExternalOutput").ap()
        dbg_xl = nc.dram_tensor("F_dbg_xl", [128, NCH * TT], BF16, kind="ExternalOutput").ap()
        dbg_rw = nc.dram_tensor("F_dbg_rw", [1, 5 * 512], F32, kind="ExternalOutput").ap()
        dbg_rw1 = nc.dram_tensor("F_dbg_rw1", [1, 5 * 512], F32, kind="ExternalOutput").ap()
        dbg_rw2 = nc.dram_tensor("F_dbg_rw2", [1, 5 * 512], F32, kind="ExternalOutput").ap()
    NT = NTI + 1
    NHC = H_FFN // 128
    NCK = S // CHUNK
    QC = 1408

    xin = nc.dram_tensor("F_x", [D, WARM + SH], F32, kind="ExternalInput").ap()
    winT = nc.dram_tensor("F_winT", [D, D], F32, kind="ExternalInput").ap()
    woutT = nc.dram_tensor("F_woutT", [D, D], F32, kind="ExternalInput").ap()
    dcol_d = nc.dram_tensor("F_dcol", [128, NCH], F32, kind="ExternalInput").ap()
    bcol_d = nc.dram_tensor("F_bcol", [128, NCH], F32, kind="ExternalInput").ap()
    w1T = nc.dram_tensor("F_w1T", [D, H_FFN], F32, kind="ExternalInput").ap()
    w2T = nc.dram_tensor("F_w2T", [D, H_FFN], F32, kind="ExternalInput").ap()
    w3T = nc.dram_tensor("F_w3T", [H_FFN, D], F32, kind="ExternalInput").ap()
    wqkvT = nc.dram_tensor("F_wqkvT", [D, 3072], F32, kind="ExternalInput").ap()
    theta_d = nc.dram_tensor("F_theta", [1, 1024], F32, kind="ExternalInput").ap()
    pos_d = nc.dram_tensor("F_pos", [1, S], F32, kind="ExternalInput").ap()
    maskT_d = nc.dram_tensor("F_maskT", [128, 1024], F32, kind="ExternalInput").ap()
    qdbc_d = nc.dram_tensor("F_qdbc", [128, 1024], F32, kind="ExternalInput").ap()
    kdbc_d = nc.dram_tensor("F_kdbc", [128, 1024], F32, kind="ExternalInput").ap()
    identb_d = nc.dram_tensor("F_identb", [128, 128], BF16, kind="ExternalInput").ap()
    identf_d = nc.dram_tensor("F_identf", [128, 128], F32, kind="ExternalInput").ap()
    bdcol_d = nc.dram_tensor("F_bdcol", [128, 8], F32, kind="ExternalInput").ap()
    zf32_d = nc.dram_tensor("F_zf32", [128, 2048], F32, kind="ExternalInput").ap()
    wprojT = nc.dram_tensor("F_wprojT", [D, D], F32, kind="ExternalInput").ap()
    wr_d = nc.dram_tensor("F_wr", [D, 8], F32, kind="ExternalInput").ap()
    wg_d = nc.dram_tensor("F_wg", [8, D, D], F32, kind="ExternalInput").ap()
    wu_d = nc.dram_tensor("F_wu", [8, D, D], F32, kind="ExternalInput").ap()
    wd_d = nc.dram_tensor("F_wd", [8, D, D], F32, kind="ExternalInput").ap()
    oh8_d = nc.dram_tensor("F_oh8", [8, 8 * 128], F32, kind="ExternalInput").ap()
    onesf_d = nc.dram_tensor("F_onesf", [1, 128], F32, kind="ExternalInput").ap()
    onesb_d = nc.dram_tensor("F_onesb", [128, 1], BF16, kind="ExternalInput").ap()
    onesc_d = nc.dram_tensor("F_onesc", [128, 1], F32, kind="ExternalInput").ap()
    msel_d = nc.dram_tensor("F_msel", [128, 2], F32, kind="ExternalInput").ap()
    xout = nc.dram_tensor("F_out", [D, SH], F32, kind="ExternalOutput").ap()
    w1q = nc.dram_tensor("F_w1q", [D, H_FFN], BF16).ap()
    w2q = nc.dram_tensor("F_w2q", [D, H_FFN], BF16).ap()
    w3q = nc.dram_tensor("F_w3q", [H_FFN, D], BF16).ap()
    x2sc = nc.dram_tensor("F_x2", [D, SH], F32).ap()
    ccin = nc.dram_tensor("F_ccin", [D, SH], BF16).ap()
    ccout = nc.dram_tensor("F_ccout", [2 * D, SH], BF16).ap()
    ocin = nc.dram_tensor("F_ocin", [1024, S], BF16).ap()
    ocout = nc.dram_tensor("F_ocout", [2048, S], BF16).ap()
    RG = [[0, 1], [2, 3], [4, 5], [6, 7]]

    with ExitStack() as ctx:
        sb = lambda n, s, d: ctx.enter_context(nc.sbuf_tensor(n, s, d))
        sems = {k: Sem(ctx, nc, "F_s" + k) for k in ("x", "w", "v", "t", "c", "o", "k", "p", "q", "g")}
        sems["wp"] = [Sem(ctx, nc, "F_swp0"), Sem(ctx, nc, "F_swp1")]
        sems["_wmm"] = [None, None]
        block = ctx.enter_context(nc.Block())
        prog = Prog()
        sx, sw, sv, st, sc, so = (sems[k] for k in ("x", "w", "v", "t", "c", "o"))
        sk, sp, sq, sg = sems["k"], sems["p"], sems["q"], sems["g"]
        ones_b = sb("F_o1", [128, 1], BF16)
        onesf = sb("F_o2", [1, 128], F32)

        with ExitStack() as cab:
            sba = lambda n, s, d: cab.enter_context(nc.sbuf_tensor(n, s, d))
            wbuf = [sba(f"F_wbA{i}", [128, 4 * 1024], BF16) for i in range(2)]
            xr = sba("F_xr", [128, NCH * TT], F32)
            xln = sba("F_xln", [128, NCH * TT], BF16)
            q = sba("F_q", [128, NCH * TT], F32)
            stk = sba("F_stk", [128, NCH * TT], BF16)
            gact = sba("F_g", [128, NHC * TT], BF16)
            xxsc = sba("F_xxsc", [128, 2 * TT], BF16)
            t1sc = sba("F_t1sc", [128, TT], F32)
            qbuf = sba("F_qb", [128, QC], F32)
            qb2 = sba("F_qb2", [128, QC], F32)
            wqbuf = sba("F_wqb", [128, QC], BF16)
            ssc = sba("F_ssc", [128, 4 * TT], F32)
            redc = sba("F_redc", [128, 96], F32)
            acc = sba("F_acc", [128, 1], F32)
            gcol = sba("F_gcol", [128, 1], F32)
            ones_c = sba("F_oc", [128, 1], F32)
            grows = sba("F_grows", [1, 4], F32)
            dcol = sba("F_dc", [128, NCH], F32)
            bcol = sba("F_bc", [128, NCH], F32)
            dsig = sba("F_ds", [128, NCH], F32)
            omd = sba("F_om", [128, NCH], F32)
            stcol = sba("F_st", [128, NCH], F32)
            rows = sba("F_rows", [1, 5 * 512], F32)
            ps = cab.enter_context(nc.psum_tensor("F_psAB", [128, 4096], F32))

            inc(prog, "sync", sk, lambda e: e.dma_start(out=ones_b[:], in_=onesb_d[:]), 16)
            inc(prog, "sync", sk, lambda e: e.dma_start(out=onesf[:], in_=onesf_d[:]), 16)
            inc(prog, "sync", sk, lambda e: e.dma_start(out=ones_c[:], in_=onesc_d[:]), 16)
            inc(prog, "sync", sk, lambda e: e.dma_start(out=dcol[:], in_=dcol_d[:]), 16)
            k0 = inc(prog, "sync", sk, lambda e: e.dma_start(out=bcol[:], in_=bcol_d[:]), 16)
            wait(prog, "scalar", sk, k0)
            wait(prog, "tensor", sk, k0)
            wait(prog, "vector", sk, k0)
            c0s = inc(prog, "scalar", sc, lambda e: e.activation(dsig[:], dcol[:], AF.Sigmoid))
            wait(prog, "vector", sc, c0s)
            inc(prog, "vector", sv, lambda e: e.tensor_scalar(omd[:], dsig[:], -1.0, 1.0, ALU.mult, ALU.add))

            quant_weight_emit(prog, sems, w1T, w1q, D, H_FFN, qbuf, qb2, wqbuf, redc, acc,
                              gcol, ones_c, onesf, ps, grows, qcols=QC)
            quant_weight_emit(prog, sems, w2T, w2q, D, H_FFN, qbuf, qb2, wqbuf, redc, acc,
                              gcol, ones_c, onesf, ps, grows, qcols=QC)
            w_qdone = quant_weight_emit(prog, sems, w3T, w3q, H_FFN, D, qbuf, qb2, wqbuf, redc, acc,
                                        gcol, ones_c, onesf, ps, grows, qcols=QC)
            wait(prog, "gpsimd", sw, w_qdone)

            v_xr_free = None
            for i in range(NT):
                warm = (i == 0)
                c0_ = i * TT if warm else WARM + (i - 1) * TT
                c1_ = c0_ + TT
                if v_xr_free is not None:
                    wait(prog, "sync", sv, v_xr_free)
                src = xin.rearrange("(j p) t -> p j t", p=128)[:, :, c0_:c1_]
                x_ld = inc(prog, "sync", sx, lambda e, s=src:
                           e.dma_start(out=xr.ap().rearrange("p (j t) -> p j t", j=NCH), in_=s), 16)
                wait(prog, "vector", sx, x_ld)
                if not warm:
                    wait(prog, "vector", so, so.n)
                v_ln = ln_tile_emit(prog, sems, xr, xln, xln, xxsc, t1sc, ps, ones_b, onesf, rows)
                if debug and i == 1:
                    wait(prog, "sync", sv, v_ln)
                    inc(prog, "sync", so, lambda e: e.dma_start(out=dbg_rw1[:], in_=rows[:]), 16)
                v_ev = [None]

                def evac_q(j_abs, psl):
                    v_ev[0] = inc(prog, "vector", sv,
                                  lambda e, o=q[:, j_abs * TT:(j_abs + 1) * TT], p=psl,
                                  s=omd[:, j_abs:j_abs + 1]:
                                  e.tensor_scalar(o, p, s, None, ALU.mult))
                mm_streamed_emit(prog, sems, winT, D, D, xln, ps, wbuf, evac_q,
                                 pre_mm_wait=(sv, v_ln))
                for j in range(NCH):
                    init = 0.0 if warm else stcol[:, j:j + 1]
                    dbc = dsig[:, j:j + 1].broadcast_to([128, TT])
                    inc(prog, "vector", sv,
                        lambda e, o=stk[:, j * TT:(j + 1) * TT], d=dbc,
                        qq=q[:, j * TT:(j + 1) * TT], ii=init:
                        e.tensor_tensor_scan(o, d, qq, ii, ALU.mult, ALU.add))
                v_scan = inc(prog, "vector", sv, lambda e: e.tensor_copy(stcol[:], stk[:, TT - 1::TT]))
                if warm:
                    v_xr_free = v_scan
                    continue

                def evac_x1(j_abs, psl):
                    v_ev[0] = inc(prog, "vector", sv,
                                  lambda e, o=q[:, j_abs * TT:(j_abs + 1) * TT], p=psl,
                                  s=bcol[:, j_abs:j_abs + 1], x=xr[:, j_abs * TT:(j_abs + 1) * TT]:
                                  e.scalar_tensor_tensor(o, p, s, x, ALU.add, ALU.add))
                mm_streamed_emit(prog, sems, woutT, D, D, stk, ps, wbuf, evac_x1,
                                 pre_mm_wait=(sv, v_scan))
                v_ln2 = ln_tile_emit(prog, sems, q, xln, xln, xxsc, t1sc, ps, ones_b, onesf, rows)
                if debug and i == 1:
                    wait(prog, "sync", sv, v_ln2)
                    inc(prog, "sync", so, lambda e: e.dma_start(out=dbg_rw2[:], in_=rows[:]), 16)
                pre = (sv, v_ln2)
                ev2 = [pre, pre]
                for gi in range(NHC // 2):
                    par = gi % 2
                    b0 = par * 4
                    t1 = mm_group_emit(prog, sems, w1q, D, gi * 256, 2, xln, ps, b0, wbuf,
                                       pre_mm_wait=ev2[par])
                    t2 = mm_group_emit(prog, sems, w2q, D, gi * 256, 2, xln, ps, b0 + 2, wbuf)
                    wait(prog, "scalar", st, t1)
                    wait(prog, "scalar", sv, sv.n)
                    c_sig = None
                    for j in range(2):
                        c_sig = inc(prog, "scalar", sems["c"],
                                    lambda e, o=ssc[:, (par * 2 + j) * TT:(par * 2 + j + 1) * TT],
                                    p=ps[:, (b0 + j) * 512:(b0 + j) * 512 + TT]:
                                    e.activation(o, p, AF.Sigmoid))
                    wait(prog, "vector", st, t2)
                    wait(prog, "vector", sems["c"], c_sig)
                    v_h = None
                    for j in range(2):
                        hh = gi * 2 + j
                        v_h = inc(prog, "vector", sv,
                                  lambda e, o=gact[:, hh * TT:(hh + 1) * TT],
                                  a=ssc[:, (par * 2 + j) * TT:(par * 2 + j + 1) * TT],
                                  p=ps[:, (b0 + 2 + j) * 512:(b0 + 2 + j) * 512 + TT]:
                                  e.tensor_tensor(o, a, p, ALU.mult))
                    ev2[par] = (sv, v_h)
                pre = (sv, v_h)

                def evac_x2(j_abs, psl):
                    v_ev[0] = inc(prog, "vector", sv,
                                  lambda e, o=xr[:, j_abs * TT:(j_abs + 1) * TT], p=psl,
                                  x=q[:, j_abs * TT:(j_abs + 1) * TT]:
                                  e.tensor_tensor(o, x, p, ALU.add))
                mm_streamed_emit(prog, sems, w3q, H_FFN, D, gact, ps, wbuf, evac_x2,
                                 pre_mm_wait=pre)
                v_x2 = v_ev[0]
                wait(prog, "sync", sv, v_x2)
                dx2 = x2sc.rearrange("(j p) t -> p j t", p=128)[:, :, c0_ - WARM:c1_ - WARM]
                inc(prog, "sync", so, lambda e, d=dx2:
                    e.dma_start(out=d, in_=xr.ap().rearrange("p (j t) -> p j t", j=NCH)), 16)
                v_ln3 = ln_tile_emit(prog, sems, xr, xln, gact, xxsc, t1sc, ps, ones_b, onesf, rows,
                                     off=(2048, 2560, 3072, 3584))
                v_xr_free = v_ln3
                wait(prog, "sync", sv, v_ln3)
                dln = ccin.rearrange("(j p) t -> p j t", p=128)[:, :, c0_ - WARM:c1_ - WARM]
                inc(prog, "sync", so, lambda e, d=dln:
                    e.dma_start(out=d, in_=gact.ap().rearrange("p (j t) -> p j t", t=TT)[:, 0:NCH, :]), 16)
                if debug and i == 1:
                    inc(prog, "sync", so, lambda e: e.dma_start(out=dbg_xl[:], in_=xln[:]), 16)
                    inc(prog, "sync", so, lambda e: e.dma_start(out=dbg_rw[:], in_=rows[:]), 16)
            V_ab = so.n

        wait(prog, "gpsimd", so, V_ab)
        inc(prog, "gpsimd", sg, lambda e: e.collective_compute(
            "AllGather", ALU.bypass, replica_groups=RG,
            ins=[ccin.opt()], outs=[ccout.opt()]))
        CC1 = sg.n

        with ExitStack() as ccx:
            sbc = lambda n, s, d: ccx.enter_context(nc.sbuf_tensor(n, s, d))
            wq16 = sbc("F_wq16", [128, NCH * 3072], BF16)
            xlnC = sbc("F_xlnC", [128, NCH * CHUNK], BF16)
            cosf = sbc("F_cosf", [128, 1024], F32)
            sinf = sbc("F_sinf", [128, 1024], F32)
            trg = sbc("F_trg", [128, 7 * 512], F32)
            cosq = sbc("F_cosq", [128, 1024], F32)
            sinq = sbc("F_sinq", [128, 1024], F32)
            cosk = sbc("F_cosk", [128, 1024], F32)
            sink = sbc("F_sink", [128, 1024], F32)
            qp_lo = sbc("F_qplo", [128, 1024], BF16)
            qp_hi = sbc("F_qphi", [128, 1024], BF16)
            qq_lo = sbc("F_qqlo", [128, 1024], BF16)
            qq_hi = sbc("F_qqhi", [128, 1024], BF16)
            kp_lo = sbc("F_kplo", [128, 1024], BF16)
            kp_hi = sbc("F_kphi", [128, 1024], BF16)
            kd_lo = sbc("F_kdlo", [128, 1024], BF16)
            kd_hi = sbc("F_kdhi", [128, 1024], BF16)
            vt = sbc("F_vt", [128, 1024], BF16)
            stf = [sbc(f"F_stf{i}", [128, 8 * 256], F32) for i in range(2)]
            st16 = sbc("F_st16", [128, 8 * 256], BF16)
            smask = sbc("F_smask", [128, 128], BF16)
            ktm = sbc("F_ktm", [128, 256], BF16)
            obuf = sbc("F_obuf", [128, 1024], BF16)
            theta = sbc("F_th", [1, 1024], F32)
            pos = sbc("F_pos_s", [1, CHUNK], F32)
            maskT = sbc("F_maskT_s", [128, 1024], F32)
            qdbc = sbc("F_qdbc_s", [128, 1024], F32)
            kdbc = sbc("F_kdbc_s", [128, 1024], F32)
            idb = sbc("F_idb", [128, 128], BF16)
            bdcol = sbc("F_bdcol_s", [128, 8], F32)
            psC = ccx.enter_context(nc.psum_tensor("F_psC", [128, 3584], F32))
            psB = ccx.enter_context(nc.psum_tensor("F_psB", [128, 256], BF16))

            for eng in ("sync", "gpsimd", "vector", "scalar", "tensor"):
                wait(prog, eng, so, V_ab)
            for (dst, src2) in ((theta, theta_d), (maskT, maskT_d),
                                (qdbc, qdbc_d), (kdbc, kdbc_d), (idb, identb_d),
                                (bdcol, bdcol_d)):
                inc(prog, "sync", sk, lambda e, d=dst, s=src2: e.dma_start(out=d[:], in_=s[:]), 16)
            inc(prog, "sync", sk, lambda e: e.dma_start(out=stf[0][:], in_=zf32_d[:]), 16)
            ldC = sk.n
            for kc in range(NCH):
                inc(prog, "gpsimd", sw, lambda e, kc=kc:
                    e.dma_start(out=wq16[:, kc * 3072:(kc + 1) * 3072],
                                in_=wqkvT[kc * 128:(kc + 1) * 128, :]), 16)
            inc(prog, "gpsimd", sw, lambda e: e.dma_start(out=st16[:], in_=zf32_d[:]), 16)
            ldCw = sw.n
            wait(prog, "tensor", sk, ldC)
            wait(prog, "vector", sk, ldC)
            wait(prog, "scalar", sk, ldC)
            wait(prog, "tensor", sw, ldCw)
            wait(prog, "vector", sw, ldCw)
            wait(prog, "sync", sg, CC1)

            sems["_thguard"] = None
            v_xln_free = None
            v_phase_c_last = None
            c_oevac_last = None
            c_ktm_last = None
            v_smask_last = None
            c_v = None
            for c in range(NCK):
                cur, nxt = c % 2, (c + 1) % 2
                half, tl = c // 32, c % 32
                if v_xln_free is not None:
                    wait(prog, "sync", st, v_xln_free)
                src2 = ccout[half * D:(half + 1) * D, :].rearrange(
                    "(j p) t -> p j t", p=128)[:, :, tl * CHUNK:(tl + 1) * CHUNK]
                x_ld = inc(prog, "sync", sx, lambda e, s=src2:
                           e.dma_start(out=xlnC.ap().rearrange("p (j t) -> p j t", j=NCH), in_=s), 16)
                if sems.get("_thguard") is not None:
                    wait(prog, "sync", st, sems["_tlast"])
                pos_ld = inc(prog, "sync", sp, lambda e, c=c:
                             e.dma_start(out=pos[:], in_=pos_d[0:1, c * CHUNK:(c + 1) * CHUNK]), 16)
                for hh in range(2):
                    if hh == 0 and c > 0:
                        wait(prog, "tensor", sc, c_v)
                    if hh == 0:
                        wait(prog, "tensor", sp, pos_ld)
                    if sems.get("_thguard") is not None:
                        wait(prog, "tensor", sv, sems["_thguard"])
                    t_th = None
                    for i in range(4):
                        gi = hh * 4 + i
                        t_th = inc(prog, "tensor", st, lambda e, gi=gi, i=i:
                                   e.matmul(psC[:, 2048 + i * 128:2048 + (i + 1) * 128],
                                            theta[0:1, gi * 128:(gi + 1) * 128],
                                            pos[0:1, :], start=True, stop=True))
                    sems["_tlast"] = t_th
                    wait(prog, "vector", st, t_th)
                    wait(prog, "vector", sems["c"], sems["c"].n)
                    thp = psC[:, 2048:2560]
                    PI2I = float(1.0 / (2 * np.pi))
                    PI2 = float(2 * np.pi)
                    rsl = trg[:, (3 + 2 * hh) * 512:(4 + 2 * hh) * 512]
                    r2sl = trg[:, (4 + 2 * hh) * 512:(5 + 2 * hh) * 512]
                    inc(prog, "vector", sv, lambda e: e.tensor_scalar(trg[:, 0:512], thp, PI2I, None, ALU.mult))
                    inc(prog, "vector", sv, lambda e: e.tensor_scalar(trg[:, 512:1024], trg[:, 0:512], MAGIC, MAGIC, ALU.add, ALU.subtract))
                    inc(prog, "vector", sv, lambda e, rsl=rsl: e.scalar_tensor_tensor(rsl, trg[:, 512:1024], -PI2, thp, ALU.mult, ALU.add))
                    inc(prog, "vector", sv, lambda e: e.tensor_scalar(trg[:, 512:1024], trg[:, 0:512], 0.25, MAGIC, ALU.add, ALU.add))
                    inc(prog, "vector", sv, lambda e: e.tensor_scalar(trg[:, 1024:1536], trg[:, 512:1024], MAGIC, None, ALU.subtract))
                    v_trg = inc(prog, "vector", sv, lambda e, r2sl=r2sl: e.scalar_tensor_tensor(r2sl, trg[:, 1024:1536], -PI2, thp, ALU.mult, ALU.add))
                    sems["_thguard"] = v_trg
                    PIC = float(np.nextafter(np.float32(np.pi), np.float32(0.0)))
                    inc(prog, "vector", sv, lambda e, rsl=rsl: e.tensor_scalar(trg[:, 0:512], rsl, PIC, -PIC, ALU.min, ALU.max))
                    inc(prog, "vector", sv, lambda e, r2sl=r2sl: e.tensor_scalar(trg[:, 1024:1536], r2sl, float(np.pi / 2), PIC, ALU.add, ALU.min))
                    v_trg = inc(prog, "vector", sv, lambda e: e.tensor_scalar(trg[:, 512:1024], trg[:, 1024:1536], -PIC, None, ALU.max))
                    wait(prog, "scalar", sv, v_trg)
                    inc(prog, "scalar", sc, lambda e, hh=hh: e.activation(sinf[:, hh * 512:(hh + 1) * 512], trg[:, 0:512], AF.Sin))
                    c_trig = inc(prog, "scalar", sc, lambda e, hh=hh: e.activation(cosf[:, hh * 512:(hh + 1) * 512], trg[:, 512:1024], AF.Sin))
                wait(prog, "vector", sc, c_trig)
                inc(prog, "vector", sv, lambda e: e.tensor_tensor(cosq[:], cosf[:], qdbc[:], ALU.mult))
                inc(prog, "vector", sv, lambda e: e.tensor_tensor(sinq[:], sinf[:], qdbc[:], ALU.mult))
                inc(prog, "vector", sv, lambda e: e.tensor_tensor(cosk[:], cosf[:], kdbc[:], ALU.mult))
                v_trig_all = inc(prog, "vector", sv, lambda e: e.tensor_tensor(sink[:], sinf[:], kdbc[:], ALU.mult))

                wait(prog, "tensor", sv, v_trig_all)
                wait(prog, "tensor", sx, x_ld)
                if c_oevac_last is not None:
                    wait(prog, "tensor", sc, c_oevac_last)
                for i in range(8):
                    for kc in range(NCH):
                        run_op(prog, "tensor", lambda e, kc=kc, i=i:
                               e.matmul(psC[:, i * 128:(i + 1) * 128],
                                        wq16[:, kc * 3072 + i * 128: kc * 3072 + (i + 1) * 128],
                                        xlnC[:, kc * CHUNK:(kc + 1) * CHUNK],
                                        start=(kc == 0), stop=(kc == NCH - 1)))
                for i in range(8):
                    for kc in range(NCH):
                        run_op(prog, "tensor", lambda e, kc=kc, i=i:
                               e.matmul(psC[:, 1024 + i * 128:1024 + (i + 1) * 128],
                                        wq16[:, kc * 3072 + 1024 + i * 128: kc * 3072 + 1024 + (i + 1) * 128],
                                        xlnC[:, kc * CHUNK:(kc + 1) * CHUNK],
                                        start=(kc == 0), stop=(kc == NCH - 1)))
                t_qkv = None
                for i in range(8):
                    for kc in range(NCH):
                        is_last = (kc == NCH - 1) and (i == 7)
                        bld = (lambda e, kc=kc, i=i:
                               e.matmul(psC[:, 2048 + i * 128:2048 + (i + 1) * 128],
                                        xlnC[:, kc * CHUNK:(kc + 1) * CHUNK],
                                        wq16[:, kc * 3072 + 2048 + i * 128: kc * 3072 + 2048 + (i + 1) * 128],
                                        start=(kc == 0), stop=(kc == NCH - 1)))
                        if is_last:
                            t_qkv = inc(prog, "tensor", st, bld)
                        else:
                            run_op(prog, "tensor", bld)
                v_xln_free = t_qkv

                wait(prog, "vector", st, t_qkv)
                qps = psC[:, 0:1024]
                kps = psC[:, 1024:2048]
                inc(prog, "vector", sv, lambda e: e.tensor_tensor(qp_lo[:], qps, cosf[:], ALU.mult))
                inc(prog, "vector", sv, lambda e: e.tensor_tensor(qp_hi[:], qps, sinf[:], ALU.mult))
                inc(prog, "vector", sv, lambda e: e.tensor_tensor(qq_lo[:], qps, cosq[:], ALU.mult))
                inc(prog, "vector", sv, lambda e: e.tensor_tensor(qq_hi[:], qps, sinq[:], ALU.mult))
                inc(prog, "vector", sv, lambda e: e.tensor_tensor(kp_lo[:], kps, cosf[:], ALU.mult))
                inc(prog, "vector", sv, lambda e: e.tensor_tensor(kp_hi[:], kps, sinf[:], ALU.mult))
                inc(prog, "vector", sv, lambda e: e.tensor_tensor(kd_lo[:], kps, cosk[:], ALU.mult))
                v_phase_c_last = inc(prog, "vector", sv, lambda e: e.tensor_tensor(kd_hi[:], kps, sink[:], ALU.mult))
                wait(prog, "scalar", st, t_qkv)
                c_v = inc(prog, "scalar", sc, lambda e: e.activation(vt[:], psC[:, 2048:3072], AF.Copy))

                for i in range(8):
                    if v_smask_last is not None:
                        wait(prog, "tensor", sv, v_smask_last)
                    wait(prog, "tensor", sv, v_phase_c_last)
                    wait(prog, "tensor", sc, c_v)
                    run_op(prog, "tensor", lambda e, i=i:
                           e.matmul(psC[:, 0:128], kp_lo[:, i * 128:(i + 1) * 128],
                                    qp_lo[:, i * 128:(i + 1) * 128], start=True, stop=False))
                    t_s = inc(prog, "tensor", st, lambda e, i=i:
                              e.matmul(psC[:, 0:128], kp_hi[:, i * 128:(i + 1) * 128],
                                       qp_hi[:, i * 128:(i + 1) * 128], start=False, stop=True))
                    wait(prog, "vector", st, t_s)
                    v_smask_last = inc(prog, "vector", sv, lambda e, i=i:
                                       e.tensor_tensor(smask[:], psC[:, 0:128], maskT[:, i * 128:(i + 1) * 128], ALU.mult))
                    if c_oevac_last is not None:
                        wait(prog, "tensor", sc, c_oevac_last)
                    wait(prog, "tensor", sv, v_smask_last)
                    v_c16 = inc(prog, "vector", sv, lambda e, i=i, cur=cur:
                                e.tensor_copy(st16[:, i * 256:(i + 1) * 256],
                                              stf[cur][:, i * 256:(i + 1) * 256]))
                    wait(prog, "tensor", sv, v_c16)
                    run_op(prog, "tensor", lambda e, i=i:
                           e.matmul(psC[:, 512:640], vt[:, i * 128:(i + 1) * 128], smask[:], start=True, stop=False))
                    run_op(prog, "tensor", lambda e, i=i:
                           e.matmul(psC[:, 512:640], st16[:, i * 256:i * 256 + 128],
                                    qq_lo[:, i * 128:(i + 1) * 128], start=False, stop=False))
                    t_o = inc(prog, "tensor", st, lambda e, i=i:
                              e.matmul(psC[:, 512:640], st16[:, i * 256 + 128:i * 256 + 256],
                                       qq_hi[:, i * 128:(i + 1) * 128], start=False, stop=True))
                    wait(prog, "scalar", st, t_o)
                    c_oevac_last = inc(prog, "scalar", sc, lambda e, i=i:
                                       e.activation(obuf[:, i * 128:(i + 1) * 128], psC[:, 512:640], AF.Copy))
                    if c_ktm_last is not None:
                        wait(prog, "tensor", sc, c_ktm_last)
                    run_op(prog, "tensor", lambda e, i=i:
                           e.transpose(psB[:, 0:128], kd_lo[:, i * 128:(i + 1) * 128], idb[:]))
                    t_tp = inc(prog, "tensor", st, lambda e, i=i:
                               e.transpose(psB[:, 128:256], kd_hi[:, i * 128:(i + 1) * 128], idb[:]))
                    wait(prog, "scalar", st, t_tp)
                    c_ktm_last = inc(prog, "scalar", sc, lambda e: e.activation(ktm[:], psB[:], AF.Copy))
                    wait(prog, "tensor", sv, sv.n)
                    wait(prog, "tensor", sc, c_ktm_last)
                    run_op(prog, "tensor", lambda e, i=i:
                           e.matmul(psC[:, 1024:1152], ktm[:, 0:128], vt[:, i * 128:(i + 1) * 128],
                                    start=True, stop=True))
                    t_c2 = inc(prog, "tensor", st, lambda e, i=i:
                               e.matmul(psC[:, 1536:1664], ktm[:, 128:256], vt[:, i * 128:(i + 1) * 128],
                                        start=True, stop=True))
                    wait(prog, "vector", st, t_c2)
                    inc(prog, "vector", sv, lambda e, i=i, cur=cur, nxt=nxt:
                        e.scalar_tensor_tensor(stf[nxt][:, i * 256:i * 256 + 128],
                                               stf[cur][:, i * 256:i * 256 + 128], bdcol[:, i:i + 1],
                                               psC[:, 1024:1152], ALU.mult, ALU.add))
                    inc(prog, "vector", sv, lambda e, i=i, cur=cur, nxt=nxt:
                        e.scalar_tensor_tensor(stf[nxt][:, i * 256 + 128:i * 256 + 256],
                                               stf[cur][:, i * 256 + 128:i * 256 + 256], bdcol[:, i:i + 1],
                                               psC[:, 1536:1664], ALU.mult, ALU.add))
                wait(prog, "sync", sc, c_oevac_last)
                dsto = ocin.rearrange("(i p) t -> p i t", p=128)[:, :, c * CHUNK:(c + 1) * CHUNK]
                so_last = inc(prog, "sync", so, lambda e, d=dsto:
                              e.dma_start(out=d, in_=obuf.ap().rearrange("p (i t) -> p i t", t=CHUNK)), 16)
                wait(prog, "scalar", so, so_last)
            V_c = so.n

        wait(prog, "gpsimd", so, V_c)
        inc(prog, "gpsimd", sg, lambda e: e.collective_compute(
            "AllGather", ALU.bypass, replica_groups=RG,
            ins=[ocin.opt()], outs=[ocout.opt()]))
        CC2 = sg.n

        with ExitStack() as cex:
            sbe = lambda n, s, d: cex.enter_context(nc.sbuf_tensor(n, s, d))
            wbuf = [sbe(f"F_wbE{i}", [128, 4 * 1024], BF16) for i in range(2)]
            xr = sbe("F_xrE", [128, NCH * TT], F32)
            ob = sbe("F_ob", [128, NCH * TT], BF16)
            ob0 = sbe("F_ob0", [128, NCH * TT], BF16)
            ob1 = sbe("F_ob1", [128, NCH * TT], BF16)
            msel = sbe("F_msel_s", [128, 2], F32)
            x3r = sbe("F_x3r", [128, NCH * TT], F32)
            xln = sbe("F_xlnE", [128, NCH * TT], BF16)
            hbuf = sbe("F_h", [128, NCH * TT], BF16)
            ssc = sbe("F_sscE", [128, 4 * TT], F32)
            xxsc = sbe("F_xxscE", [128, 2 * TT], BF16)
            t1sc = sbe("F_t1scE", [128, TT], F32)
            wr16 = sbe("F_wr16", [128, NCH * 8], BF16)
            lgt = sbe("F_lgt", [128, 14 * 8], F32)
            wgt8 = sbe("F_wgt8", [8, 4 * 128], F32)
            wbc16 = sbe("F_wbc", [128, TT], BF16)
            idf = sbe("F_idf", [128, 128], F32)
            oh8 = sbe("F_oh8_s", [8, 8 * 128], F32)
            rows = sbe("F_rowsE", [1, 5 * 512], F32)
            ps = cex.enter_context(nc.psum_tensor("F_psE", [128, 4096], F32))

            for eng in ("sync", "gpsimd", "vector", "scalar", "tensor"):
                wait(prog, eng, so, V_c)
            for (dst, src2) in ((idf, identf_d), (oh8, oh8_d), (msel, msel_d)):
                inc(prog, "sync", sk, lambda e, d=dst, s=src2: e.dma_start(out=d[:], in_=s[:]), 16)
            ldE = sk.n
            inc(prog, "gpsimd", sw, lambda e:
                e.dma_start(out=wr16.ap().rearrange("p (k e) -> p k e", e=8),
                            in_=wr_d.rearrange("(k p) e -> p k e", p=128)), 16)
            ldEw = sw.n
            wait(prog, "tensor", sk, ldE)
            wait(prog, "vector", sk, ldE)
            wait(prog, "scalar", sk, ldE)
            wait(prog, "tensor", sw, ldEw)
            wait(prog, "sync", sg, CC2)

            v_tile_free = None
            for i in range(NTI):
                c0_, c1_ = i * TT, (i + 1) * TT
                if v_tile_free is not None:
                    wait(prog, "sync", sv, v_tile_free)
                srcx = x2sc.rearrange("(j p) t -> p j t", p=128)[:, :, c0_:c1_]
                x_ld = inc(prog, "sync", sx, lambda e, s=srcx:
                           e.dma_start(out=xr.ap().rearrange("p (j t) -> p j t", j=NCH), in_=s), 16)
                srco = ocout.rearrange("(j p) t -> p j t", p=128)
                inc(prog, "sync", sx, lambda e, s=srco, c0_=c0_, c1_=c1_:
                    e.dma_start(out=ob0.ap().rearrange("p (j t) -> p j t", j=NCH),
                                in_=s[:, :, c0_:c1_]), 16)
                o_ld = inc(prog, "sync", sx, lambda e, s=srco, c0_=c0_, c1_=c1_:
                           e.dma_start(out=ob1.ap().rearrange("p (j t) -> p j t", j=NCH),
                                       in_=s[:, :, SH + c0_:SH + c1_]), 16)
                wait(prog, "vector", sx, o_ld)
                if i > 0:
                    wait(prog, "vector", so, so.n)
                inc(prog, "vector", sv, lambda e: e.tensor_scalar(ob[:], ob0[:], msel[:, 0:1], None, ALU.mult))
                inc(prog, "vector", sv, lambda e: e.scalar_tensor_tensor(ob0[:], ob1[:], msel[:, 1:2], ob[:], ALU.mult, ALU.add))
                v_ev = [None]

                def evac_x3(j_abs, psl):
                    v_ev[0] = inc(prog, "vector", sv,
                                  lambda e, o=x3r[:, j_abs * TT:(j_abs + 1) * TT], p=psl,
                                  x=xr[:, j_abs * TT:(j_abs + 1) * TT]:
                                  e.tensor_tensor(o, x, p, ALU.add))
                wait(prog, "tensor", sx, o_ld)
                mm_streamed_emit(prog, sems, wprojT, D, D, ob0, ps, wbuf, evac_x3,
                                 pre_mm_wait=(sv, sv.n))
                v_ln4 = ln_tile_emit(prog, sems, x3r, xln, xln, xxsc, t1sc, ps, ones_b, onesf, rows)
                wait(prog, "tensor", sv, v_ln4)
                c_w = None
                t_tr = None
                for tsub in range(4):
                    wait(prog, "tensor", sv, sv.n)
                    wait(prog, "tensor", sems["c"], sems["c"].n)
                    if t_tr is not None:
                        wait(prog, "vector", st, t_tr)
                    lgp = ps[0:128, 2048 + tsub * 8: 2056 + tsub * 8]
                    t_lg = None
                    for kc in range(NCH):
                        t_lg = inc(prog, "tensor", st, lambda e, kc=kc, tsub=tsub, lgp=lgp:
                                   e.matmul(lgp, xln[:, kc * TT + tsub * 128: kc * TT + (tsub + 1) * 128],
                                            wr16[:, kc * 8:(kc + 1) * 8],
                                            start=(kc == 0), stop=(kc == NCH - 1)))
                    wait(prog, "vector", st, t_lg)
                    L = lambda k: lgt[:, k * 8:(k + 1) * 8]
                    C1 = lambda k: lgt[:, 96 + k:96 + k + 1]
                    inc(prog, "vector", sv, lambda e, lgp=lgp: e.tensor_copy(L(0), lgp))
                    inc(prog, "vector", sv, lambda e: e.reduce_max(C1(0), L(0), AX.X))
                    v_sm = inc(prog, "vector", sv, lambda e: e.tensor_scalar(L(1), L(0), C1(0), None, ALU.subtract))
                    wait(prog, "scalar", sv, v_sm)
                    c_ex = inc(prog, "scalar", sc, lambda e: e.activation(L(2), L(1), AF.Exp))
                    wait(prog, "vector", sc, c_ex)
                    inc(prog, "vector", sv, lambda e: e.reduce_sum(C1(1), L(2), AX.X))
                    inc(prog, "vector", sv, lambda e: e.reciprocal(C1(2), C1(1)))
                    inc(prog, "vector", sv, lambda e: e.tensor_scalar(L(3), L(2), C1(2), None, ALU.mult))
                    inc(prog, "vector", sv, lambda e: e.reduce_max(C1(3), L(3), AX.X))
                    inc(prog, "vector", sv, lambda e: e.tensor_scalar(L(4), L(3), C1(3), None, ALU.is_ge))
                    inc(prog, "vector", sv, lambda e: e.tensor_scalar(L(5), L(4), -1.0, 1.0, ALU.mult, ALU.add))
                    inc(prog, "vector", sv, lambda e: e.tensor_tensor(L(6), L(3), L(5), ALU.mult))
                    inc(prog, "vector", sv, lambda e: e.reduce_max(C1(4), L(6), AX.X))
                    inc(prog, "vector", sv, lambda e: e.tensor_scalar(L(7), L(3), C1(4), None, ALU.is_ge))
                    inc(prog, "vector", sv, lambda e: e.tensor_tensor(L(8), L(3), L(7), ALU.mult))
                    inc(prog, "vector", sv, lambda e: e.tensor_tensor(C1(5), C1(3), C1(4), ALU.add))
                    inc(prog, "vector", sv, lambda e: e.reciprocal(C1(6), C1(5)))
                    v_wn = inc(prog, "vector", sv, lambda e: e.tensor_scalar(L(9), L(8), C1(6), None, ALU.mult))
                    wait(prog, "tensor", sv, v_wn)
                    wnp = ps[:, 2176:2304]
                    t_tr = inc(prog, "tensor", st, lambda e: e.transpose(wnp[0:8, 0:128], L(9), idf[:]))
                    wait(prog, "scalar", st, t_tr)
                    c_w = inc(prog, "scalar", sc, lambda e, tsub=tsub:
                              e.activation(wgt8[0:8, tsub * 128:(tsub + 1) * 128], wnp[0:8, 0:128], AF.Copy))
                wait(prog, "vector", sems["c"], sems["c"].n)

                for ex in range(8):
                    wait(prog, "tensor", sems["c"], sems["c"].n)
                    wait(prog, "tensor", sv, sv.n)
                    wbp = ps[:, 2560:2560 + TT]
                    t_wb = inc(prog, "tensor", st, lambda e, ex=ex:
                               e.matmul(wbp, oh8[0:8, ex * 128:(ex + 1) * 128], wgt8[0:8, :], start=True, stop=True))
                    wait(prog, "scalar", st, t_wb)
                    c_wb = inc(prog, "scalar", sc, lambda e: e.activation(wbc16[:], wbp, AF.Copy))
                    wait(prog, "vector", sems["c"], c_wb)
                    pre = (sv, sv.n)
                    ev2 = [pre, pre]
                    for gi in range(8):
                        par = gi % 2
                        b0 = par * 4
                        t1 = mm_group_emit(prog, sems, wg_d[ex], D, gi * 256, 2, xln, ps, b0, wbuf,
                                           pre_mm_wait=ev2[par])
                        t2 = mm_group_emit(prog, sems, wu_d[ex], D, gi * 256, 2, xln, ps, b0 + 2, wbuf,
                                           pre_mm_wait=(sems["c"], c_wb) if gi <= 1 else None)
                        wait(prog, "scalar", st, t1)
                        wait(prog, "scalar", sv, sv.n)
                        c_sig = None
                        for j in range(2):
                            c_sig = inc(prog, "scalar", sems["c"],
                                        lambda e, o=ssc[:, (par * 2 + j) * TT:(par * 2 + j + 1) * TT],
                                        p=ps[:, (b0 + j) * 512:(b0 + j) * 512 + TT]:
                                        e.activation(o, p, AF.Silu))
                        wait(prog, "vector", st, t2)
                        wait(prog, "vector", sems["c"], c_sig)
                        v_h = None
                        for j in range(2):
                            hh = gi * 2 + j
                            inc(prog, "vector", sv,
                                lambda e, o=t1sc[:], a=ssc[:, (par * 2 + j) * TT:(par * 2 + j + 1) * TT],
                                p=ps[:, (b0 + 2 + j) * 512:(b0 + 2 + j) * 512 + TT]:
                                e.tensor_tensor(o, a, p, ALU.mult))
                            v_h = inc(prog, "vector", sv,
                                      lambda e, o=hbuf[:, hh * TT:(hh + 1) * TT], a=t1sc[:]:
                                      e.tensor_tensor(o, a, wbc16[:], ALU.mult))
                        ev2[par] = (sv, v_h)
                    pre = (sv, v_h)
                    src_y = x3r if ex == 0 else (xr if ex % 2 == 1 else x3r)
                    dst_y = xr if ex % 2 == 0 else x3r
                    v_ev2 = [None]

                    def evac_y(j_abs, psl, src_y=src_y, dst_y=dst_y):
                        v_ev2[0] = inc(prog, "vector", sv,
                                       lambda e, o=dst_y[:, j_abs * TT:(j_abs + 1) * TT], p=psl,
                                       x=src_y[:, j_abs * TT:(j_abs + 1) * TT]:
                                       e.tensor_tensor(o, x, p, ALU.add))
                    mm_streamed_emit(prog, sems, wd_d[ex], D, D, hbuf, ps, wbuf, evac_y,
                                     pre_mm_wait=pre)
                v_tile_free = v_ev2[0]
                wait(prog, "sync", sv, v_tile_free)
                dst = xout.rearrange("(j p) t -> p j t", p=128)[:, :, c0_:c1_]
                inc(prog, "sync", so, lambda e, d=dst:
                    e.dma_start(out=d, in_=x3r.ap().rearrange("p (j t) -> p j t", j=NCH)), 16)
            wait(prog, "sync", so, so.n)
            if debug:
                inc(prog, "sync", so, lambda e: e.dma_start(out=dbg_ci[:, :], in_=ccin[:, :]), 16)
                inc(prog, "sync", so, lambda e: e.dma_start(out=dbg_x2[:, :], in_=x2sc[:, :]), 16)
                inc(prog, "sync", so, lambda e: e.dma_start(out=dbg_cc[:, :], in_=ccout[:, :]), 16)
                inc(prog, "sync", so, lambda e: e.dma_start(out=dbg_oc[:, :], in_=ocout[:, :]), 16)
                wait(prog, "sync", so, so.n)
        prog.run(block)
    return nc


def host_prep_fused(inputs):
    x = inputs["L0_attn_x"]
    winT_h = np.ascontiguousarray(inputs["L0_attn_W_in"].T.astype(f32))
    woutT_h = np.ascontiguousarray(inputs["L0_attn_W_out"].T.astype(f32))
    dcol_h = np.ascontiguousarray(inputs["L0_attn_decay"].reshape(NCH, 128).T.astype(f32))
    bcol_h = np.ascontiguousarray(inputs["L0_attn_bias"].reshape(NCH, 128).T.astype(f32))
    w1Th = np.ascontiguousarray(inputs["L0_ffn_w1"].T.astype(f32))
    w2Th = np.ascontiguousarray(inputs["L0_ffn_w2"].T.astype(f32))
    w3Th = np.ascontiguousarray(inputs["L0_ffn_w3"].T.astype(f32))
    w_qkv = inputs["L1_attn_w_qkv"].astype(f32)
    theta = inputs["L1_attn_theta"].astype(f32).reshape(NH, EH)
    wprojT_h = np.ascontiguousarray(inputs["L1_attn_w_proj"].T.astype(f32))
    wr_h = np.ascontiguousarray(inputs["L1_ffn_Wr"].astype(f32))
    wg_h = np.ascontiguousarray(inputs["L1_ffn_Wg"].astype(f32))
    wu_h = np.ascontiguousarray(inputs["L1_ffn_Wu"].astype(f32))
    wd_h = np.ascontiguousarray(inputs["L1_ffn_Wd"].astype(f32))
    slopes = _slopes16()
    idx = np.arange(CHUNK, dtype=np.float64)
    diff = idx[:, None] - idx[None, :]
    oh8_h = np.zeros((8, 8 * 128), f32)
    for ex in range(8):
        oh8_h[ex, ex * 128:(ex + 1) * 128] = 1.0
    onesf_h = np.ones((1, 128), f32)
    onesb_h = np.ones((128, 1), ml_dtypes.bfloat16)
    onesc_h = np.ones((128, 1), f32)
    identb_h = np.eye(128, dtype=ml_dtypes.bfloat16)
    identf_h = np.eye(128, dtype=f32)
    pos_h = np.arange(S, dtype=f32).reshape(1, S)
    zf32_h = np.zeros((128, 2048), f32)
    in_maps = []
    for core in range(8):
        b, g = core // 2, core % 2
        t0 = g * SH
        xs = np.empty((D, WARM + SH), f32)
        if g == 0:
            xs[:, :WARM] = 0.0
        else:
            xs[:, :WARM] = x[b, t0 - WARM:t0, :].T
        xs[:, WARM:] = x[b, t0:t0 + SH, :].T
        wgq = np.concatenate([w_qkv[g * 1024:(g + 1) * 1024],
                              w_qkv[2048 + g * 1024:2048 + (g + 1) * 1024],
                              w_qkv[4096 + g * 1024:4096 + (g + 1) * 1024]], axis=0)
        wqkvT_h = np.ascontiguousarray(wgq.T)
        theta_h = np.ascontiguousarray(theta[g * 8:(g + 1) * 8].reshape(1, 1024))
        maskT_h = np.empty((128, 1024), f32)
        qdbc_h = np.empty((128, 1024), f32)
        kdbc_h = np.empty((128, 1024), f32)
        for i in range(8):
            s = float(slopes[g * 8 + i])
            m = np.where(diff >= 0, np.exp(np.float32(-s) * diff.astype(f32)), 0.0).astype(f32)
            maskT_h[:, i * 128:(i + 1) * 128] = m.T
            qdbc_h[:, i * 128:(i + 1) * 128] = np.exp(np.float32(-s) * (idx + 1.0).astype(f32))[None, :]
            kdbc_h[:, i * 128:(i + 1) * 128] = np.exp(np.float32(-s) * (CHUNK - 1.0 - idx).astype(f32))[None, :]
        bdcol_h = np.broadcast_to(
            np.exp(-slopes[g * 8:(g + 1) * 8].astype(f32) * np.float32(CHUNK))[None, :],
            (128, 8)).astype(f32)
        msel_h = np.zeros((128, 2), f32)
        msel_h[:, g] = 1.0
        in_maps.append({
            "F_msel": msel_h,
            "F_x": np.ascontiguousarray(xs), "F_winT": winT_h, "F_woutT": woutT_h,
            "F_dcol": dcol_h, "F_bcol": bcol_h, "F_w1T": w1Th, "F_w2T": w2Th,
            "F_w3T": w3Th, "F_wqkvT": wqkvT_h, "F_theta": theta_h, "F_pos": pos_h,
            "F_maskT": maskT_h, "F_qdbc": qdbc_h, "F_kdbc": kdbc_h,
            "F_identb": identb_h, "F_identf": identf_h,
            "F_bdcol": np.ascontiguousarray(bdcol_h), "F_zf32": zf32_h,
            "F_wprojT": wprojT_h, "F_wr": wr_h, "F_wg": wg_h, "F_wu": wu_h,
            "F_wd": wd_h, "F_oh8": oh8_h, "F_onesf": onesf_h, "F_onesb": onesb_h,
            "F_onesc": onesc_h})
    return in_maps


def kernel(**inputs):
    nc = build_fused()
    in_maps = host_prep_fused(inputs)
    res = run_bass_kernel_spmd(nc, in_maps, core_ids=list(range(8)))
    full = np.empty((B, S, D), f32)
    for c in range(8):
        b, h = c // 2, c % 2
        full[b, h * SH:(h + 1) * SH, :] = res.results[c]["F_out"].T
    return full


def kernel_unfused(**inputs):
    x1 = run_launchA(inputs)
    x2 = run_launchB(inputs, x1)
    oc = run_launchC(inputs, x2)
    out = run_launchD(inputs, x2, oc)
    full = np.empty((B, S, D), f32)
    for c in range(8):
        b, h = c // 2, c % 2
        full[b, h * SH:(h + 1) * SH, :] = out[c].T
    return full


# version 07
# speedup vs baseline: 1.1583x; 1.1583x over previous
import numpy as np
import ml_dtypes
import concourse.bass as bass
import concourse.mybir as mybir
from concourse.bass_utils import run_bass_kernel_spmd
from contextlib import ExitStack

F32 = mybir.dt.float32
BF16 = mybir.dt.bfloat16
AF = mybir.ActivationFunctionType
ALU = mybir.AluOpType
AX = mybir.AxisListType

B, S, D = 4, 8192, 2048
NCH = D // 128
H_FFN = 5632
NH = 16
EH = 128
CHUNK = 128
TT = 512
SH = S // 2
NTI = SH // TT
WARM = TT

f32 = np.float32


class Prog:
    ENGS = ("sync", "gpsimd", "vector", "scalar", "tensor")

    def __init__(self):
        self.ops = {e: [] for e in self.ENGS}
        self.last = {}

    def emit(self, eng, fn):
        self.ops[eng].append(fn)

    def run(self, block):
        ops = self.ops

        @block.sync
        def _(e):
            for f in ops["sync"]:
                f(e)

        @block.gpsimd
        def _(e):
            for f in ops["gpsimd"]:
                f(e)

        @block.vector
        def _(e):
            for f in ops["vector"]:
                f(e)

        @block.scalar
        def _(e):
            for f in ops["scalar"]:
                f(e)

        @block.tensor
        def _(e):
            for f in ops["tensor"]:
                f(e)


class Sem:
    def __init__(self, ctx, nc, name):
        self.h = ctx.enter_context(nc.semaphore(name))
        self.n = 0

    def bump(self, k=1):
        self.n += k
        return self.n


AUTOCHAIN = ("vector", "scalar")


def inc(prog, eng, sem, build, k=1):
    if eng in AUTOCHAIN:
        last = prog.last.get(eng)
        if last is not None:
            wait(prog, eng, last[0], last[1])
    v = sem.bump(k)
    h = sem.h

    def f(e, build=build, h=h, k=k):
        build(e).then_inc(h, k)
    prog.emit(eng, f)
    if eng in AUTOCHAIN:
        prog.last[eng] = (sem, v)
    return v


def wait(prog, eng, sem, v):
    h = sem.h
    prog.emit(eng, lambda e, h=h, v=v: e.wait_ge(h, v))


def run_op(prog, eng, build, prog_sems=None):
    assert eng not in AUTOCHAIN, "vector/scalar ops must go through inc()"
    prog.emit(eng, build)


def ln_tile_emit(prog, sems, xr, xb16, xln, xxsc, t1sc, ps, ones_b, onesf, rows, ncols=TT,
                 off=(0, 512, 1024, 1536)):
    sv, st, sc = sems["v"], sems["t"], sems["c"]
    s1 = ps[0:1, off[0]:off[0] + ncols]
    s2 = ps[0:1, off[1]:off[1] + ncols]
    bcm = ps[:, off[2]:off[2] + ncols]
    bcr = ps[:, off[3]:off[3] + ncols]
    mrow = rows[0:1, 0:ncols]
    msq = rows[0:1, ncols:2 * ncols]
    var = rows[0:1, 2 * ncols:3 * ncols]
    srow = rows[0:1, 3 * ncols:4 * ncols]
    rrow = rows[0:1, 4 * ncols:5 * ncols]
    xx_t = {}

    v_cast = inc(prog, "vector", sv, lambda e: e.tensor_copy(xb16[:, 0:NCH * ncols], xr[:, 0:NCH * ncols]))
    wait(prog, "tensor", sv, v_cast)
    for j in range(NCH):
        run_op(prog, "tensor", lambda e, a=s1, b=ones_b, c=xb16[:, j * ncols:(j + 1) * ncols], j=j:
               e.matmul(a, b[:, 0:1], c, start=(j == 0), stop=(j == NCH - 1)))
    for j in range(NCH):
        xsl = xb16[:, j * ncols:(j + 1) * ncols]
        xxj = xxsc[:, (j % 2) * ncols:((j % 2) + 1) * ncols]
        if j >= 2:
            wait(prog, "vector", st, xx_t[j - 2])
        v_sq = inc(prog, "vector", sv, lambda e, a=xxj, b=xsl: e.tensor_mul(a, b, b))
        wait(prog, "tensor", sv, v_sq)
        xx_t[j] = inc(prog, "tensor", st, lambda e, a=s2, b=ones_b, c=xxj, j=j:
                      e.matmul(a, b[:, 0:1], c, start=(j == 0), stop=(j == NCH - 1)))
    wait(prog, "vector", st, xx_t[NCH - 1])
    inc(prog, "vector", sv, lambda e: e.tensor_scalar(mrow, s1, 1.0 / D, None, ALU.mult))
    inc(prog, "vector", sv, lambda e: e.tensor_mul(msq, mrow, mrow))
    v_var = inc(prog, "vector", sv, lambda e: e.scalar_tensor_tensor(var, s2, 1.0 / D, msq, ALU.mult, ALU.subtract))
    wait(prog, "scalar", sv, v_var)
    c_sqrt = inc(prog, "scalar", sc, lambda e: e.activation(srow, var, AF.Sqrt, bias=np.float32(1e-5), scale=np.float32(1.0)))
    wait(prog, "vector", sc, c_sqrt)
    v_rec = inc(prog, "vector", sv, lambda e: e.reciprocal(rrow, srow))
    wait(prog, "tensor", sv, v_rec)
    run_op(prog, "tensor", lambda e: e.matmul(bcm, onesf[0:1, :], mrow, start=True, stop=True))
    t_bc = inc(prog, "tensor", st, lambda e: e.matmul(bcr, onesf[0:1, :], rrow, start=True, stop=True))
    wait(prog, "vector", st, t_bc)
    v_last = None
    for j in range(NCH):
        inc(prog, "vector", sv, lambda e, a=t1sc[:, 0:ncols], x=xr[:, j * ncols:(j + 1) * ncols], m=bcm:
            e.tensor_tensor(a, x, m, ALU.subtract))
        v_last = inc(prog, "vector", sv, lambda e, o=xln[:, j * ncols:(j + 1) * ncols], a=t1sc[:, 0:ncols], r=bcr:
                     e.tensor_tensor(o, a, r, ALU.mult))
    return v_last


def mm_group_emit(prog, sems, wdram, kdim, mcol0, mg, rhs16, ps, psbank0, wbuf, evac=None,
                  pre_mm_wait=None, ncols=TT):
    sv, st = sems["v"], sems["t"]
    nk = kdim // 128
    assert nk % 4 == 0 and 1 <= mg <= 8
    mcw = mg * 128
    KG = 8 if (nk % 8 == 0 and 8 * mcw <= 4096) else 4
    t_last = None
    for kg in range(nk // KG):
        buf = wbuf[kg % 2]
        swp = sems["wp"][kg % 2]
        if sems["_wmm"][kg % 2] is not None:
            wait(prog, "gpsimd", st, sems["_wmm"][kg % 2])
        src = wdram.rearrange("(a p) m -> p a m", p=128)[:, KG * kg:KG * (kg + 1),
                                                        mcol0:mcol0 + mcw]
        w_ld = inc(prog, "gpsimd", swp, lambda e, b=buf[:, 0:KG * mcw], s=src, KG=KG:
                   e.dma_start(out=b.rearrange("p (a m) -> p a m", a=KG), in_=s), 16)
        wait(prog, "tensor", swp, w_ld)
        if pre_mm_wait is not None:
            wait(prog, "tensor", pre_mm_wait[0], pre_mm_wait[1])
            pre_mm_wait = None
        for kl in range(KG):
            ka = kg * KG + kl
            for j in range(mg):
                bld = (lambda e, p=ps[:, (psbank0 + j) * 512:(psbank0 + j) * 512 + ncols],
                       w=buf[:, kl * mcw + j * 128: kl * mcw + (j + 1) * 128],
                       r=rhs16[:, ka * ncols:(ka + 1) * ncols],
                       s=(ka == 0), z=(ka == nk - 1):
                       e.matmul(p, w, r, start=s, stop=z))
                if kl == KG - 1 and j == mg - 1:
                    t_last = inc(prog, "tensor", st, bld)
                    sems["_wmm"][kg % 2] = t_last
                else:
                    run_op(prog, "tensor", bld)
    if evac is not None:
        wait(prog, "vector", st, t_last)
        for j in range(mg):
            evac(j, ps[:, (psbank0 + j) * 512:(psbank0 + j) * 512 + ncols])
    return t_last


def mm_streamed_emit(prog, sems, wdram, kdim, mdim, rhs16, ps, wbuf, evac,
                     pre_mm_wait=None, ncols=TT):
    sv = sems["v"]
    nm = mdim // 128
    assert nm % 4 == 0
    ev_done = [None, None]
    for mq in range(nm // 4):
        par = mq % 2
        def evac4(j, psl, mq=mq):
            evac(mq * 4 + j, psl)
        pw = pre_mm_wait if mq <= 1 else ev_done[par]
        mm_group_emit(prog, sems, wdram, kdim, 512 * mq, 4, rhs16, ps, par * 4, wbuf,
                      evac=evac4, pre_mm_wait=pw, ncols=ncols)
        ev_done[par] = (sv, sv.n)


def build_launchA():
    nc = bass.Bass("TRN2", target_bir_lowering=False)
    NT = NTI + 1
    xin = nc.dram_tensor("A_x", [D, WARM + SH], F32, kind="ExternalInput").ap()
    winT = nc.dram_tensor("A_winT", [D, D], F32, kind="ExternalInput").ap()
    woutT = nc.dram_tensor("A_woutT", [D, D], F32, kind="ExternalInput").ap()
    dcol_d = nc.dram_tensor("A_dcol", [128, NCH], F32, kind="ExternalInput").ap()
    bcol_d = nc.dram_tensor("A_bcol", [128, NCH], F32, kind="ExternalInput").ap()
    onesf_d = nc.dram_tensor("A_onesf", [1, 128], F32, kind="ExternalInput").ap()
    onesb_d = nc.dram_tensor("A_onesb", [128, 1], BF16, kind="ExternalInput").ap()
    xout = nc.dram_tensor("A_out", [D, SH], F32, kind="ExternalOutput").ap()

    with ExitStack() as ctx:
        sb = lambda n, s, d: ctx.enter_context(nc.sbuf_tensor(n, s, d))
        xr = sb("A_xr", [128, NCH * TT], F32)
        xb16 = sb("A_xb16", [128, NCH * TT], BF16)
        xln = sb("A_xln", [128, NCH * TT], BF16)
        q = sb("A_q", [128, NCH * TT], F32)
        stk = sb("A_stk", [128, NCH * TT], F32)
        stb = sb("A_stb", [128, NCH * TT], BF16)
        xxsc = sb("A_xxsc", [128, 2 * TT], BF16)
        t1sc = sb("A_t1sc", [128, TT], F32)
        wbuf = [sb(f"A_wb{i}", [128, 4 * 1024], BF16) for i in range(2)]
        dcol = sb("A_dc", [128, NCH], F32)
        bcol = sb("A_bc", [128, NCH], F32)
        dsig = sb("A_ds", [128, NCH], F32)
        omd = sb("A_om", [128, NCH], F32)
        stcol = sb("A_st", [128, NCH], F32)
        ones_b = sb("A_o1", [128, 1], BF16)
        onesf = sb("A_o2", [1, 128], F32)
        rows = sb("A_rows", [1, 5 * 512], F32)
        ps = ctx.enter_context(nc.psum_tensor("A_ps", [128, 4096], F32))

        sems = {k: Sem(ctx, nc, "A_s" + k) for k in ("x", "w", "v", "t", "c", "o", "k")}
        sems["wp"] = [Sem(ctx, nc, "A_swp0"), Sem(ctx, nc, "A_swp1")]
        sems["_wmm"] = [None, None]
        block = ctx.enter_context(nc.Block())
        prog = Prog()
        sx, sw, sv, st, sc, so = (sems[k] for k in ("x", "w", "v", "t", "c", "o"))

        sk = sems["k"]
        inc(prog, "sync", sk, lambda e: e.dma_start(out=ones_b[:], in_=onesb_d[:]), 16)
        inc(prog, "sync", sk, lambda e: e.dma_start(out=onesf[:], in_=onesf_d[:]), 16)
        inc(prog, "sync", sk, lambda e: e.dma_start(out=dcol[:], in_=dcol_d[:]), 16)
        x0 = inc(prog, "sync", sk, lambda e: e.dma_start(out=bcol[:], in_=bcol_d[:]), 16)
        wait(prog, "scalar", sk, x0)
        wait(prog, "tensor", sk, x0)
        wait(prog, "vector", sk, x0)
        c0 = inc(prog, "scalar", sc, lambda e: e.activation(dsig[:], dcol[:], AF.Sigmoid))
        wait(prog, "vector", sc, c0)
        inc(prog, "vector", sv, lambda e: e.tensor_scalar(omd[:], dsig[:], -1.0, 1.0, ALU.mult, ALU.add))

        v_xr_free = None
        for i in range(NT):
            warm = (i == 0)
            c0_ = i * TT if warm else WARM + (i - 1) * TT
            c1_ = c0_ + TT
            if v_xr_free is not None:
                wait(prog, "sync", sv, v_xr_free)
            src = xin.rearrange("(j p) t -> p j t", p=128)[:, :, c0_:c1_]
            x_ld = inc(prog, "sync", sx, lambda e, s=src:
                       e.dma_start(out=xr.ap().rearrange("p (j t) -> p j t", j=NCH), in_=s), 16)
            wait(prog, "vector", sx, x_ld)
            if not warm:
                wait(prog, "vector", so, so.n)
            v_ln = ln_tile_emit(prog, sems, xr, xb16, xln, xxsc, t1sc, ps, ones_b, onesf, rows)

            v_ev = [None]

            def evac_q(j_abs, psl):
                v_ev[0] = inc(prog, "vector", sv,
                              lambda e, o=q[:, j_abs * TT:(j_abs + 1) * TT], p=psl,
                              s=omd[:, j_abs:j_abs + 1]:
                              e.tensor_scalar(o, p, s, None, ALU.mult))
            mm_streamed_emit(prog, sems, winT, D, D, xln, ps, wbuf, evac_q,
                             pre_mm_wait=(sv, v_ln))
            for j in range(NCH):
                init = 0.0 if warm else stcol[:, j:j + 1]
                dbc = dsig[:, j:j + 1].broadcast_to([128, TT])
                inc(prog, "vector", sv,
                    lambda e, o=stk[:, j * TT:(j + 1) * TT], d=dbc,
                    qq=q[:, j * TT:(j + 1) * TT], ii=init:
                    e.tensor_tensor_scan(o, d, qq, ii, ALU.mult, ALU.add))
            v_scan = inc(prog, "vector", sv, lambda e: e.tensor_copy(stcol[:], stk[:, TT - 1::TT]))
            if warm:
                v_xr_free = v_scan
                continue
            v_stb = inc(prog, "vector", sv, lambda e: e.tensor_copy(stb[:], stk[:]))

            def evac_o(j_abs, psl):
                v_ev[0] = inc(prog, "vector", sv,
                              lambda e, o=q[:, j_abs * TT:(j_abs + 1) * TT], p=psl,
                              s=bcol[:, j_abs:j_abs + 1], x=xr[:, j_abs * TT:(j_abs + 1) * TT]:
                              e.scalar_tensor_tensor(o, p, s, x, ALU.add, ALU.add))
            mm_streamed_emit(prog, sems, woutT, D, D, stb, ps, wbuf, evac_o,
                             pre_mm_wait=(sv, v_stb))
            v_xr_free = v_ev[0]
            wait(prog, "sync", sv, v_xr_free)
            dst = xout.rearrange("(j p) t -> p j t", p=128)[:, :, c0_ - WARM:c1_ - WARM]
            inc(prog, "sync", so, lambda e, d=dst:
                e.dma_start(out=d, in_=q.ap().rearrange("p (j t) -> p j t", j=NCH)), 16)
        wait(prog, "sync", so, so.n)
        prog.run(block)
    return nc


def host_prep_A(inputs):
    x = inputs["L0_attn_x"]
    winT = np.ascontiguousarray(inputs["L0_attn_W_in"].T.astype(f32))
    woutT = np.ascontiguousarray(inputs["L0_attn_W_out"].T.astype(f32))
    dcol = np.ascontiguousarray(inputs["L0_attn_decay"].reshape(NCH, 128).T.astype(f32))
    bcol = np.ascontiguousarray(inputs["L0_attn_bias"].reshape(NCH, 128).T.astype(f32))
    onesf_h = np.ones((1, 128), f32)
    onesb_h = np.ones((128, 1), ml_dtypes.bfloat16)
    in_maps = []
    for c in range(8):
        b, h = c // 2, c % 2
        t0 = h * SH
        xs = np.empty((D, WARM + SH), f32)
        if h == 0:
            xs[:, :WARM] = 0.0
        else:
            xs[:, :WARM] = x[b, t0 - WARM:t0, :].T
        xs[:, WARM:] = x[b, t0:t0 + SH, :].T
        in_maps.append({"A_x": np.ascontiguousarray(xs), "A_winT": winT,
                        "A_woutT": woutT, "A_dcol": dcol, "A_bcol": bcol,
                        "A_onesf": onesf_h, "A_onesb": onesb_h})
    return in_maps


def run_launchA(inputs):
    nc = build_launchA()
    in_maps = host_prep_A(inputs)
    res = run_bass_kernel_spmd(nc, in_maps, core_ids=list(range(8)))
    return [r["A_out"] for r in res.results]


MAGIC = 12582912.0


def quant_weight_emit(prog, sems, wsrc, wdst, kdim, mdim, qbuf, qb2, wqbuf, redc, acc,
                      gcol, ones_c, onesf, ps, grows, qcols=2816):
    sx, sw, sv, st = sems["q"], sems["w"], sems["v"], sems["t"]
    nrow = kdim // 128
    csplits = [(c, min(qcols, mdim - c)) for c in range(0, mdim, qcols)]
    nit = nrow * len(csplits)
    assert nit <= redc.shape[1]
    v_done = sv.n if sv.n > 0 else None
    it = 0
    for r in range(nrow):
        for (c0, cw) in csplits:
            if v_done is not None:
                wait(prog, "sync", sv, v_done)
            x_ld = inc(prog, "sync", sx, lambda e, s=wsrc[r * 128:(r + 1) * 128, c0:c0 + cw], cw=cw:
                       e.dma_start(out=qbuf[:, 0:cw], in_=s), 16)
            wait(prog, "vector", sx, x_ld)
            v_done = inc(prog, "vector", sv,
                         lambda e, cw=cw, it=it: e.reduce_sum(redc[:, it:it + 1], qbuf[:, 0:cw], AX.X,
                                                              apply_absolute_value=True))
            it += 1
    v_done = inc(prog, "vector", sv, lambda e: e.reduce_sum(acc[:], redc[:, 0:nit], AX.X))
    wait(prog, "tensor", sv, v_done)
    psg = ps[0:1, 0:1]
    psb = ps[:, 512:513]
    t_g = inc(prog, "tensor", st, lambda e: e.matmul(psg, ones_c[:, 0:1], acc[:], start=True, stop=True))
    wait(prog, "vector", st, t_g)
    v_pre = inc(prog, "vector", sv, lambda e: e.tensor_scalar(grows[0:1, 0:1], psg, 1.0 / (kdim * mdim), 1e-8,
                                                              ALU.mult, ALU.add))
    wait(prog, "vector", sv, v_pre)
    v_gr = inc(prog, "vector", sv, lambda e: e.reciprocal(grows[0:1, 1:2], grows[0:1, 0:1]))
    wait(prog, "tensor", sv, v_gr)
    t_b = inc(prog, "tensor", st, lambda e: e.matmul(psb, onesf[0:1, :], grows[0:1, 1:2], start=True, stop=True))
    wait(prog, "vector", st, t_b)
    v_gc = inc(prog, "vector", sv, lambda e: e.tensor_copy(gcol[:], psb))
    v_q = sv.n
    st_prev = sems["w"].n if sems["w"].n > 0 else None
    for r in range(nrow):
        for (c0, cw) in csplits:
            if v_q is not None:
                wait(prog, "sync", sv, v_q)
            x_ld = inc(prog, "sync", sx, lambda e, s=wsrc[r * 128:(r + 1) * 128, c0:c0 + cw], cw=cw:
                       e.dma_start(out=qbuf[:, 0:cw], in_=s), 16)
            wait(prog, "vector", sx, x_ld)
            if v_gc is not None:
                wait(prog, "vector", sv, v_gc)
                v_gc = None
            if st_prev is not None:
                wait(prog, "vector", sw, st_prev)
            inc(prog, "vector", sv, lambda e, cw=cw: e.tensor_scalar(qb2[:, 0:cw], qbuf[:, 0:cw],
                                                                     gcol[:, 0:1], 1.49, ALU.mult, ALU.min))
            inc(prog, "vector", sv, lambda e, cw=cw: e.tensor_scalar(qbuf[:, 0:cw], qb2[:, 0:cw],
                                                                     -1.49, MAGIC, ALU.max, ALU.add))
            v_q = inc(prog, "vector", sv, lambda e, cw=cw: e.tensor_scalar(wqbuf[:, 0:cw], qbuf[:, 0:cw],
                                                                           MAGIC, None, ALU.subtract))
            wait(prog, "gpsimd", sv, v_q)
            st_prev = inc(prog, "gpsimd", sw, lambda e, d=wdst[r * 128:(r + 1) * 128, c0:c0 + cw], cw=cw:
                          e.dma_start(out=d, in_=wqbuf[:, 0:cw]), 16)
    return st_prev


def build_launchB(debug=False):
    nc = bass.Bass("TRN2", target_bir_lowering=False)
    x1in = nc.dram_tensor("B_x", [D, SH], F32, kind="ExternalInput").ap()
    w1T = nc.dram_tensor("B_w1T", [D, H_FFN], F32, kind="ExternalInput").ap()
    w2T = nc.dram_tensor("B_w2T", [D, H_FFN], F32, kind="ExternalInput").ap()
    w3T = nc.dram_tensor("B_w3T", [H_FFN, D], F32, kind="ExternalInput").ap()
    if debug:
        dbg_g = nc.dram_tensor("B_dbg_g", [128, 44 * TT], BF16, kind="ExternalOutput").ap()
        dbg_xln = nc.dram_tensor("B_dbg_xln", [128, NCH * TT], BF16, kind="ExternalOutput").ap()
        dbg_w1q = nc.dram_tensor("B_dbg_w1q", [128, H_FFN], BF16, kind="ExternalOutput").ap()
        dbg_x2a = nc.dram_tensor("B_dbg_x2a", [128, TT], F32, kind="ExternalOutput").ap()
        dbg_ssc = nc.dram_tensor("B_dbg_ssc", [128, 4 * TT], F32, kind="ExternalOutput").ap()
        dbg_g0 = nc.dram_tensor("B_dbg_g0", [128, 4 * TT], BF16, kind="ExternalOutput").ap()
    onesf_d = nc.dram_tensor("B_onesf", [1, 128], F32, kind="ExternalInput").ap()
    onesb_d = nc.dram_tensor("B_onesb", [128, 1], BF16, kind="ExternalInput").ap()
    onesc_d = nc.dram_tensor("B_onesc", [128, 1], F32, kind="ExternalInput").ap()
    xout = nc.dram_tensor("B_out", [D, SH], F32, kind="ExternalOutput").ap()
    w1q = nc.dram_tensor("B_w1q", [D, H_FFN], BF16).ap()
    w2q = nc.dram_tensor("B_w2q", [D, H_FFN], BF16).ap()
    w3q = nc.dram_tensor("B_w3q", [H_FFN, D], BF16).ap()
    NHC = H_FFN // 128

    with ExitStack() as ctx:
        sb = lambda n, s, d: ctx.enter_context(nc.sbuf_tensor(n, s, d))
        xr = sb("B_xr", [128, NCH * TT], F32)
        xb16 = sb("B_xb16", [128, NCH * TT], BF16)
        xln = sb("B_xln", [128, NCH * TT], BF16)
        gact = sb("B_g", [128, NHC * TT], BF16)
        xo = sb("B_xo", [128, NCH * TT], F32)
        qbuf = sb("B_qb", [128, 2816], F32)
        qb2 = sb("B_qb2", [128, 2816], F32)
        wqbuf = sb("B_wqb", [128, 2816], BF16)
        ssc = sb("B_ssc", [128, 4 * TT], F32)
        xxsc = sb("B_xxsc", [128, 2 * TT], BF16)
        t1sc = sb("B_t1sc", [128, TT], F32)
        wbuf = [sb(f"B_wb{i}", [128, 4 * 1024], BF16) for i in range(2)]
        redc = sb("B_redc", [128, 96], F32)
        acc = sb("B_acc", [128, 1], F32)
        gcol = sb("B_gcol", [128, 1], F32)
        ones_c = sb("B_oc", [128, 1], F32)
        ones_b = sb("B_o1", [128, 1], BF16)
        onesf = sb("B_o2", [1, 128], F32)
        grows = sb("B_grows", [1, 4], F32)
        rows = sb("B_rows", [1, 5 * 512], F32)
        ps = ctx.enter_context(nc.psum_tensor("B_ps", [128, 4096], F32))

        sems = {k: Sem(ctx, nc, "B_s" + k) for k in ("x", "w", "v", "t", "c", "o", "k", "q")}
        sems["wp"] = [Sem(ctx, nc, "B_swp0"), Sem(ctx, nc, "B_swp1")]
        sems["_wmm"] = [None, None]
        block = ctx.enter_context(nc.Block())
        prog = Prog()
        sx, sw, sv, st, sc, so = (sems[k] for k in ("x", "w", "v", "t", "c", "o"))

        sk = sems["k"]
        inc(prog, "sync", sk, lambda e: e.dma_start(out=ones_b[:], in_=onesb_d[:]), 16)
        inc(prog, "sync", sk, lambda e: e.dma_start(out=onesf[:], in_=onesf_d[:]), 16)
        v_c = inc(prog, "sync", sk, lambda e: e.dma_start(out=ones_c[:], in_=onesc_d[:]), 16)
        wait(prog, "tensor", sk, v_c)
        wait(prog, "vector", sk, v_c)

        quant_weight_emit(prog, sems, w1T, w1q, D, H_FFN, qbuf, qb2, wqbuf, redc, acc,
                          gcol, ones_c, onesf, ps, grows)
        quant_weight_emit(prog, sems, w2T, w2q, D, H_FFN, qbuf, qb2, wqbuf, redc, acc,
                          gcol, ones_c, onesf, ps, grows)
        w_qdone = quant_weight_emit(prog, sems, w3T, w3q, H_FFN, D, qbuf, qb2, wqbuf, redc, acc,
                                    gcol, ones_c, onesf, ps, grows)
        wait(prog, "gpsimd", sw, w_qdone)

        v_xr_free = None
        for i in range(NTI):
            c0_, c1_ = i * TT, (i + 1) * TT
            if v_xr_free is not None:
                wait(prog, "sync", sv, v_xr_free)
            src = x1in.rearrange("(j p) t -> p j t", p=128)[:, :, c0_:c1_]
            x_ld = inc(prog, "sync", sx, lambda e, s=src:
                       e.dma_start(out=xr.ap().rearrange("p (j t) -> p j t", j=NCH), in_=s), 16)
            wait(prog, "vector", sx, x_ld)
            if i > 0:
                wait(prog, "vector", so, so.n)
            v_ln = ln_tile_emit(prog, sems, xr, xb16, xln, xxsc, t1sc, ps, ones_b, onesf, rows)

            pre = (sv, v_ln)
            for gi in range(NHC // 4):
                t1 = mm_group_emit(prog, sems, w1q, D, gi * 512, 4, xln, ps, 0, wbuf,
                                   pre_mm_wait=pre)
                t2 = mm_group_emit(prog, sems, w2q, D, gi * 512, 4, xln, ps, 4, wbuf)
                wait(prog, "scalar", st, t1)
                wait(prog, "scalar", sv, sv.n)
                c_sig = None
                for j in range(4):
                    c_sig = inc(prog, "scalar", sems["c"],
                                lambda e, o=ssc[:, j * TT:(j + 1) * TT], p=ps[:, j * 512:j * 512 + TT]:
                                e.activation(o, p, AF.Sigmoid))
                wait(prog, "vector", st, t2)
                wait(prog, "vector", sems["c"], c_sig)
                v_h = None
                for j in range(4):
                    hh = gi * 4 + j
                    v_h = inc(prog, "vector", sv,
                              lambda e, o=gact[:, hh * TT:(hh + 1) * TT],
                              a=ssc[:, j * TT:(j + 1) * TT], p=ps[:, (4 + j) * 512:(4 + j) * 512 + TT]:
                              e.tensor_tensor(o, a, p, ALU.mult))
                if debug and i == 0 and gi == 0:
                    v_dbg = inc(prog, "vector", sv, lambda e:
                                e.tensor_copy(t1sc[:], ps[:, 4 * 512:4 * 512 + TT]))
                    wait(prog, "sync", sv, v_dbg)
                    inc(prog, "sync", so, lambda e: e.dma_start(out=dbg_x2a[:], in_=t1sc[:]), 16)
                    inc(prog, "sync", so, lambda e: e.dma_start(out=dbg_ssc[:], in_=ssc[:]), 16)
                    inc(prog, "sync", so, lambda e: e.dma_start(out=dbg_g0[:], in_=gact[:, 0:4 * TT]), 16)
                pre = (sv, sv.n)

            v_ev = [None]

            def evac_o(j_abs, psl):
                v_ev[0] = inc(prog, "vector", sv,
                              lambda e, o=xo[:, j_abs * TT:(j_abs + 1) * TT], p=psl,
                              x=xr[:, j_abs * TT:(j_abs + 1) * TT]:
                              e.tensor_tensor(o, x, p, ALU.add))
            mm_streamed_emit(prog, sems, w3q, H_FFN, D, gact, ps, wbuf, evac_o,
                             pre_mm_wait=pre)
            v_xr_free = v_ev[0]
            wait(prog, "sync", sv, v_xr_free)
            dst = xout.rearrange("(j p) t -> p j t", p=128)[:, :, c0_:c1_]
            inc(prog, "sync", so, lambda e, d=dst:
                e.dma_start(out=d, in_=xo.ap().rearrange("p (j t) -> p j t", j=NCH)), 16)
            if debug and i == 0:
                inc(prog, "sync", so, lambda e: e.dma_start(out=dbg_g[:], in_=gact[:]), 16)
                inc(prog, "sync", so, lambda e: e.dma_start(out=dbg_xln[:], in_=xln[:]), 16)
                inc(prog, "sync", so, lambda e: e.dma_start(out=dbg_w1q[:], in_=w2q[0:128, :]), 16)
        wait(prog, "sync", so, so.n)
        prog.run(block)
    return nc


def host_prep_B(inputs, x1_shards):
    w1Th = np.ascontiguousarray(inputs["L0_ffn_w1"].T.astype(f32))
    w2Th = np.ascontiguousarray(inputs["L0_ffn_w2"].T.astype(f32))
    w3Th = np.ascontiguousarray(inputs["L0_ffn_w3"].T.astype(f32))
    onesf_h = np.ones((1, 128), f32)
    onesb_h = np.ones((128, 1), ml_dtypes.bfloat16)
    onesc_h = np.ones((128, 1), f32)
    return [{"B_x": np.ascontiguousarray(x1_shards[c]), "B_w1T": w1Th,
             "B_w2T": w2Th, "B_w3T": w3Th, "B_onesf": onesf_h,
             "B_onesb": onesb_h, "B_onesc": onesc_h} for c in range(8)]


def run_launchB(inputs, x1_shards):
    nc = build_launchB()
    in_maps = host_prep_B(inputs, x1_shards)
    res = run_bass_kernel_spmd(nc, in_maps, core_ids=list(range(8)))
    return [r["B_out"] for r in res.results]


def _slopes16():
    start = 2.0 ** (-2.0 ** (-(np.log2(16) - 3)))
    return np.array([start ** (i + 1) for i in range(16)], dtype=np.float32)


def build_launchC(debug=False, nck=None, no_o1=False, no_o2=False):
    nc = bass.Bass("TRN2", target_bir_lowering=False)
    NCKE = nck if nck is not None else (S // CHUNK)
    if debug:
        dbg = {nm: nc.dram_tensor("C_dbg_" + nm, sh, dt, kind="ExternalOutput").ap()
               for nm, sh, dt in (("xln", [128, 2048], BF16), ("qplo", [128, 1024], BF16),
                                  ("qphi", [128, 1024], BF16), ("kplo", [128, 1024], BF16),
                                  ("vt", [128, 1024], BF16), ("smask", [128, 128], BF16),
                                  ("wq", [128, 3072], BF16), ("cos", [128, 1024], F32),
                                  ("st16", [128, 2048], BF16), ("kdlo", [128, 1024], BF16),
                                  ("obuf0", [128, 1024], BF16), ("qqlo", [128, 1024], BF16),
                                  ("st16b", [128, 2048], BF16), ("ops", [128, 128], F32),
                                  ("stf1", [128, 2048], F32))}
    NCK = S // CHUNK
    x2in = nc.dram_tensor("C_x", [D, S], F32, kind="ExternalInput").ap()
    wqkvT = nc.dram_tensor("C_wqkvT", [D, 3072], F32, kind="ExternalInput").ap()
    theta_d = nc.dram_tensor("C_theta", [1, 1024], F32, kind="ExternalInput").ap()
    pos_d = nc.dram_tensor("C_pos", [1, S], F32, kind="ExternalInput").ap()
    maskT_d = nc.dram_tensor("C_maskT", [128, 1024], F32, kind="ExternalInput").ap()
    qdbc_d = nc.dram_tensor("C_qdbc", [128, 1024], F32, kind="ExternalInput").ap()
    kdbc_d = nc.dram_tensor("C_kdbc", [128, 1024], F32, kind="ExternalInput").ap()
    ident_d = nc.dram_tensor("C_ident", [128, 128], BF16, kind="ExternalInput").ap()
    onesf_d = nc.dram_tensor("C_onesf", [1, 128], F32, kind="ExternalInput").ap()
    onesb_d = nc.dram_tensor("C_onesb", [128, 1], BF16, kind="ExternalInput").ap()
    bdcol_d = nc.dram_tensor("C_bdcol", [128, 8], F32, kind="ExternalInput").ap()
    zf32_d = nc.dram_tensor("C_zf32", [128, 2048], F32, kind="ExternalInput").ap()
    ofm = nc.dram_tensor("C_o", [1024, S], BF16, kind="ExternalOutput").ap()

    with ExitStack() as ctx:
        sb = lambda n, s, d: ctx.enter_context(nc.sbuf_tensor(n, s, d))
        wq16 = sb("C_wq16", [128, NCH * 3072], BF16)
        xr = sb("C_xr", [128, NCH * CHUNK], F32)
        xln = sb("C_xln", [128, NCH * CHUNK], BF16)
        xxsc = sb("C_xxsc", [128, 2 * CHUNK], BF16)
        t1sc = sb("C_t1sc", [128, CHUNK], F32)
        rows = sb("C_rows", [1, 5 * CHUNK], F32)
        cosf = sb("C_cosf", [128, 1024], F32)
        sinf = sb("C_sinf", [128, 1024], F32)
        trg = sb("C_trg", [128, 7 * 512], F32)
        cosq = sb("C_cosq", [128, 1024], F32)
        sinq = sb("C_sinq", [128, 1024], F32)
        cosk = sb("C_cosk", [128, 1024], F32)
        sink = sb("C_sink", [128, 1024], F32)
        qp_lo = sb("C_qplo", [128, 1024], BF16)
        qp_hi = sb("C_qphi", [128, 1024], BF16)
        qq_lo = sb("C_qqlo", [128, 1024], BF16)
        qq_hi = sb("C_qqhi", [128, 1024], BF16)
        kp_lo = sb("C_kplo", [128, 1024], BF16)
        kp_hi = sb("C_kphi", [128, 1024], BF16)
        kd_lo = sb("C_kdlo", [128, 1024], BF16)
        kd_hi = sb("C_kdhi", [128, 1024], BF16)
        vt = sb("C_vt", [128, 1024], BF16)
        stf = [sb(f"C_stf{i}", [128, 8 * 256], F32) for i in range(2)]
        st16 = sb("C_st16", [128, 8 * 256], BF16)
        smask = sb("C_smask", [128, 128], BF16)
        ktm = sb("C_ktm", [128, 256], BF16)
        obuf = sb("C_obuf", [128, 1024], BF16)
        theta = sb("C_th", [1, 1024], F32)
        pos = sb("C_pos_s", [1, CHUNK], F32)
        maskT = sb("C_maskT_s", [128, 1024], F32)
        qdbc = sb("C_qdbc_s", [128, 1024], F32)
        kdbc = sb("C_kdbc_s", [128, 1024], F32)
        idb = sb("C_idb", [128, 128], BF16)
        onesf = sb("C_onesf_s", [1, 128], F32)
        ones_b = sb("C_onesb_s", [128, 1], BF16)
        bdcol = sb("C_bdcol_s", [128, 8], F32)
        ps = ctx.enter_context(nc.psum_tensor("C_ps", [128, 3584], F32))
        psB = ctx.enter_context(nc.psum_tensor("C_psB", [128, 256], BF16))

        LNOFF = (3072, 3200, 3328, 3456)

        sems = {k: Sem(ctx, nc, "C_s" + k) for k in ("x", "w", "v", "t", "c", "o", "k", "p")}
        sems["wp"] = [Sem(ctx, nc, "C_swp0"), Sem(ctx, nc, "C_swp1")]
        sems["_wmm"] = [None, None]
        block = ctx.enter_context(nc.Block())
        prog = Prog()
        sx, sw, sv, st, sc, so = (sems[k] for k in ("x", "w", "v", "t", "c", "o"))

        sk = sems["k"]
        sp = sems["p"]
        for (dst, src) in ((theta, theta_d), (maskT, maskT_d),
                           (qdbc, qdbc_d), (kdbc, kdbc_d), (idb, ident_d),
                           (onesf, onesf_d), (ones_b, onesb_d), (bdcol, bdcol_d)):
            inc(prog, "sync", sk, lambda e, d=dst, s=src: e.dma_start(out=d[:], in_=s[:]), 16)
        inc(prog, "sync", sk, lambda e: e.dma_start(out=stf[0][:], in_=zf32_d[:]), 16)
        ld_consts = sk.n
        for kc in range(NCH):
            inc(prog, "gpsimd", sw, lambda e, kc=kc:
                e.dma_start(out=wq16[:, kc * 3072:(kc + 1) * 3072],
                            in_=wqkvT[kc * 128:(kc + 1) * 128, :]), 16)
        inc(prog, "gpsimd", sw, lambda e: e.dma_start(out=st16[:], in_=zf32_d[:]), 16)
        ld_w = sw.n
        wait(prog, "tensor", sk, ld_consts)
        wait(prog, "vector", sk, ld_consts)
        wait(prog, "scalar", sk, ld_consts)
        wait(prog, "tensor", sw, ld_w)
        wait(prog, "vector", sw, ld_w)

        v_xr_free = None
        v_phase_c_last = None
        v_state_upd = {i: None for i in range(8)}
        c_oevac_last = None
        c_ktm_last = None
        v_smask_last = None
        so_last = 0
        for c in range(NCKE):
            cur, nxt = c % 2, (c + 1) % 2
            if v_xr_free is not None:
                wait(prog, "sync", sv, v_xr_free)
            src = x2in.rearrange("(j p) t -> p j t", p=128)[:, :, c * CHUNK:(c + 1) * CHUNK]
            x_ld = inc(prog, "sync", sx, lambda e, s=src:
                       e.dma_start(out=xr.ap().rearrange("p (j t) -> p j t", j=NCH), in_=s), 16)
            if sems.get("_thguard") is not None:
                wait(prog, "sync", st, sems["_tlast"])
            pos_ld = inc(prog, "sync", sp, lambda e, c=c:
                         e.dma_start(out=pos[:], in_=pos_d[0:1, c * CHUNK:(c + 1) * CHUNK]), 16)
            wait(prog, "vector", sx, x_ld)
            v_ln = ln_tile_emit(prog, sems, xr, xln, xln, xxsc, t1sc, ps, ones_b, onesf,
                                rows, ncols=CHUNK, off=LNOFF)
            v_xr_free = v_ln

            for hh in range(2):
                if hh == 0 and c > 0:
                    wait(prog, "tensor", sc, c_v)
                if hh == 0:
                    wait(prog, "tensor", sp, pos_ld)
                if sems.get("_thguard") is not None:
                    wait(prog, "tensor", sv, sems["_thguard"])
                t_th = None
                for i in range(4):
                    gi = hh * 4 + i
                    t_th = inc(prog, "tensor", st, lambda e, gi=gi, i=i:
                               e.matmul(ps[:, 2048 + i * 128:2048 + (i + 1) * 128],
                                        theta[0:1, gi * 128:(gi + 1) * 128],
                                        pos[0:1, :],
                                        start=True, stop=True))
                sems["_tlast"] = t_th
                wait(prog, "vector", st, t_th)
                wait(prog, "vector", sems["c"], sems["c"].n)
                thp = ps[:, 2048:2560]
                PI2I = float(1.0 / (2 * np.pi))
                PI2 = float(2 * np.pi)
                rsl = trg[:, (3 + 2 * hh) * 512:(4 + 2 * hh) * 512]
                r2sl = trg[:, (4 + 2 * hh) * 512:(5 + 2 * hh) * 512]
                inc(prog, "vector", sv, lambda e: e.tensor_scalar(trg[:, 0:512], thp, PI2I, None, ALU.mult))
                inc(prog, "vector", sv, lambda e: e.tensor_scalar(trg[:, 512:1024], trg[:, 0:512], MAGIC, MAGIC, ALU.add, ALU.subtract))
                inc(prog, "vector", sv, lambda e, rsl=rsl: e.scalar_tensor_tensor(rsl, trg[:, 512:1024], -PI2, thp, ALU.mult, ALU.add))
                inc(prog, "vector", sv, lambda e: e.tensor_scalar(trg[:, 512:1024], trg[:, 0:512], 0.25, MAGIC, ALU.add, ALU.add))
                inc(prog, "vector", sv, lambda e: e.tensor_scalar(trg[:, 1024:1536], trg[:, 512:1024], MAGIC, None, ALU.subtract))
                v_trg = inc(prog, "vector", sv, lambda e, r2sl=r2sl: e.scalar_tensor_tensor(r2sl, trg[:, 1024:1536], -PI2, thp, ALU.mult, ALU.add))
                sems["_thguard"] = v_trg
                PIC = float(np.nextafter(np.float32(np.pi), np.float32(0.0)))
                inc(prog, "vector", sv, lambda e, rsl=rsl: e.tensor_scalar(trg[:, 0:512], rsl, PIC, -PIC, ALU.min, ALU.max))
                inc(prog, "vector", sv, lambda e, r2sl=r2sl: e.tensor_scalar(trg[:, 1024:1536], r2sl, float(np.pi / 2), PIC, ALU.add, ALU.min))
                v_trg = inc(prog, "vector", sv, lambda e: e.tensor_scalar(trg[:, 512:1024], trg[:, 1024:1536], -PIC, None, ALU.max))
                wait(prog, "scalar", sv, v_trg)
                inc(prog, "scalar", sc, lambda e, hh=hh: e.activation(sinf[:, hh * 512:(hh + 1) * 512], trg[:, 0:512], AF.Sin))
                c_trig = inc(prog, "scalar", sc, lambda e, hh=hh: e.activation(cosf[:, hh * 512:(hh + 1) * 512], trg[:, 512:1024], AF.Sin))
            wait(prog, "vector", sc, c_trig)
            inc(prog, "vector", sv, lambda e: e.tensor_tensor(cosq[:], cosf[:], qdbc[:], ALU.mult))
            inc(prog, "vector", sv, lambda e: e.tensor_tensor(sinq[:], sinf[:], qdbc[:], ALU.mult))
            inc(prog, "vector", sv, lambda e: e.tensor_tensor(cosk[:], cosf[:], kdbc[:], ALU.mult))
            v_trig_all = inc(prog, "vector", sv, lambda e: e.tensor_tensor(sink[:], sinf[:], kdbc[:], ALU.mult))

            wait(prog, "tensor", sv, v_trig_all)
            if c_oevac_last is not None:
                wait(prog, "tensor", sc, c_oevac_last)
            for i in range(8):
                for kc in range(NCH):
                    run_op(prog, "tensor", lambda e, kc=kc, i=i:
                           e.matmul(ps[:, i * 128:(i + 1) * 128],
                                    wq16[:, kc * 3072 + i * 128: kc * 3072 + (i + 1) * 128],
                                    xln[:, kc * CHUNK:(kc + 1) * CHUNK],
                                    start=(kc == 0), stop=(kc == NCH - 1)))
            for i in range(8):
                for kc in range(NCH):
                    run_op(prog, "tensor", lambda e, kc=kc, i=i:
                           e.matmul(ps[:, 1024 + i * 128:1024 + (i + 1) * 128],
                                    wq16[:, kc * 3072 + 1024 + i * 128: kc * 3072 + 1024 + (i + 1) * 128],
                                    xln[:, kc * CHUNK:(kc + 1) * CHUNK],
                                    start=(kc == 0), stop=(kc == NCH - 1)))
            for i in range(8):
                for kc in range(NCH):
                    is_last = (kc == NCH - 1) and (i == 7)
                    bld = (lambda e, kc=kc, i=i:
                           e.matmul(ps[:, 2048 + i * 128:2048 + (i + 1) * 128],
                                    xln[:, kc * CHUNK:(kc + 1) * CHUNK],
                                    wq16[:, kc * 3072 + 2048 + i * 128: kc * 3072 + 2048 + (i + 1) * 128],
                                    start=(kc == 0), stop=(kc == NCH - 1)))
                    if is_last:
                        t_qkv = inc(prog, "tensor", st, bld)
                    else:
                        run_op(prog, "tensor", bld)

            wait(prog, "vector", st, t_qkv)
            qps = ps[:, 0:1024]
            kps = ps[:, 1024:2048]
            inc(prog, "vector", sv, lambda e: e.tensor_tensor(qp_lo[:], qps, cosf[:], ALU.mult))
            inc(prog, "vector", sv, lambda e: e.tensor_tensor(qp_hi[:], qps, sinf[:], ALU.mult))
            inc(prog, "vector", sv, lambda e: e.tensor_tensor(qq_lo[:], qps, cosq[:], ALU.mult))
            inc(prog, "vector", sv, lambda e: e.tensor_tensor(qq_hi[:], qps, sinq[:], ALU.mult))
            inc(prog, "vector", sv, lambda e: e.tensor_tensor(kp_lo[:], kps, cosf[:], ALU.mult))
            inc(prog, "vector", sv, lambda e: e.tensor_tensor(kp_hi[:], kps, sinf[:], ALU.mult))
            inc(prog, "vector", sv, lambda e: e.tensor_tensor(kd_lo[:], kps, cosk[:], ALU.mult))
            inc(prog, "vector", sv, lambda e: e.tensor_tensor(kd_hi[:], kps, sink[:], ALU.mult))
            v_st16 = sv.n
            wait(prog, "scalar", st, t_qkv)
            c_v = inc(prog, "scalar", sc, lambda e: e.activation(vt[:], ps[:, 2048:3072], AF.Copy))
            v_phase_c_last = v_st16
            if debug and c == 2:
                wait(prog, "sync", sv, v_st16)
                inc(prog, "sync", so, lambda e: e.dma_start(out=dbg["st16"][:], in_=st16[:]), 16)
                wait(prog, "vector", so, so.n)
            if debug and c == 0:
                wait(prog, "sync", sv, v_phase_c_last)
                inc(prog, "sync", so, lambda e: e.dma_start(out=dbg["kdlo"][:], in_=kd_lo[:]), 16)
                wait(prog, "vector", so, so.n)

            for i in range(8):
                if v_smask_last is not None:
                    wait(prog, "tensor", sv, v_smask_last)
                wait(prog, "tensor", sv, v_phase_c_last)
                wait(prog, "tensor", sc, c_v)
                run_op(prog, "tensor", lambda e, i=i:
                       e.matmul(ps[:, 0:128], kp_lo[:, i * 128:(i + 1) * 128],
                                qp_lo[:, i * 128:(i + 1) * 128], start=True, stop=False))
                t_s = inc(prog, "tensor", st, lambda e, i=i:
                          e.matmul(ps[:, 0:128], kp_hi[:, i * 128:(i + 1) * 128],
                                   qp_hi[:, i * 128:(i + 1) * 128], start=False, stop=True))
                wait(prog, "vector", st, t_s)
                v_smask_last = inc(prog, "vector", sv, lambda e, i=i:
                                   e.tensor_tensor(smask[:], ps[:, 0:128], maskT[:, i * 128:(i + 1) * 128], ALU.mult))
                if debug and c == 0 and i == 0:
                    for nm, srcb in (("xln", xln), ("qplo", qp_lo), ("qphi", qp_hi),
                                     ("kplo", kp_lo), ("vt", vt), ("smask", smask),
                                     ("wq", None), ("cos", cosf)):
                        if nm == "wq":
                            inc(prog, "sync", so, lambda e, d=dbg["wq"]:
                                e.dma_start(out=d[:], in_=wq16[:, 0:3072]), 16)
                            continue
                        wait(prog, "sync", sv, v_smask_last)
                        inc(prog, "sync", so, lambda e, d=dbg[nm], s=srcb: e.dma_start(out=d[:], in_=s[:]), 16)
                    wait(prog, "vector", so, so.n)
                if c_oevac_last is not None:
                    wait(prog, "tensor", sc, c_oevac_last)
                wait(prog, "tensor", sv, v_smask_last)
                o_ops = []
                if not no_o1:
                    o_ops.append((vt[:, i * 128:(i + 1) * 128], smask[:]))
                if not no_o2:
                    v_c16 = inc(prog, "vector", sv, lambda e, i=i, cur=cur:
                                e.tensor_copy(st16[:, i * 256:(i + 1) * 256],
                                              stf[cur][:, i * 256:(i + 1) * 256]))
                    wait(prog, "tensor", sv, v_c16)
                    o_ops.append((st16[:, i * 256:i * 256 + 128], qq_lo[:, i * 128:(i + 1) * 128]))
                    o_ops.append((st16[:, i * 256 + 128:i * 256 + 256], qq_hi[:, i * 128:(i + 1) * 128]))
                t_o = None
                for pi, (lh, rh) in enumerate(o_ops):
                    bld = (lambda e, lh=lh, rh=rh, s=(pi == 0), z=(pi == len(o_ops) - 1):
                           e.matmul(ps[:, 512:640], lh, rh, start=s, stop=z))
                    if pi == len(o_ops) - 1:
                        t_o = inc(prog, "tensor", st, bld)
                    else:
                        run_op(prog, "tensor", bld)
                wait(prog, "scalar", st, t_o)
                if debug and c == 1 and i == 0:
                    wait(prog, "sync", st, t_o)
                    inc(prog, "sync", so, lambda e: e.dma_start(out=dbg["qqlo"][:], in_=qq_lo[:]), 16)
                    inc(prog, "sync", so, lambda e: e.dma_start(out=dbg["st16b"][:], in_=st16[:]), 16)
                    wait(prog, "scalar", so, so.n)
                c_oevac_last = inc(prog, "scalar", sc, lambda e, i=i:
                                   e.activation(obuf[:, i * 128:(i + 1) * 128], ps[:, 512:640], AF.Copy))
                if c_ktm_last is not None:
                    wait(prog, "tensor", sc, c_ktm_last)
                run_op(prog, "tensor", lambda e, i=i:
                       e.transpose(psB[:, 0:128], kd_lo[:, i * 128:(i + 1) * 128], idb[:]))
                t_tp = inc(prog, "tensor", st, lambda e, i=i:
                           e.transpose(psB[:, 128:256], kd_hi[:, i * 128:(i + 1) * 128], idb[:]))
                wait(prog, "scalar", st, t_tp)
                c_ktm_last = inc(prog, "scalar", sc, lambda e: e.activation(ktm[:], psB[:], AF.Copy))
                wait(prog, "tensor", sv, sv.n)
                wait(prog, "tensor", sc, c_ktm_last)
                run_op(prog, "tensor", lambda e, i=i:
                       e.matmul(ps[:, 1024:1152], ktm[:, 0:128], vt[:, i * 128:(i + 1) * 128],
                                start=True, stop=True))
                t_c2 = inc(prog, "tensor", st, lambda e, i=i:
                           e.matmul(ps[:, 1536:1664], ktm[:, 128:256], vt[:, i * 128:(i + 1) * 128],
                                    start=True, stop=True))
                wait(prog, "vector", st, t_c2)
                inc(prog, "vector", sv, lambda e, i=i, cur=cur, nxt=nxt:
                    e.scalar_tensor_tensor(stf[nxt][:, i * 256:i * 256 + 128],
                                           stf[cur][:, i * 256:i * 256 + 128], bdcol[:, i:i + 1],
                                           ps[:, 1024:1152], ALU.mult, ALU.add))
                v_state_upd[i] = inc(prog, "vector", sv, lambda e, i=i, cur=cur, nxt=nxt:
                                     e.scalar_tensor_tensor(stf[nxt][:, i * 256 + 128:i * 256 + 256],
                                                            stf[cur][:, i * 256 + 128:i * 256 + 256], bdcol[:, i:i + 1],
                                                            ps[:, 1536:1664], ALU.mult, ALU.add))
            if debug and c == 0:
                wait(prog, "sync", sc, c_oevac_last)
                inc(prog, "sync", so, lambda e: e.dma_start(out=dbg["obuf0"][:], in_=obuf[:]), 16)
                wait(prog, "sync", sv, v_state_upd[7])
                inc(prog, "sync", so, lambda e, nxt=nxt: e.dma_start(out=dbg["stf1"][:], in_=stf[nxt][:]), 16)
                wait(prog, "vector", so, so.n)
            wait(prog, "sync", sc, c_oevac_last)
            if so_last:
                pass
            dst = ofm.rearrange("(i p) t -> p i t", p=128)[:, :, c * CHUNK:(c + 1) * CHUNK]
            so_last = inc(prog, "sync", so, lambda e, d=dst:
                          e.dma_start(out=d, in_=obuf.ap().rearrange("p (i t) -> p i t", t=CHUNK)), 16)
            wait(prog, "scalar", so, so_last)
        wait(prog, "sync", so, so.n)
        prog.run(block)
    return nc


def host_prep_C(inputs, x2_shards):
    w_qkv = inputs["L1_attn_w_qkv"].astype(f32)
    theta = inputs["L1_attn_theta"].astype(f32).reshape(NH, EH)
    slopes = _slopes16()
    idx = np.arange(CHUNK, dtype=np.float64)
    diff = idx[:, None] - idx[None, :]
    in_maps = []
    onesf_h = np.ones((1, 128), f32)
    onesb_h = np.ones((128, 1), ml_dtypes.bfloat16)
    ident_h = np.eye(128, dtype=ml_dtypes.bfloat16)
    pos_h = np.arange(S, dtype=f32).reshape(1, S)
    for core in range(8):
        b, g = core // 2, core % 2
        hsl = slice(g * 8, (g + 1) * 8)
        wg = np.concatenate([w_qkv[g * 1024:(g + 1) * 1024],
                             w_qkv[2048 + g * 1024:2048 + (g + 1) * 1024],
                             w_qkv[4096 + g * 1024:4096 + (g + 1) * 1024]], axis=0)
        wqkvT_h = np.ascontiguousarray(wg.T)
        theta_h = np.ascontiguousarray(theta[hsl].reshape(1, 1024))
        maskT_h = np.empty((128, 1024), f32)
        qdbc_h = np.empty((128, 1024), f32)
        kdbc_h = np.empty((128, 1024), f32)
        for i in range(8):
            s = float(slopes[g * 8 + i])
            m = np.where(diff >= 0, np.exp(np.float32(-s) * diff.astype(f32)), 0.0).astype(f32)
            maskT_h[:, i * 128:(i + 1) * 128] = m.T
            qdbc_h[:, i * 128:(i + 1) * 128] = np.exp(np.float32(-s) * (idx + 1.0).astype(f32))[None, :]
            kdbc_h[:, i * 128:(i + 1) * 128] = np.exp(np.float32(-s) * (CHUNK - 1.0 - idx).astype(f32))[None, :]
        x2full = np.concatenate([x2_shards[2 * b], x2_shards[2 * b + 1]], axis=1)
        bdcol_h = np.broadcast_to(
            np.exp(-slopes[g * 8:(g + 1) * 8].astype(f32) * np.float32(CHUNK))[None, :],
            (128, 8)).astype(f32)
        in_maps.append({"C_zf32": np.zeros((128, 2048), f32),
                        "C_x": np.ascontiguousarray(x2full), "C_wqkvT": wqkvT_h,
                        "C_theta": theta_h, "C_pos": pos_h, "C_maskT": maskT_h,
                        "C_qdbc": qdbc_h, "C_kdbc": kdbc_h, "C_ident": ident_h,
                        "C_onesf": onesf_h, "C_onesb": onesb_h,
                        "C_bdcol": np.ascontiguousarray(bdcol_h)})
    return in_maps


def run_launchC(inputs, x2_shards):
    nc = build_launchC()
    in_maps = host_prep_C(inputs, x2_shards)
    res = run_bass_kernel_spmd(nc, in_maps, core_ids=list(range(8)))
    return [r["C_o"] for r in res.results]


def build_launchD():
    nc = bass.Bass("TRN2", target_bir_lowering=False)
    x2in = nc.dram_tensor("D_x", [D, SH], F32, kind="ExternalInput").ap()
    oin = nc.dram_tensor("D_o", [D, SH], BF16, kind="ExternalInput").ap()
    wprojT = nc.dram_tensor("D_wprojT", [D, D], F32, kind="ExternalInput").ap()
    wr_d = nc.dram_tensor("D_wr", [D, 8], F32, kind="ExternalInput").ap()
    wg_d = nc.dram_tensor("D_wg", [8, D, D], F32, kind="ExternalInput").ap()
    wu_d = nc.dram_tensor("D_wu", [8, D, D], F32, kind="ExternalInput").ap()
    wd_d = nc.dram_tensor("D_wd", [8, D, D], F32, kind="ExternalInput").ap()
    oh8_d = nc.dram_tensor("D_oh8", [8, 8 * 128], F32, kind="ExternalInput").ap()
    ident_d = nc.dram_tensor("D_ident", [128, 128], F32, kind="ExternalInput").ap()
    onesf_d = nc.dram_tensor("D_onesf", [1, 128], F32, kind="ExternalInput").ap()
    onesb_d = nc.dram_tensor("D_onesb", [128, 1], BF16, kind="ExternalInput").ap()
    xout = nc.dram_tensor("D_out", [D, SH], F32, kind="ExternalOutput").ap()

    with ExitStack() as ctx:
        sb = lambda n, s, d: ctx.enter_context(nc.sbuf_tensor(n, s, d))
        xr = sb("D_xr", [128, NCH * TT], F32)
        ob = sb("D_ob", [128, NCH * TT], BF16)
        x3r = sb("D_x3r", [128, NCH * TT], F32)
        xln = sb("D_xln", [128, NCH * TT], BF16)
        hbuf = sb("D_h", [128, NCH * TT], BF16)
        ssc = sb("D_ssc", [128, 4 * TT], F32)
        xxsc = sb("D_xxsc", [128, 2 * TT], BF16)
        t1sc = sb("D_t1sc", [128, TT], F32)
        wbuf = [sb(f"D_wb{i}", [128, 4 * 1024], BF16) for i in range(2)]
        wr16 = sb("D_wr16", [128, NCH * 8], BF16)
        lgt = sb("D_lgt", [128, 14 * 8], F32)
        wgt8 = sb("D_wgt8", [8, 4 * 128], F32)
        wbc16 = sb("D_wbc", [128, TT], BF16)
        idb = sb("D_idb", [128, 128], F32)
        oh8 = sb("D_oh8_s", [8, 8 * 128], F32)
        onesf = sb("D_onesf_s", [1, 128], F32)
        ones_b = sb("D_onesb_s", [128, 1], BF16)
        rows = sb("D_rows", [1, 5 * 512], F32)
        ps = ctx.enter_context(nc.psum_tensor("D_ps", [128, 4096], F32))

        sems = {k: Sem(ctx, nc, "D_s" + k) for k in ("x", "w", "v", "t", "c", "o", "k")}
        sems["wp"] = [Sem(ctx, nc, "D_swp0"), Sem(ctx, nc, "D_swp1")]
        sems["_wmm"] = [None, None]
        block = ctx.enter_context(nc.Block())
        prog = Prog()
        sx, sw, sv, st, sc, so = (sems[k] for k in ("x", "w", "v", "t", "c", "o"))

        sk = sems["k"]
        for (dst, src) in ((idb, ident_d), (oh8, oh8_d), (onesf, onesf_d), (ones_b, onesb_d)):
            inc(prog, "sync", sk, lambda e, d=dst, s=src: e.dma_start(out=d[:], in_=s[:]), 16)
        inc(prog, "gpsimd", sw, lambda e:
            e.dma_start(out=wr16.ap().rearrange("p (k e) -> p k e", e=8),
                        in_=wr_d.rearrange("(k p) e -> p k e", p=128)), 16)
        ld_consts = sk.n
        ld_wr = sw.n
        wait(prog, "tensor", sk, ld_consts)
        wait(prog, "vector", sk, ld_consts)
        wait(prog, "scalar", sk, ld_consts)
        wait(prog, "tensor", sw, ld_wr)

        v_tile_free = None
        for i in range(NTI):
            c0_, c1_ = i * TT, (i + 1) * TT
            if v_tile_free is not None:
                wait(prog, "sync", sv, v_tile_free)
            srcx = x2in.rearrange("(j p) t -> p j t", p=128)[:, :, c0_:c1_]
            x_ld = inc(prog, "sync", sx, lambda e, s=srcx:
                       e.dma_start(out=xr.ap().rearrange("p (j t) -> p j t", j=NCH), in_=s), 16)
            srco = oin.rearrange("(j p) t -> p j t", p=128)[:, :, c0_:c1_]
            o_ld = inc(prog, "sync", sx, lambda e, s=srco:
                       e.dma_start(out=ob.ap().rearrange("p (j t) -> p j t", j=NCH), in_=s), 16)
            wait(prog, "vector", sx, o_ld)
            if i > 0:
                wait(prog, "vector", so, so.n)

            v_ev = [None]

            def evac_x3(j_abs, psl):
                v_ev[0] = inc(prog, "vector", sv,
                              lambda e, o=x3r[:, j_abs * TT:(j_abs + 1) * TT], p=psl,
                              x=xr[:, j_abs * TT:(j_abs + 1) * TT]:
                              e.tensor_tensor(o, x, p, ALU.add))
            wait(prog, "tensor", sx, o_ld)
            mm_streamed_emit(prog, sems, wprojT, D, D, ob, ps, wbuf, evac_x3,
                             pre_mm_wait=(sv, sv.n))
            v_ln = ln_tile_emit(prog, sems, x3r, xln, xln, xxsc, t1sc, ps, ones_b, onesf, rows)

            wait(prog, "tensor", sv, v_ln)
            for tsub in range(4):
                wait(prog, "tensor", sv, sv.n)
                wait(prog, "tensor", sems["c"], sems["c"].n)
                wait(prog, "vector", st, st.n)
                lgp = ps[0:128, 2048 + tsub * 8: 2056 + tsub * 8]
                t_lg = None
                for kc in range(NCH):
                    t_lg = inc(prog, "tensor", st, lambda e, kc=kc, tsub=tsub, lgp=lgp:
                               e.matmul(lgp, xln[:, kc * TT + tsub * 128: kc * TT + (tsub + 1) * 128],
                                        wr16[:, kc * 8:(kc + 1) * 8],
                                        start=(kc == 0), stop=(kc == NCH - 1)))
                wait(prog, "vector", st, t_lg)
                L = lambda k: lgt[:, k * 8:(k + 1) * 8]
                C1 = lambda k: lgt[:, 96 + k:96 + k + 1]
                inc(prog, "vector", sv, lambda e: e.tensor_copy(L(0), lgp))
                inc(prog, "vector", sv, lambda e: e.reduce_max(C1(0), L(0), AX.X))
                v_sm = inc(prog, "vector", sv, lambda e: e.tensor_scalar(L(1), L(0), C1(0), None, ALU.subtract))
                wait(prog, "scalar", sv, v_sm)
                c_ex = inc(prog, "scalar", sc, lambda e: e.activation(L(2), L(1), AF.Exp))
                wait(prog, "vector", sc, c_ex)
                inc(prog, "vector", sv, lambda e: e.reduce_sum(C1(1), L(2), AX.X))
                inc(prog, "vector", sv, lambda e: e.reciprocal(C1(2), C1(1)))
                inc(prog, "vector", sv, lambda e: e.tensor_scalar(L(3), L(2), C1(2), None, ALU.mult))
                inc(prog, "vector", sv, lambda e: e.reduce_max(C1(3), L(3), AX.X))
                inc(prog, "vector", sv, lambda e: e.tensor_scalar(L(4), L(3), C1(3), None, ALU.is_ge))
                inc(prog, "vector", sv, lambda e: e.tensor_scalar(L(5), L(4), -1.0, 1.0, ALU.mult, ALU.add))
                inc(prog, "vector", sv, lambda e: e.tensor_tensor(L(6), L(3), L(5), ALU.mult))
                inc(prog, "vector", sv, lambda e: e.reduce_max(C1(4), L(6), AX.X))
                inc(prog, "vector", sv, lambda e: e.tensor_scalar(L(7), L(3), C1(4), None, ALU.is_ge))
                inc(prog, "vector", sv, lambda e: e.tensor_tensor(L(8), L(3), L(7), ALU.mult))
                inc(prog, "vector", sv, lambda e: e.tensor_tensor(C1(5), C1(3), C1(4), ALU.add))
                inc(prog, "vector", sv, lambda e: e.reciprocal(C1(6), C1(5)))
                v_wn = inc(prog, "vector", sv, lambda e: e.tensor_scalar(L(9), L(8), C1(6), None, ALU.mult))
                wait(prog, "tensor", sv, v_wn)
                wnp = ps[:, 2176:2304]
                t_tr = inc(prog, "tensor", st, lambda e: e.transpose(wnp[0:8, 0:128], L(9), idb[:]))
                wait(prog, "scalar", st, t_tr)
                c_w = inc(prog, "scalar", sc, lambda e, tsub=tsub:
                          e.activation(wgt8[0:8, tsub * 128:(tsub + 1) * 128], wnp[0:8, 0:128], AF.Copy))
            wait(prog, "vector", sems["c"], sems["c"].n)

            for ex in range(8):
                wait(prog, "tensor", sems["c"], sems["c"].n)
                wait(prog, "tensor", sv, sv.n)
                wbp = ps[:, 2560:2560 + TT]
                t_wb = inc(prog, "tensor", st, lambda e, ex=ex:
                           e.matmul(wbp, oh8[0:8, ex * 128:(ex + 1) * 128], wgt8[0:8, :], start=True, stop=True))
                wait(prog, "scalar", st, t_wb)
                c_wb = inc(prog, "scalar", sc, lambda e: e.activation(wbc16[:], wbp, AF.Copy))
                wait(prog, "vector", sems["c"], c_wb)
                pre = (sv, sv.n)
                for gi in range(4):
                    t1 = mm_group_emit(prog, sems, wg_d[ex], D, gi * 512, 4, xln, ps, 0, wbuf,
                                       pre_mm_wait=pre)
                    t2 = mm_group_emit(prog, sems, wu_d[ex], D, gi * 512, 4, xln, ps, 4, wbuf,
                                       pre_mm_wait=(sems["c"], c_wb) if gi == 0 else None)
                    wait(prog, "scalar", st, t1)
                    wait(prog, "scalar", sv, sv.n)
                    c_sig = None
                    for j in range(4):
                        c_sig = inc(prog, "scalar", sems["c"],
                                    lambda e, o=ssc[:, j * TT:(j + 1) * TT], p=ps[:, j * 512:j * 512 + TT]:
                                    e.activation(o, p, AF.Silu))
                    wait(prog, "vector", st, t2)
                    wait(prog, "vector", sems["c"], c_sig)
                    v_h = None
                    for j in range(4):
                        hh = gi * 4 + j
                        inc(prog, "vector", sv,
                            lambda e, o=t1sc[:], a=ssc[:, j * TT:(j + 1) * TT],
                            p=ps[:, (4 + j) * 512:(4 + j) * 512 + TT]:
                            e.tensor_tensor(o, a, p, ALU.mult))
                        v_h = inc(prog, "vector", sv,
                                  lambda e, o=hbuf[:, hh * TT:(hh + 1) * TT], a=t1sc[:]:
                                  e.tensor_tensor(o, a, wbc16[:], ALU.mult))
                    pre = (sv, v_h)
                src_y = x3r if ex == 0 else (xr if ex % 2 == 1 else x3r)
                dst_y = xr if ex % 2 == 0 else x3r
                v_ev2 = [None]

                def evac_y(j_abs, psl, src_y=src_y, dst_y=dst_y):
                    v_ev2[0] = inc(prog, "vector", sv,
                                   lambda e, o=dst_y[:, j_abs * TT:(j_abs + 1) * TT], p=psl,
                                   x=src_y[:, j_abs * TT:(j_abs + 1) * TT]:
                                   e.tensor_tensor(o, x, p, ALU.add))
                mm_streamed_emit(prog, sems, wd_d[ex], D, D, hbuf, ps, wbuf, evac_y,
                                 pre_mm_wait=pre)
            v_tile_free = v_ev2[0]
            wait(prog, "sync", sv, v_tile_free)
            dst = xout.rearrange("(j p) t -> p j t", p=128)[:, :, c0_:c1_]
            inc(prog, "sync", so, lambda e, d=dst:
                e.dma_start(out=d, in_=x3r.ap().rearrange("p (j t) -> p j t", j=NCH)), 16)
        wait(prog, "sync", so, so.n)
        prog.run(block)
    return nc


def host_prep_D(inputs, x2_shards, o_cores):
    wprojT_h = np.ascontiguousarray(inputs["L1_attn_w_proj"].T.astype(f32))
    wr_h = np.ascontiguousarray(inputs["L1_ffn_Wr"].astype(f32))
    wg_h = np.ascontiguousarray(inputs["L1_ffn_Wg"].astype(f32))
    wu_h = np.ascontiguousarray(inputs["L1_ffn_Wu"].astype(f32))
    wd_h = np.ascontiguousarray(inputs["L1_ffn_Wd"].astype(f32))
    oh8_h = np.zeros((8, 8 * 128), f32)
    for ex in range(8):
        oh8_h[ex, ex * 128:(ex + 1) * 128] = 1.0
    ident_h = np.eye(128, dtype=f32)
    onesf_h = np.ones((1, 128), f32)
    onesb_h = np.ones((128, 1), ml_dtypes.bfloat16)
    in_maps = []
    for core in range(8):
        b, h = core // 2, core % 2
        ofull = np.concatenate([o_cores[2 * b], o_cores[2 * b + 1]], axis=0)
        osh = np.ascontiguousarray(ofull[:, h * SH:(h + 1) * SH])
        in_maps.append({"D_x": np.ascontiguousarray(x2_shards[core]), "D_o": osh,
                        "D_wprojT": wprojT_h, "D_wr": wr_h, "D_wg": wg_h,
                        "D_wu": wu_h, "D_wd": wd_h, "D_oh8": oh8_h,
                        "D_ident": ident_h, "D_onesf": onesf_h, "D_onesb": onesb_h})
    return in_maps


def run_launchD(inputs, x2_shards, o_cores):
    nc = build_launchD()
    in_maps = host_prep_D(inputs, x2_shards, o_cores)
    res = run_bass_kernel_spmd(nc, in_maps, core_ids=list(range(8)))
    return [r["D_out"] for r in res.results]


def build_fused(debug=False):
    nc = bass.Bass("TRN2", target_bir_lowering=False)
    if debug:
        dbg_x2 = nc.dram_tensor("F_dbg_x2", [D, SH], F32, kind="ExternalOutput").ap()
        dbg_cc = nc.dram_tensor("F_dbg_cc", [2 * D, SH], BF16, kind="ExternalOutput").ap()
        dbg_oc = nc.dram_tensor("F_dbg_oc", [2048, S], BF16, kind="ExternalOutput").ap()
        dbg_ci = nc.dram_tensor("F_dbg_ci", [D, SH], BF16, kind="ExternalOutput").ap()
        dbg_xl = nc.dram_tensor("F_dbg_xl", [128, NCH * TT], BF16, kind="ExternalOutput").ap()
        dbg_rw = nc.dram_tensor("F_dbg_rw", [1, 5 * 512], F32, kind="ExternalOutput").ap()
        dbg_rw1 = nc.dram_tensor("F_dbg_rw1", [1, 5 * 512], F32, kind="ExternalOutput").ap()
        dbg_rw2 = nc.dram_tensor("F_dbg_rw2", [1, 5 * 512], F32, kind="ExternalOutput").ap()
    NT = NTI + 1
    NHC = H_FFN // 128
    NCK = S // CHUNK
    QC = 1408

    xin = nc.dram_tensor("F_x", [D, WARM + SH], F32, kind="ExternalInput").ap()
    winT = nc.dram_tensor("F_winT", [D, D], F32, kind="ExternalInput").ap()
    woutT = nc.dram_tensor("F_woutT", [D, D], F32, kind="ExternalInput").ap()
    dcol_d = nc.dram_tensor("F_dcol", [128, NCH], F32, kind="ExternalInput").ap()
    bcol_d = nc.dram_tensor("F_bcol", [128, NCH], F32, kind="ExternalInput").ap()
    w1T = nc.dram_tensor("F_w1T", [D, H_FFN], F32, kind="ExternalInput").ap()
    w2T = nc.dram_tensor("F_w2T", [D, H_FFN], F32, kind="ExternalInput").ap()
    w3T = nc.dram_tensor("F_w3T", [H_FFN, D], F32, kind="ExternalInput").ap()
    wqkvT = nc.dram_tensor("F_wqkvT", [D, 3072], F32, kind="ExternalInput").ap()
    theta_d = nc.dram_tensor("F_theta", [1, 1024], F32, kind="ExternalInput").ap()
    pos_d = nc.dram_tensor("F_pos", [1, S], F32, kind="ExternalInput").ap()
    maskT_d = nc.dram_tensor("F_maskT", [128, 1024], F32, kind="ExternalInput").ap()
    qdbc_d = nc.dram_tensor("F_qdbc", [128, 1024], F32, kind="ExternalInput").ap()
    kdbc_d = nc.dram_tensor("F_kdbc", [128, 1024], F32, kind="ExternalInput").ap()
    identb_d = nc.dram_tensor("F_identb", [128, 128], BF16, kind="ExternalInput").ap()
    identf_d = nc.dram_tensor("F_identf", [128, 128], F32, kind="ExternalInput").ap()
    bdcol_d = nc.dram_tensor("F_bdcol", [128, 8], F32, kind="ExternalInput").ap()
    zf32_d = nc.dram_tensor("F_zf32", [128, 2048], F32, kind="ExternalInput").ap()
    wprojT = nc.dram_tensor("F_wprojT", [D, D], F32, kind="ExternalInput").ap()
    wr_d = nc.dram_tensor("F_wr", [D, 8], F32, kind="ExternalInput").ap()
    wg_d = nc.dram_tensor("F_wg", [8, D, D], F32, kind="ExternalInput").ap()
    wu_d = nc.dram_tensor("F_wu", [8, D, D], F32, kind="ExternalInput").ap()
    wd_d = nc.dram_tensor("F_wd", [8, D, D], F32, kind="ExternalInput").ap()
    oh8_d = nc.dram_tensor("F_oh8", [8, 8 * 128], F32, kind="ExternalInput").ap()
    onesf_d = nc.dram_tensor("F_onesf", [1, 128], F32, kind="ExternalInput").ap()
    onesb_d = nc.dram_tensor("F_onesb", [128, 1], BF16, kind="ExternalInput").ap()
    onesc_d = nc.dram_tensor("F_onesc", [128, 1], F32, kind="ExternalInput").ap()
    msel_d = nc.dram_tensor("F_msel", [128, 2], F32, kind="ExternalInput").ap()
    xout = nc.dram_tensor("F_out", [D, SH], F32, kind="ExternalOutput").ap()
    w1q = nc.dram_tensor("F_w1q", [D, H_FFN], BF16).ap()
    win16 = nc.dram_tensor("F_win16", [D, D], BF16).ap()
    wout16 = nc.dram_tensor("F_wout16", [D, D], BF16).ap()
    wproj16 = nc.dram_tensor("F_wproj16", [D, D], BF16).ap()
    wg16 = nc.dram_tensor("F_wg16", [8, D, D], BF16).ap()
    wu16 = nc.dram_tensor("F_wu16", [8, D, D], BF16).ap()
    wd16 = nc.dram_tensor("F_wd16", [8, D, D], BF16).ap()
    w2q = nc.dram_tensor("F_w2q", [D, H_FFN], BF16).ap()
    w3q = nc.dram_tensor("F_w3q", [H_FFN, D], BF16).ap()
    x2sc = nc.dram_tensor("F_x2", [D, SH], F32).ap()
    ccin = nc.dram_tensor("F_ccin", [D, SH], BF16).ap()
    ccout = nc.dram_tensor("F_ccout", [2 * D, SH], BF16).ap()
    ocin = nc.dram_tensor("F_ocin", [1024, S], BF16).ap()
    ocout = nc.dram_tensor("F_ocout", [2048, S], BF16).ap()
    RG = [[0, 1], [2, 3], [4, 5], [6, 7]]

    with ExitStack() as ctx:
        sb = lambda n, s, d: ctx.enter_context(nc.sbuf_tensor(n, s, d))
        sems = {k: Sem(ctx, nc, "F_s" + k) for k in ("x", "w", "v", "t", "c", "o", "k", "p", "q", "g", "d")}
        sems["wp"] = [Sem(ctx, nc, "F_swp0"), Sem(ctx, nc, "F_swp1")]
        sems["_wmm"] = [None, None]
        block = ctx.enter_context(nc.Block())
        prog = Prog()
        sx, sw, sv, st, sc, so = (sems[k] for k in ("x", "w", "v", "t", "c", "o"))
        sk, sp, sq, sg = sems["k"], sems["p"], sems["q"], sems["g"]
        ones_b = sb("F_o1", [128, 1], BF16)
        onesf = sb("F_o2", [1, 128], F32)

        with ExitStack() as cab:
            sba = lambda n, s, d: cab.enter_context(nc.sbuf_tensor(n, s, d))
            wbuf = [sba(f"F_wbA{i}", [128, 4 * 1024], BF16) for i in range(2)]
            xr = sba("F_xr", [128, NCH * TT], F32)
            xln = sba("F_xln", [128, NCH * TT], BF16)
            q = sba("F_q", [128, NCH * TT], F32)
            stk = sba("F_stk", [128, NCH * TT], BF16)
            gact = sba("F_g", [128, NHC * TT], BF16)
            xxsc = sba("F_xxsc", [128, 2 * TT], BF16)
            t1sc = sba("F_t1sc", [128, TT], F32)
            qbuf = sba("F_qb", [128, QC], F32)
            qb2 = sba("F_qb2", [128, QC], F32)
            wqbuf = sba("F_wqb", [128, QC], BF16)
            ssc = sba("F_ssc", [128, 4 * TT], F32)
            redc = sba("F_redc", [128, 96], F32)
            acc = sba("F_acc", [128, 1], F32)
            gcol = sba("F_gcol", [128, 1], F32)
            ones_c = sba("F_oc", [128, 1], F32)
            grows = sba("F_grows", [1, 4], F32)
            dcol = sba("F_dc", [128, NCH], F32)
            bcol = sba("F_bc", [128, NCH], F32)
            dsig = sba("F_ds", [128, NCH], F32)
            omd = sba("F_om", [128, NCH], F32)
            stcol = sba("F_st", [128, NCH], F32)
            rows = sba("F_rows", [1, 5 * 512], F32)
            ps = cab.enter_context(nc.psum_tensor("F_psAB", [128, 4096], F32))

            inc(prog, "sync", sk, lambda e: e.dma_start(out=ones_b[:], in_=onesb_d[:]), 16)
            inc(prog, "sync", sk, lambda e: e.dma_start(out=onesf[:], in_=onesf_d[:]), 16)
            inc(prog, "sync", sk, lambda e: e.dma_start(out=ones_c[:], in_=onesc_d[:]), 16)
            inc(prog, "sync", sk, lambda e: e.dma_start(out=dcol[:], in_=dcol_d[:]), 16)
            k0 = inc(prog, "sync", sk, lambda e: e.dma_start(out=bcol[:], in_=bcol_d[:]), 16)
            wait(prog, "scalar", sk, k0)
            wait(prog, "tensor", sk, k0)
            wait(prog, "vector", sk, k0)
            c0s = inc(prog, "scalar", sc, lambda e: e.activation(dsig[:], dcol[:], AF.Sigmoid))
            wait(prog, "vector", sc, c0s)
            inc(prog, "vector", sv, lambda e: e.tensor_scalar(omd[:], dsig[:], -1.0, 1.0, ALU.mult, ALU.add))
            sd = sems["d"]
            inc(prog, "gpsimd", sd, lambda e: e.dma_start(out=win16[:, :], in_=winT[:, :]), 16)
            sd_win = inc(prog, "gpsimd", sd, lambda e: e.dma_start(out=wout16[:, :], in_=woutT[:, :]), 16)
            wait(prog, "gpsimd", sd, sd_win)
            conv_jobs = [(wproj16, wprojT)]
            for ex in range(8):
                conv_jobs += [(wg16[ex], wg_d[ex]), (wu16[ex], wu_d[ex]), (wd16[ex], wd_d[ex])]

            quant_weight_emit(prog, sems, w1T, w1q, D, H_FFN, qbuf, qb2, wqbuf, redc, acc,
                              gcol, ones_c, onesf, ps, grows, qcols=QC)
            quant_weight_emit(prog, sems, w2T, w2q, D, H_FFN, qbuf, qb2, wqbuf, redc, acc,
                              gcol, ones_c, onesf, ps, grows, qcols=QC)
            w_qdone = quant_weight_emit(prog, sems, w3T, w3q, H_FFN, D, qbuf, qb2, wqbuf, redc, acc,
                                        gcol, ones_c, onesf, ps, grows, qcols=QC)
            wait(prog, "gpsimd", sw, w_qdone)

            v_xr_free = None
            for i in range(NT):
                warm = (i == 0)
                c0_ = i * TT if warm else WARM + (i - 1) * TT
                c1_ = c0_ + TT
                if v_xr_free is not None:
                    wait(prog, "sync", sv, v_xr_free)
                for _ in range(3):
                    if conv_jobs:
                        cdst, csrc = conv_jobs.pop(0)
                        inc(prog, "gpsimd", sd, lambda e, d=cdst, s=csrc:
                            e.dma_start(out=d[:, :], in_=s[:, :]), 16)
                src = xin.rearrange("(j p) t -> p j t", p=128)[:, :, c0_:c1_]
                x_ld = inc(prog, "sync", sx, lambda e, s=src:
                           e.dma_start(out=xr.ap().rearrange("p (j t) -> p j t", j=NCH), in_=s), 16)
                wait(prog, "vector", sx, x_ld)
                if not warm:
                    wait(prog, "vector", so, so.n)
                v_ln = ln_tile_emit(prog, sems, xr, xln, xln, xxsc, t1sc, ps, ones_b, onesf, rows)
                if debug and i == 1:
                    wait(prog, "sync", sv, v_ln)
                    inc(prog, "sync", so, lambda e: e.dma_start(out=dbg_rw1[:], in_=rows[:]), 16)
                v_ev = [None]

                def evac_q(j_abs, psl):
                    v_ev[0] = inc(prog, "vector", sv,
                                  lambda e, o=q[:, j_abs * TT:(j_abs + 1) * TT], p=psl,
                                  s=omd[:, j_abs:j_abs + 1]:
                                  e.tensor_scalar(o, p, s, None, ALU.mult))
                mm_streamed_emit(prog, sems, win16, D, D, xln, ps, wbuf, evac_q,
                                 pre_mm_wait=(sv, v_ln))
                for j in range(NCH):
                    init = 0.0 if warm else stcol[:, j:j + 1]
                    dbc = dsig[:, j:j + 1].broadcast_to([128, TT])
                    inc(prog, "vector", sv,
                        lambda e, o=stk[:, j * TT:(j + 1) * TT], d=dbc,
                        qq=q[:, j * TT:(j + 1) * TT], ii=init:
                        e.tensor_tensor_scan(o, d, qq, ii, ALU.mult, ALU.add))
                v_scan = inc(prog, "vector", sv, lambda e: e.tensor_copy(stcol[:], stk[:, TT - 1::TT]))
                if warm:
                    v_xr_free = v_scan
                    continue

                def evac_x1(j_abs, psl):
                    v_ev[0] = inc(prog, "vector", sv,
                                  lambda e, o=q[:, j_abs * TT:(j_abs + 1) * TT], p=psl,
                                  s=bcol[:, j_abs:j_abs + 1], x=xr[:, j_abs * TT:(j_abs + 1) * TT]:
                                  e.scalar_tensor_tensor(o, p, s, x, ALU.add, ALU.add))
                mm_streamed_emit(prog, sems, wout16, D, D, stk, ps, wbuf, evac_x1,
                                 pre_mm_wait=(sv, v_scan))
                v_ln2 = ln_tile_emit(prog, sems, q, xln, xln, xxsc, t1sc, ps, ones_b, onesf, rows)
                if debug and i == 1:
                    wait(prog, "sync", sv, v_ln2)
                    inc(prog, "sync", so, lambda e: e.dma_start(out=dbg_rw2[:], in_=rows[:]), 16)
                pre = (sv, v_ln2)
                ev2 = [pre, pre]
                for gi in range(NHC // 2):
                    par = gi % 2
                    b0 = par * 4
                    t1 = mm_group_emit(prog, sems, w1q, D, gi * 256, 2, xln, ps, b0, wbuf,
                                       pre_mm_wait=ev2[par])
                    t2 = mm_group_emit(prog, sems, w2q, D, gi * 256, 2, xln, ps, b0 + 2, wbuf)
                    wait(prog, "scalar", st, t1)
                    wait(prog, "scalar", sv, sv.n)
                    c_sig = None
                    for j in range(2):
                        c_sig = inc(prog, "scalar", sems["c"],
                                    lambda e, o=ssc[:, (par * 2 + j) * TT:(par * 2 + j + 1) * TT],
                                    p=ps[:, (b0 + j) * 512:(b0 + j) * 512 + TT]:
                                    e.activation(o, p, AF.Sigmoid))
                    wait(prog, "vector", st, t2)
                    wait(prog, "vector", sems["c"], c_sig)
                    v_h = None
                    for j in range(2):
                        hh = gi * 2 + j
                        v_h = inc(prog, "vector", sv,
                                  lambda e, o=gact[:, hh * TT:(hh + 1) * TT],
                                  a=ssc[:, (par * 2 + j) * TT:(par * 2 + j + 1) * TT],
                                  p=ps[:, (b0 + 2 + j) * 512:(b0 + 2 + j) * 512 + TT]:
                                  e.tensor_tensor(o, a, p, ALU.mult))
                    ev2[par] = (sv, v_h)
                pre = (sv, v_h)

                def evac_x2(j_abs, psl):
                    v_ev[0] = inc(prog, "vector", sv,
                                  lambda e, o=xr[:, j_abs * TT:(j_abs + 1) * TT], p=psl,
                                  x=q[:, j_abs * TT:(j_abs + 1) * TT]:
                                  e.tensor_tensor(o, x, p, ALU.add))
                mm_streamed_emit(prog, sems, w3q, H_FFN, D, gact, ps, wbuf, evac_x2,
                                 pre_mm_wait=pre)
                v_x2 = v_ev[0]
                wait(prog, "sync", sv, v_x2)
                dx2 = x2sc.rearrange("(j p) t -> p j t", p=128)[:, :, c0_ - WARM:c1_ - WARM]
                inc(prog, "sync", so, lambda e, d=dx2:
                    e.dma_start(out=d, in_=xr.ap().rearrange("p (j t) -> p j t", j=NCH)), 16)
                v_ln3 = ln_tile_emit(prog, sems, xr, xln, gact, xxsc, t1sc, ps, ones_b, onesf, rows,
                                     off=(2048, 2560, 3072, 3584))
                v_xr_free = v_ln3
                wait(prog, "sync", sv, v_ln3)
                dln = ccin.rearrange("(j p) t -> p j t", p=128)[:, :, c0_ - WARM:c1_ - WARM]
                inc(prog, "sync", so, lambda e, d=dln:
                    e.dma_start(out=d, in_=gact.ap().rearrange("p (j t) -> p j t", t=TT)[:, 0:NCH, :]), 16)
                if debug and i == 1:
                    inc(prog, "sync", so, lambda e: e.dma_start(out=dbg_xl[:], in_=xln[:]), 16)
                    inc(prog, "sync", so, lambda e: e.dma_start(out=dbg_rw[:], in_=rows[:]), 16)
            while conv_jobs:
                cdst, csrc = conv_jobs.pop(0)
                inc(prog, "gpsimd", sd, lambda e, d=cdst, s=csrc:
                    e.dma_start(out=d[:, :], in_=s[:, :]), 16)
            SD_ALL = sems["d"].n
            V_ab = so.n

        wait(prog, "gpsimd", so, V_ab)
        inc(prog, "gpsimd", sg, lambda e: e.collective_compute(
            "AllGather", ALU.bypass, replica_groups=RG,
            ins=[ccin.opt()], outs=[ccout.opt()]))
        CC1 = sg.n

        with ExitStack() as ccx:
            sbc = lambda n, s, d: ccx.enter_context(nc.sbuf_tensor(n, s, d))
            wq16 = sbc("F_wq16", [128, NCH * 3072], BF16)
            xlnC = sbc("F_xlnC", [128, NCH * CHUNK], BF16)
            cosf = sbc("F_cosf", [128, 1024], F32)
            sinf = sbc("F_sinf", [128, 1024], F32)
            trg = sbc("F_trg", [128, 7 * 512], F32)
            cosq = sbc("F_cosq", [128, 1024], F32)
            sinq = sbc("F_sinq", [128, 1024], F32)
            cosk = sbc("F_cosk", [128, 1024], F32)
            sink = sbc("F_sink", [128, 1024], F32)
            qp_lo = sbc("F_qplo", [128, 1024], BF16)
            qp_hi = sbc("F_qphi", [128, 1024], BF16)
            qq_lo = sbc("F_qqlo", [128, 1024], BF16)
            qq_hi = sbc("F_qqhi", [128, 1024], BF16)
            kp_lo = sbc("F_kplo", [128, 1024], BF16)
            kp_hi = sbc("F_kphi", [128, 1024], BF16)
            kd_lo = sbc("F_kdlo", [128, 1024], BF16)
            kd_hi = sbc("F_kdhi", [128, 1024], BF16)
            vt = sbc("F_vt", [128, 1024], BF16)
            stf = [sbc(f"F_stf{i}", [128, 8 * 256], F32) for i in range(2)]
            st16 = sbc("F_st16", [128, 8 * 256], BF16)
            smask = sbc("F_smask", [128, 128], BF16)
            ktm = sbc("F_ktm", [128, 256], BF16)
            obuf = sbc("F_obuf", [128, 1024], BF16)
            theta = sbc("F_th", [1, 1024], F32)
            pos = sbc("F_pos_s", [1, CHUNK], F32)
            maskT = sbc("F_maskT_s", [128, 1024], F32)
            qdbc = sbc("F_qdbc_s", [128, 1024], F32)
            kdbc = sbc("F_kdbc_s", [128, 1024], F32)
            idb = sbc("F_idb", [128, 128], BF16)
            bdcol = sbc("F_bdcol_s", [128, 8], F32)
            psC = ccx.enter_context(nc.psum_tensor("F_psC", [128, 3584], F32))
            psB = ccx.enter_context(nc.psum_tensor("F_psB", [128, 256], BF16))

            for eng in ("sync", "gpsimd", "vector", "scalar", "tensor"):
                wait(prog, eng, so, V_ab)
            for (dst, src2) in ((theta, theta_d), (maskT, maskT_d),
                                (qdbc, qdbc_d), (kdbc, kdbc_d), (idb, identb_d),
                                (bdcol, bdcol_d)):
                inc(prog, "sync", sk, lambda e, d=dst, s=src2: e.dma_start(out=d[:], in_=s[:]), 16)
            inc(prog, "sync", sk, lambda e: e.dma_start(out=stf[0][:], in_=zf32_d[:]), 16)
            ldC = sk.n
            for kc in range(NCH):
                inc(prog, "gpsimd", sw, lambda e, kc=kc:
                    e.dma_start(out=wq16[:, kc * 3072:(kc + 1) * 3072],
                                in_=wqkvT[kc * 128:(kc + 1) * 128, :]), 16)
            inc(prog, "gpsimd", sw, lambda e: e.dma_start(out=st16[:], in_=zf32_d[:]), 16)
            ldCw = sw.n
            wait(prog, "tensor", sk, ldC)
            wait(prog, "vector", sk, ldC)
            wait(prog, "scalar", sk, ldC)
            wait(prog, "tensor", sw, ldCw)
            wait(prog, "vector", sw, ldCw)
            wait(prog, "sync", sg, CC1)

            sems["_thguard"] = None
            v_xln_free = None
            v_phase_c_last = None
            c_oevac_last = None
            c_ktm_last = None
            v_smask_last = None
            c_v = None
            for c in range(NCK):
                cur, nxt = c % 2, (c + 1) % 2
                half, tl = c // 32, c % 32
                if v_xln_free is not None:
                    wait(prog, "sync", st, v_xln_free)
                src2 = ccout[half * D:(half + 1) * D, :].rearrange(
                    "(j p) t -> p j t", p=128)[:, :, tl * CHUNK:(tl + 1) * CHUNK]
                x_ld = inc(prog, "sync", sx, lambda e, s=src2:
                           e.dma_start(out=xlnC.ap().rearrange("p (j t) -> p j t", j=NCH), in_=s), 16)
                if sems.get("_thguard") is not None:
                    wait(prog, "sync", st, sems["_tlast"])
                pos_ld = inc(prog, "sync", sp, lambda e, c=c:
                             e.dma_start(out=pos[:], in_=pos_d[0:1, c * CHUNK:(c + 1) * CHUNK]), 16)
                for hh in range(2):
                    if hh == 0 and c > 0:
                        wait(prog, "tensor", sc, c_v)
                    if hh == 0:
                        wait(prog, "tensor", sp, pos_ld)
                    if sems.get("_thguard") is not None:
                        wait(prog, "tensor", sv, sems["_thguard"])
                    t_th = None
                    for i in range(4):
                        gi = hh * 4 + i
                        t_th = inc(prog, "tensor", st, lambda e, gi=gi, i=i:
                                   e.matmul(psC[:, 2048 + i * 128:2048 + (i + 1) * 128],
                                            theta[0:1, gi * 128:(gi + 1) * 128],
                                            pos[0:1, :], start=True, stop=True))
                    sems["_tlast"] = t_th
                    wait(prog, "vector", st, t_th)
                    wait(prog, "vector", sems["c"], sems["c"].n)
                    thp = psC[:, 2048:2560]
                    PI2I = float(1.0 / (2 * np.pi))
                    PI2 = float(2 * np.pi)
                    rsl = trg[:, (3 + 2 * hh) * 512:(4 + 2 * hh) * 512]
                    r2sl = trg[:, (4 + 2 * hh) * 512:(5 + 2 * hh) * 512]
                    inc(prog, "vector", sv, lambda e: e.tensor_scalar(trg[:, 0:512], thp, PI2I, None, ALU.mult))
                    inc(prog, "vector", sv, lambda e: e.tensor_scalar(trg[:, 512:1024], trg[:, 0:512], MAGIC, MAGIC, ALU.add, ALU.subtract))
                    inc(prog, "vector", sv, lambda e, rsl=rsl: e.scalar_tensor_tensor(rsl, trg[:, 512:1024], -PI2, thp, ALU.mult, ALU.add))
                    inc(prog, "vector", sv, lambda e: e.tensor_scalar(trg[:, 512:1024], trg[:, 0:512], 0.25, MAGIC, ALU.add, ALU.add))
                    inc(prog, "vector", sv, lambda e: e.tensor_scalar(trg[:, 1024:1536], trg[:, 512:1024], MAGIC, None, ALU.subtract))
                    v_trg = inc(prog, "vector", sv, lambda e, r2sl=r2sl: e.scalar_tensor_tensor(r2sl, trg[:, 1024:1536], -PI2, thp, ALU.mult, ALU.add))
                    sems["_thguard"] = v_trg
                    PIC = float(np.nextafter(np.float32(np.pi), np.float32(0.0)))
                    inc(prog, "vector", sv, lambda e, rsl=rsl: e.tensor_scalar(trg[:, 0:512], rsl, PIC, -PIC, ALU.min, ALU.max))
                    inc(prog, "vector", sv, lambda e, r2sl=r2sl: e.tensor_scalar(trg[:, 1024:1536], r2sl, float(np.pi / 2), PIC, ALU.add, ALU.min))
                    v_trg = inc(prog, "vector", sv, lambda e: e.tensor_scalar(trg[:, 512:1024], trg[:, 1024:1536], -PIC, None, ALU.max))
                    wait(prog, "scalar", sv, v_trg)
                    inc(prog, "scalar", sc, lambda e, hh=hh: e.activation(sinf[:, hh * 512:(hh + 1) * 512], trg[:, 0:512], AF.Sin))
                    c_trig = inc(prog, "scalar", sc, lambda e, hh=hh: e.activation(cosf[:, hh * 512:(hh + 1) * 512], trg[:, 512:1024], AF.Sin))
                wait(prog, "vector", sc, c_trig)
                inc(prog, "vector", sv, lambda e: e.tensor_tensor(cosq[:], cosf[:], qdbc[:], ALU.mult))
                inc(prog, "vector", sv, lambda e: e.tensor_tensor(sinq[:], sinf[:], qdbc[:], ALU.mult))
                inc(prog, "vector", sv, lambda e: e.tensor_tensor(cosk[:], cosf[:], kdbc[:], ALU.mult))
                v_trig_all = inc(prog, "vector", sv, lambda e: e.tensor_tensor(sink[:], sinf[:], kdbc[:], ALU.mult))

                wait(prog, "tensor", sv, v_trig_all)
                wait(prog, "tensor", sx, x_ld)
                if c_oevac_last is not None:
                    wait(prog, "tensor", sc, c_oevac_last)
                for i in range(8):
                    for kc in range(NCH):
                        run_op(prog, "tensor", lambda e, kc=kc, i=i:
                               e.matmul(psC[:, i * 128:(i + 1) * 128],
                                        wq16[:, kc * 3072 + i * 128: kc * 3072 + (i + 1) * 128],
                                        xlnC[:, kc * CHUNK:(kc + 1) * CHUNK],
                                        start=(kc == 0), stop=(kc == NCH - 1)))
                for i in range(8):
                    for kc in range(NCH):
                        run_op(prog, "tensor", lambda e, kc=kc, i=i:
                               e.matmul(psC[:, 1024 + i * 128:1024 + (i + 1) * 128],
                                        wq16[:, kc * 3072 + 1024 + i * 128: kc * 3072 + 1024 + (i + 1) * 128],
                                        xlnC[:, kc * CHUNK:(kc + 1) * CHUNK],
                                        start=(kc == 0), stop=(kc == NCH - 1)))
                t_qkv = None
                for i in range(8):
                    for kc in range(NCH):
                        is_last = (kc == NCH - 1) and (i == 7)
                        bld = (lambda e, kc=kc, i=i:
                               e.matmul(psC[:, 2048 + i * 128:2048 + (i + 1) * 128],
                                        xlnC[:, kc * CHUNK:(kc + 1) * CHUNK],
                                        wq16[:, kc * 3072 + 2048 + i * 128: kc * 3072 + 2048 + (i + 1) * 128],
                                        start=(kc == 0), stop=(kc == NCH - 1)))
                        if is_last:
                            t_qkv = inc(prog, "tensor", st, bld)
                        else:
                            run_op(prog, "tensor", bld)
                v_xln_free = t_qkv

                wait(prog, "vector", st, t_qkv)
                qps = psC[:, 0:1024]
                kps = psC[:, 1024:2048]
                inc(prog, "vector", sv, lambda e: e.tensor_tensor(qp_lo[:], qps, cosf[:], ALU.mult))
                inc(prog, "vector", sv, lambda e: e.tensor_tensor(qp_hi[:], qps, sinf[:], ALU.mult))
                inc(prog, "vector", sv, lambda e: e.tensor_tensor(qq_lo[:], qps, cosq[:], ALU.mult))
                inc(prog, "vector", sv, lambda e: e.tensor_tensor(qq_hi[:], qps, sinq[:], ALU.mult))
                inc(prog, "vector", sv, lambda e: e.tensor_tensor(kp_lo[:], kps, cosf[:], ALU.mult))
                inc(prog, "vector", sv, lambda e: e.tensor_tensor(kp_hi[:], kps, sinf[:], ALU.mult))
                inc(prog, "vector", sv, lambda e: e.tensor_tensor(kd_lo[:], kps, cosk[:], ALU.mult))
                v_phase_c_last = inc(prog, "vector", sv, lambda e: e.tensor_tensor(kd_hi[:], kps, sink[:], ALU.mult))
                wait(prog, "scalar", st, t_qkv)
                c_v = inc(prog, "scalar", sc, lambda e: e.activation(vt[:], psC[:, 2048:3072], AF.Copy))

                for i in range(8):
                    if v_smask_last is not None:
                        wait(prog, "tensor", sv, v_smask_last)
                    wait(prog, "tensor", sv, v_phase_c_last)
                    wait(prog, "tensor", sc, c_v)
                    run_op(prog, "tensor", lambda e, i=i:
                           e.matmul(psC[:, 0:128], kp_lo[:, i * 128:(i + 1) * 128],
                                    qp_lo[:, i * 128:(i + 1) * 128], start=True, stop=False))
                    t_s = inc(prog, "tensor", st, lambda e, i=i:
                              e.matmul(psC[:, 0:128], kp_hi[:, i * 128:(i + 1) * 128],
                                       qp_hi[:, i * 128:(i + 1) * 128], start=False, stop=True))
                    wait(prog, "vector", st, t_s)
                    v_smask_last = inc(prog, "vector", sv, lambda e, i=i:
                                       e.tensor_tensor(smask[:], psC[:, 0:128], maskT[:, i * 128:(i + 1) * 128], ALU.mult))
                    if c_oevac_last is not None:
                        wait(prog, "tensor", sc, c_oevac_last)
                    wait(prog, "tensor", sv, v_smask_last)
                    v_c16 = inc(prog, "vector", sv, lambda e, i=i, cur=cur:
                                e.tensor_copy(st16[:, i * 256:(i + 1) * 256],
                                              stf[cur][:, i * 256:(i + 1) * 256]))
                    wait(prog, "tensor", sv, v_c16)
                    run_op(prog, "tensor", lambda e, i=i:
                           e.matmul(psC[:, 512:640], vt[:, i * 128:(i + 1) * 128], smask[:], start=True, stop=False))
                    run_op(prog, "tensor", lambda e, i=i:
                           e.matmul(psC[:, 512:640], st16[:, i * 256:i * 256 + 128],
                                    qq_lo[:, i * 128:(i + 1) * 128], start=False, stop=False))
                    t_o = inc(prog, "tensor", st, lambda e, i=i:
                              e.matmul(psC[:, 512:640], st16[:, i * 256 + 128:i * 256 + 256],
                                       qq_hi[:, i * 128:(i + 1) * 128], start=False, stop=True))
                    wait(prog, "scalar", st, t_o)
                    c_oevac_last = inc(prog, "scalar", sc, lambda e, i=i:
                                       e.activation(obuf[:, i * 128:(i + 1) * 128], psC[:, 512:640], AF.Copy))
                    if c_ktm_last is not None:
                        wait(prog, "tensor", sc, c_ktm_last)
                    run_op(prog, "tensor", lambda e, i=i:
                           e.transpose(psB[:, 0:128], kd_lo[:, i * 128:(i + 1) * 128], idb[:]))
                    t_tp = inc(prog, "tensor", st, lambda e, i=i:
                               e.transpose(psB[:, 128:256], kd_hi[:, i * 128:(i + 1) * 128], idb[:]))
                    wait(prog, "scalar", st, t_tp)
                    c_ktm_last = inc(prog, "scalar", sc, lambda e: e.activation(ktm[:], psB[:], AF.Copy))
                    wait(prog, "tensor", sv, sv.n)
                    wait(prog, "tensor", sc, c_ktm_last)
                    run_op(prog, "tensor", lambda e, i=i:
                           e.matmul(psC[:, 1024:1152], ktm[:, 0:128], vt[:, i * 128:(i + 1) * 128],
                                    start=True, stop=True))
                    t_c2 = inc(prog, "tensor", st, lambda e, i=i:
                               e.matmul(psC[:, 1536:1664], ktm[:, 128:256], vt[:, i * 128:(i + 1) * 128],
                                        start=True, stop=True))
                    wait(prog, "vector", st, t_c2)
                    inc(prog, "vector", sv, lambda e, i=i, cur=cur, nxt=nxt:
                        e.scalar_tensor_tensor(stf[nxt][:, i * 256:i * 256 + 128],
                                               stf[cur][:, i * 256:i * 256 + 128], bdcol[:, i:i + 1],
                                               psC[:, 1024:1152], ALU.mult, ALU.add))
                    inc(prog, "vector", sv, lambda e, i=i, cur=cur, nxt=nxt:
                        e.scalar_tensor_tensor(stf[nxt][:, i * 256 + 128:i * 256 + 256],
                                               stf[cur][:, i * 256 + 128:i * 256 + 256], bdcol[:, i:i + 1],
                                               psC[:, 1536:1664], ALU.mult, ALU.add))
                wait(prog, "sync", sc, c_oevac_last)
                dsto = ocin.rearrange("(i p) t -> p i t", p=128)[:, :, c * CHUNK:(c + 1) * CHUNK]
                so_last = inc(prog, "sync", so, lambda e, d=dsto:
                              e.dma_start(out=d, in_=obuf.ap().rearrange("p (i t) -> p i t", t=CHUNK)), 16)
                wait(prog, "scalar", so, so_last)
            V_c = so.n

        wait(prog, "gpsimd", so, V_c)
        inc(prog, "gpsimd", sg, lambda e: e.collective_compute(
            "AllGather", ALU.bypass, replica_groups=RG,
            ins=[ocin.opt()], outs=[ocout.opt()]))
        CC2 = sg.n

        with ExitStack() as cex:
            sbe = lambda n, s, d: cex.enter_context(nc.sbuf_tensor(n, s, d))
            wbuf = [sbe(f"F_wbE{i}", [128, 4 * 1024], BF16) for i in range(2)]
            xr = sbe("F_xrE", [128, NCH * TT], F32)
            ob = sbe("F_ob", [128, NCH * TT], BF16)
            ob0 = sbe("F_ob0", [128, NCH * TT], BF16)
            ob1 = sbe("F_ob1", [128, NCH * TT], BF16)
            msel = sbe("F_msel_s", [128, 2], F32)
            x3r = sbe("F_x3r", [128, NCH * TT], F32)
            xln = sbe("F_xlnE", [128, NCH * TT], BF16)
            hbuf = sbe("F_h", [128, NCH * TT], BF16)
            ssc = sbe("F_sscE", [128, 4 * TT], F32)
            xxsc = sbe("F_xxscE", [128, 2 * TT], BF16)
            t1sc = sbe("F_t1scE", [128, TT], F32)
            wr16 = sbe("F_wr16", [128, NCH * 8], BF16)
            lgt = sbe("F_lgt", [128, 14 * 8], F32)
            wgt8 = sbe("F_wgt8", [8, 4 * 128], F32)
            wbc16 = sbe("F_wbc", [128, TT], BF16)
            idf = sbe("F_idf", [128, 128], F32)
            oh8 = sbe("F_oh8_s", [8, 8 * 128], F32)
            rows = sbe("F_rowsE", [1, 5 * 512], F32)
            ps = cex.enter_context(nc.psum_tensor("F_psE", [128, 4096], F32))

            for eng in ("sync", "gpsimd", "vector", "scalar", "tensor"):
                wait(prog, eng, so, V_c)
            for (dst, src2) in ((idf, identf_d), (oh8, oh8_d), (msel, msel_d)):
                inc(prog, "sync", sk, lambda e, d=dst, s=src2: e.dma_start(out=d[:], in_=s[:]), 16)
            ldE = sk.n
            inc(prog, "gpsimd", sw, lambda e:
                e.dma_start(out=wr16.ap().rearrange("p (k e) -> p k e", e=8),
                            in_=wr_d.rearrange("(k p) e -> p k e", p=128)), 16)
            ldEw = sw.n
            wait(prog, "tensor", sk, ldE)
            wait(prog, "vector", sk, ldE)
            wait(prog, "scalar", sk, ldE)
            wait(prog, "tensor", sw, ldEw)
            wait(prog, "sync", sg, CC2)
            wait(prog, "gpsimd", sems["d"], SD_ALL)

            v_tile_free = None
            for i in range(NTI):
                c0_, c1_ = i * TT, (i + 1) * TT
                if v_tile_free is not None:
                    wait(prog, "sync", sv, v_tile_free)
                srcx = x2sc.rearrange("(j p) t -> p j t", p=128)[:, :, c0_:c1_]
                x_ld = inc(prog, "sync", sx, lambda e, s=srcx:
                           e.dma_start(out=xr.ap().rearrange("p (j t) -> p j t", j=NCH), in_=s), 16)
                srco = ocout.rearrange("(j p) t -> p j t", p=128)
                inc(prog, "sync", sx, lambda e, s=srco, c0_=c0_, c1_=c1_:
                    e.dma_start(out=ob0.ap().rearrange("p (j t) -> p j t", j=NCH),
                                in_=s[:, :, c0_:c1_]), 16)
                o_ld = inc(prog, "sync", sx, lambda e, s=srco, c0_=c0_, c1_=c1_:
                           e.dma_start(out=ob1.ap().rearrange("p (j t) -> p j t", j=NCH),
                                       in_=s[:, :, SH + c0_:SH + c1_]), 16)
                wait(prog, "vector", sx, o_ld)
                if i > 0:
                    wait(prog, "vector", so, so.n)
                inc(prog, "vector", sv, lambda e: e.tensor_scalar(ob[:], ob0[:], msel[:, 0:1], None, ALU.mult))
                inc(prog, "vector", sv, lambda e: e.scalar_tensor_tensor(ob0[:], ob1[:], msel[:, 1:2], ob[:], ALU.mult, ALU.add))
                v_ev = [None]

                def evac_x3(j_abs, psl):
                    v_ev[0] = inc(prog, "vector", sv,
                                  lambda e, o=x3r[:, j_abs * TT:(j_abs + 1) * TT], p=psl,
                                  x=xr[:, j_abs * TT:(j_abs + 1) * TT]:
                                  e.tensor_tensor(o, x, p, ALU.add))
                wait(prog, "tensor", sx, o_ld)
                mm_streamed_emit(prog, sems, wproj16, D, D, ob0, ps, wbuf, evac_x3,
                                 pre_mm_wait=(sv, sv.n))
                v_ln4 = ln_tile_emit(prog, sems, x3r, xln, xln, xxsc, t1sc, ps, ones_b, onesf, rows)
                wait(prog, "tensor", sv, v_ln4)
                c_w = None
                t_tr = None
                for tsub in range(4):
                    wait(prog, "tensor", sv, sv.n)
                    wait(prog, "tensor", sems["c"], sems["c"].n)
                    if t_tr is not None:
                        wait(prog, "vector", st, t_tr)
                    lgp = ps[0:128, 2048 + tsub * 8: 2056 + tsub * 8]
                    t_lg = None
                    for kc in range(NCH):
                        t_lg = inc(prog, "tensor", st, lambda e, kc=kc, tsub=tsub, lgp=lgp:
                                   e.matmul(lgp, xln[:, kc * TT + tsub * 128: kc * TT + (tsub + 1) * 128],
                                            wr16[:, kc * 8:(kc + 1) * 8],
                                            start=(kc == 0), stop=(kc == NCH - 1)))
                    wait(prog, "vector", st, t_lg)
                    L = lambda k: lgt[:, k * 8:(k + 1) * 8]
                    C1 = lambda k: lgt[:, 96 + k:96 + k + 1]
                    inc(prog, "vector", sv, lambda e, lgp=lgp: e.tensor_copy(L(0), lgp))
                    inc(prog, "vector", sv, lambda e: e.reduce_max(C1(0), L(0), AX.X))
                    v_sm = inc(prog, "vector", sv, lambda e: e.tensor_scalar(L(1), L(0), C1(0), None, ALU.subtract))
                    wait(prog, "scalar", sv, v_sm)
                    c_ex = inc(prog, "scalar", sc, lambda e: e.activation(L(2), L(1), AF.Exp))
                    wait(prog, "vector", sc, c_ex)
                    inc(prog, "vector", sv, lambda e: e.reduce_sum(C1(1), L(2), AX.X))
                    inc(prog, "vector", sv, lambda e: e.reciprocal(C1(2), C1(1)))
                    inc(prog, "vector", sv, lambda e: e.tensor_scalar(L(3), L(2), C1(2), None, ALU.mult))
                    inc(prog, "vector", sv, lambda e: e.reduce_max(C1(3), L(3), AX.X))
                    inc(prog, "vector", sv, lambda e: e.tensor_scalar(L(4), L(3), C1(3), None, ALU.is_ge))
                    inc(prog, "vector", sv, lambda e: e.tensor_scalar(L(5), L(4), -1.0, 1.0, ALU.mult, ALU.add))
                    inc(prog, "vector", sv, lambda e: e.tensor_tensor(L(6), L(3), L(5), ALU.mult))
                    inc(prog, "vector", sv, lambda e: e.reduce_max(C1(4), L(6), AX.X))
                    inc(prog, "vector", sv, lambda e: e.tensor_scalar(L(7), L(3), C1(4), None, ALU.is_ge))
                    inc(prog, "vector", sv, lambda e: e.tensor_tensor(L(8), L(3), L(7), ALU.mult))
                    inc(prog, "vector", sv, lambda e: e.tensor_tensor(C1(5), C1(3), C1(4), ALU.add))
                    inc(prog, "vector", sv, lambda e: e.reciprocal(C1(6), C1(5)))
                    v_wn = inc(prog, "vector", sv, lambda e: e.tensor_scalar(L(9), L(8), C1(6), None, ALU.mult))
                    wait(prog, "tensor", sv, v_wn)
                    wnp = ps[:, 2176:2304]
                    t_tr = inc(prog, "tensor", st, lambda e: e.transpose(wnp[0:8, 0:128], L(9), idf[:]))
                    wait(prog, "scalar", st, t_tr)
                    c_w = inc(prog, "scalar", sc, lambda e, tsub=tsub:
                              e.activation(wgt8[0:8, tsub * 128:(tsub + 1) * 128], wnp[0:8, 0:128], AF.Copy))
                wait(prog, "vector", sems["c"], sems["c"].n)

                for ex in range(8):
                    wait(prog, "tensor", sems["c"], sems["c"].n)
                    wait(prog, "tensor", sv, sv.n)
                    wbp = ps[:, 2560:2560 + TT]
                    t_wb = inc(prog, "tensor", st, lambda e, ex=ex:
                               e.matmul(wbp, oh8[0:8, ex * 128:(ex + 1) * 128], wgt8[0:8, :], start=True, stop=True))
                    wait(prog, "scalar", st, t_wb)
                    c_wb = inc(prog, "scalar", sc, lambda e: e.activation(wbc16[:], wbp, AF.Copy))
                    wait(prog, "vector", sems["c"], c_wb)
                    pre = (sv, sv.n)
                    ev2 = [pre, pre]
                    for gi in range(8):
                        par = gi % 2
                        b0 = par * 4
                        t1 = mm_group_emit(prog, sems, wg16[ex], D, gi * 256, 2, xln, ps, b0, wbuf,
                                           pre_mm_wait=ev2[par])
                        t2 = mm_group_emit(prog, sems, wu16[ex], D, gi * 256, 2, xln, ps, b0 + 2, wbuf,
                                           pre_mm_wait=(sems["c"], c_wb) if gi <= 1 else None)
                        wait(prog, "scalar", st, t1)
                        wait(prog, "scalar", sv, sv.n)
                        c_sig = None
                        for j in range(2):
                            c_sig = inc(prog, "scalar", sems["c"],
                                        lambda e, o=ssc[:, (par * 2 + j) * TT:(par * 2 + j + 1) * TT],
                                        p=ps[:, (b0 + j) * 512:(b0 + j) * 512 + TT]:
                                        e.activation(o, p, AF.Silu))
                        wait(prog, "vector", st, t2)
                        wait(prog, "vector", sems["c"], c_sig)
                        v_h = None
                        for j in range(2):
                            hh = gi * 2 + j
                            inc(prog, "vector", sv,
                                lambda e, o=t1sc[:], a=ssc[:, (par * 2 + j) * TT:(par * 2 + j + 1) * TT],
                                p=ps[:, (b0 + 2 + j) * 512:(b0 + 2 + j) * 512 + TT]:
                                e.tensor_tensor(o, a, p, ALU.mult))
                            v_h = inc(prog, "vector", sv,
                                      lambda e, o=hbuf[:, hh * TT:(hh + 1) * TT], a=t1sc[:]:
                                      e.tensor_tensor(o, a, wbc16[:], ALU.mult))
                        ev2[par] = (sv, v_h)
                    pre = (sv, v_h)
                    src_y = x3r if ex == 0 else (xr if ex % 2 == 1 else x3r)
                    dst_y = xr if ex % 2 == 0 else x3r
                    v_ev2 = [None]

                    def evac_y(j_abs, psl, src_y=src_y, dst_y=dst_y):
                        v_ev2[0] = inc(prog, "vector", sv,
                                       lambda e, o=dst_y[:, j_abs * TT:(j_abs + 1) * TT], p=psl,
                                       x=src_y[:, j_abs * TT:(j_abs + 1) * TT]:
                                       e.tensor_tensor(o, x, p, ALU.add))
                    mm_streamed_emit(prog, sems, wd16[ex], D, D, hbuf, ps, wbuf, evac_y,
                                     pre_mm_wait=pre)
                v_tile_free = v_ev2[0]
                wait(prog, "sync", sv, v_tile_free)
                dst = xout.rearrange("(j p) t -> p j t", p=128)[:, :, c0_:c1_]
                inc(prog, "sync", so, lambda e, d=dst:
                    e.dma_start(out=d, in_=x3r.ap().rearrange("p (j t) -> p j t", j=NCH)), 16)
            wait(prog, "sync", so, so.n)
            if debug:
                inc(prog, "sync", so, lambda e: e.dma_start(out=dbg_ci[:, :], in_=ccin[:, :]), 16)
                inc(prog, "sync", so, lambda e: e.dma_start(out=dbg_x2[:, :], in_=x2sc[:, :]), 16)
                inc(prog, "sync", so, lambda e: e.dma_start(out=dbg_cc[:, :], in_=ccout[:, :]), 16)
                inc(prog, "sync", so, lambda e: e.dma_start(out=dbg_oc[:, :], in_=ocout[:, :]), 16)
                wait(prog, "sync", so, so.n)
        prog.run(block)
    return nc


def host_prep_fused(inputs):
    x = inputs["L0_attn_x"]
    winT_h = np.ascontiguousarray(inputs["L0_attn_W_in"].T.astype(f32))
    woutT_h = np.ascontiguousarray(inputs["L0_attn_W_out"].T.astype(f32))
    dcol_h = np.ascontiguousarray(inputs["L0_attn_decay"].reshape(NCH, 128).T.astype(f32))
    bcol_h = np.ascontiguousarray(inputs["L0_attn_bias"].reshape(NCH, 128).T.astype(f32))
    w1Th = np.ascontiguousarray(inputs["L0_ffn_w1"].T.astype(f32))
    w2Th = np.ascontiguousarray(inputs["L0_ffn_w2"].T.astype(f32))
    w3Th = np.ascontiguousarray(inputs["L0_ffn_w3"].T.astype(f32))
    w_qkv = inputs["L1_attn_w_qkv"].astype(f32)
    theta = inputs["L1_attn_theta"].astype(f32).reshape(NH, EH)
    wprojT_h = np.ascontiguousarray(inputs["L1_attn_w_proj"].T.astype(f32))
    wr_h = np.ascontiguousarray(inputs["L1_ffn_Wr"].astype(f32))
    wg_h = np.ascontiguousarray(inputs["L1_ffn_Wg"].astype(f32))
    wu_h = np.ascontiguousarray(inputs["L1_ffn_Wu"].astype(f32))
    wd_h = np.ascontiguousarray(inputs["L1_ffn_Wd"].astype(f32))
    slopes = _slopes16()
    idx = np.arange(CHUNK, dtype=np.float64)
    diff = idx[:, None] - idx[None, :]
    oh8_h = np.zeros((8, 8 * 128), f32)
    for ex in range(8):
        oh8_h[ex, ex * 128:(ex + 1) * 128] = 1.0
    onesf_h = np.ones((1, 128), f32)
    onesb_h = np.ones((128, 1), ml_dtypes.bfloat16)
    onesc_h = np.ones((128, 1), f32)
    identb_h = np.eye(128, dtype=ml_dtypes.bfloat16)
    identf_h = np.eye(128, dtype=f32)
    pos_h = np.arange(S, dtype=f32).reshape(1, S)
    zf32_h = np.zeros((128, 2048), f32)
    in_maps = []
    for core in range(8):
        b, g = core // 2, core % 2
        t0 = g * SH
        xs = np.empty((D, WARM + SH), f32)
        if g == 0:
            xs[:, :WARM] = 0.0
        else:
            xs[:, :WARM] = x[b, t0 - WARM:t0, :].T
        xs[:, WARM:] = x[b, t0:t0 + SH, :].T
        wgq = np.concatenate([w_qkv[g * 1024:(g + 1) * 1024],
                              w_qkv[2048 + g * 1024:2048 + (g + 1) * 1024],
                              w_qkv[4096 + g * 1024:4096 + (g + 1) * 1024]], axis=0)
        wqkvT_h = np.ascontiguousarray(wgq.T)
        theta_h = np.ascontiguousarray(theta[g * 8:(g + 1) * 8].reshape(1, 1024))
        maskT_h = np.empty((128, 1024), f32)
        qdbc_h = np.empty((128, 1024), f32)
        kdbc_h = np.empty((128, 1024), f32)
        for i in range(8):
            s = float(slopes[g * 8 + i])
            m = np.where(diff >= 0, np.exp(np.float32(-s) * diff.astype(f32)), 0.0).astype(f32)
            maskT_h[:, i * 128:(i + 1) * 128] = m.T
            qdbc_h[:, i * 128:(i + 1) * 128] = np.exp(np.float32(-s) * (idx + 1.0).astype(f32))[None, :]
            kdbc_h[:, i * 128:(i + 1) * 128] = np.exp(np.float32(-s) * (CHUNK - 1.0 - idx).astype(f32))[None, :]
        bdcol_h = np.broadcast_to(
            np.exp(-slopes[g * 8:(g + 1) * 8].astype(f32) * np.float32(CHUNK))[None, :],
            (128, 8)).astype(f32)
        msel_h = np.zeros((128, 2), f32)
        msel_h[:, g] = 1.0
        in_maps.append({
            "F_msel": msel_h,
            "F_x": np.ascontiguousarray(xs), "F_winT": winT_h, "F_woutT": woutT_h,
            "F_dcol": dcol_h, "F_bcol": bcol_h, "F_w1T": w1Th, "F_w2T": w2Th,
            "F_w3T": w3Th, "F_wqkvT": wqkvT_h, "F_theta": theta_h, "F_pos": pos_h,
            "F_maskT": maskT_h, "F_qdbc": qdbc_h, "F_kdbc": kdbc_h,
            "F_identb": identb_h, "F_identf": identf_h,
            "F_bdcol": np.ascontiguousarray(bdcol_h), "F_zf32": zf32_h,
            "F_wprojT": wprojT_h, "F_wr": wr_h, "F_wg": wg_h, "F_wu": wu_h,
            "F_wd": wd_h, "F_oh8": oh8_h, "F_onesf": onesf_h, "F_onesb": onesb_h,
            "F_onesc": onesc_h})
    return in_maps


def kernel(**inputs):
    nc = build_fused()
    in_maps = host_prep_fused(inputs)
    res = run_bass_kernel_spmd(nc, in_maps, core_ids=list(range(8)))
    full = np.empty((B, S, D), f32)
    for c in range(8):
        b, h = c // 2, c % 2
        full[b, h * SH:(h + 1) * SH, :] = res.results[c]["F_out"].T
    return full


def kernel_unfused(**inputs):
    x1 = run_launchA(inputs)
    x2 = run_launchB(inputs, x1)
    oc = run_launchC(inputs, x2)
    out = run_launchD(inputs, x2, oc)
    full = np.empty((B, S, D), f32)
    for c in range(8):
        b, h = c // 2, c % 2
        full[b, h * SH:(h + 1) * SH, :] = out[c].T
    return full


# version 08
# speedup vs baseline: 1.1935x; 1.0305x over previous
import numpy as np
import ml_dtypes
import concourse.bass as bass
import concourse.mybir as mybir
from concourse.bass_utils import run_bass_kernel_spmd
from contextlib import ExitStack

F32 = mybir.dt.float32
BF16 = mybir.dt.bfloat16
AF = mybir.ActivationFunctionType
ALU = mybir.AluOpType
AX = mybir.AxisListType

B, S, D = 4, 8192, 2048
NCH = D // 128
H_FFN = 5632
NH = 16
EH = 128
CHUNK = 128
TT = 512
SH = S // 2
NTI = SH // TT
WARM = TT

f32 = np.float32


class Prog:
    ENGS = ("sync", "gpsimd", "vector", "scalar", "tensor")

    def __init__(self):
        self.ops = {e: [] for e in self.ENGS}
        self.last = {}

    def emit(self, eng, fn):
        self.ops[eng].append(fn)

    def run(self, block):
        ops = self.ops

        @block.sync
        def _(e):
            for f in ops["sync"]:
                f(e)

        @block.gpsimd
        def _(e):
            for f in ops["gpsimd"]:
                f(e)

        @block.vector
        def _(e):
            for f in ops["vector"]:
                f(e)

        @block.scalar
        def _(e):
            for f in ops["scalar"]:
                f(e)

        @block.tensor
        def _(e):
            for f in ops["tensor"]:
                f(e)


class Sem:
    def __init__(self, ctx, nc, name):
        self.h = ctx.enter_context(nc.semaphore(name))
        self.n = 0

    def bump(self, k=1):
        self.n += k
        return self.n


AUTOCHAIN = ("vector", "scalar")


def inc(prog, eng, sem, build, k=1):
    if eng in AUTOCHAIN:
        last = prog.last.get(eng)
        if last is not None:
            wait(prog, eng, last[0], last[1])
    v = sem.bump(k)
    h = sem.h

    def f(e, build=build, h=h, k=k):
        build(e).then_inc(h, k)
    prog.emit(eng, f)
    if eng in AUTOCHAIN:
        prog.last[eng] = (sem, v)
    return v


def wait(prog, eng, sem, v):
    h = sem.h
    prog.emit(eng, lambda e, h=h, v=v: e.wait_ge(h, v))


def run_op(prog, eng, build, prog_sems=None):
    assert eng not in AUTOCHAIN, "vector/scalar ops must go through inc()"
    prog.emit(eng, build)


def ln_tile_emit(prog, sems, xr, xb16, xln, xxsc, t1sc, ps, ones_b, onesf, rows, ncols=TT,
                 off=(0, 512, 1024, 1536)):
    sv, st, sc = sems["v"], sems["t"], sems["c"]
    s1 = ps[0:1, off[0]:off[0] + ncols]
    s2 = ps[0:1, off[1]:off[1] + ncols]
    bcm = ps[:, off[2]:off[2] + ncols]
    bcr = ps[:, off[3]:off[3] + ncols]
    mrow = rows[0:1, 0:ncols]
    msq = rows[0:1, ncols:2 * ncols]
    var = rows[0:1, 2 * ncols:3 * ncols]
    srow = rows[0:1, 3 * ncols:4 * ncols]
    rrow = rows[0:1, 4 * ncols:5 * ncols]
    xx_t = {}

    v_cast = inc(prog, "vector", sv, lambda e: e.tensor_copy(xb16[:, 0:NCH * ncols], xr[:, 0:NCH * ncols]))
    wait(prog, "tensor", sv, v_cast)
    for j in range(NCH):
        run_op(prog, "tensor", lambda e, a=s1, b=ones_b, c=xb16[:, j * ncols:(j + 1) * ncols], j=j:
               e.matmul(a, b[:, 0:1], c, start=(j == 0), stop=(j == NCH - 1)))
    for j in range(NCH):
        xsl = xb16[:, j * ncols:(j + 1) * ncols]
        xxj = xxsc[:, (j % 2) * ncols:((j % 2) + 1) * ncols]
        if j >= 2:
            wait(prog, "vector", st, xx_t[j - 2])
        v_sq = inc(prog, "vector", sv, lambda e, a=xxj, b=xsl: e.tensor_mul(a, b, b))
        wait(prog, "tensor", sv, v_sq)
        xx_t[j] = inc(prog, "tensor", st, lambda e, a=s2, b=ones_b, c=xxj, j=j:
                      e.matmul(a, b[:, 0:1], c, start=(j == 0), stop=(j == NCH - 1)))
    wait(prog, "vector", st, xx_t[NCH - 1])
    inc(prog, "vector", sv, lambda e: e.tensor_scalar(mrow, s1, 1.0 / D, None, ALU.mult))
    inc(prog, "vector", sv, lambda e: e.tensor_mul(msq, mrow, mrow))
    v_var = inc(prog, "vector", sv, lambda e: e.scalar_tensor_tensor(var, s2, 1.0 / D, msq, ALU.mult, ALU.subtract))
    wait(prog, "scalar", sv, v_var)
    c_sqrt = inc(prog, "scalar", sc, lambda e: e.activation(srow, var, AF.Sqrt, bias=np.float32(1e-5), scale=np.float32(1.0)))
    wait(prog, "vector", sc, c_sqrt)
    v_rec = inc(prog, "vector", sv, lambda e: e.reciprocal(rrow, srow))
    wait(prog, "tensor", sv, v_rec)
    run_op(prog, "tensor", lambda e: e.matmul(bcm, onesf[0:1, :], mrow, start=True, stop=True))
    t_bc = inc(prog, "tensor", st, lambda e: e.matmul(bcr, onesf[0:1, :], rrow, start=True, stop=True))
    wait(prog, "vector", st, t_bc)
    v_last = None
    for j in range(NCH):
        inc(prog, "vector", sv, lambda e, a=t1sc[:, 0:ncols], x=xr[:, j * ncols:(j + 1) * ncols], m=bcm:
            e.tensor_tensor(a, x, m, ALU.subtract))
        v_last = inc(prog, "vector", sv, lambda e, o=xln[:, j * ncols:(j + 1) * ncols], a=t1sc[:, 0:ncols], r=bcr:
                     e.tensor_tensor(o, a, r, ALU.mult))
    return v_last


def mm_group_emit(prog, sems, wdram, kdim, mcol0, mg, rhs16, ps, psbank0, wbuf, evac=None,
                  pre_mm_wait=None, ncols=TT):
    sv, st = sems["v"], sems["t"]
    nk = kdim // 128
    assert nk % 4 == 0 and 1 <= mg <= 8
    mcw = mg * 128
    KG = 8 if (nk % 8 == 0 and 8 * mcw <= 4096) else 4
    t_last = None
    for kg in range(nk // KG):
        buf = wbuf[kg % 2]
        swp = sems["wp"][kg % 2]
        if sems["_wmm"][kg % 2] is not None:
            wait(prog, "gpsimd", st, sems["_wmm"][kg % 2])
        src = wdram.rearrange("(a p) m -> p a m", p=128)[:, KG * kg:KG * (kg + 1),
                                                        mcol0:mcol0 + mcw]
        w_ld = inc(prog, "gpsimd", swp, lambda e, b=buf[:, 0:KG * mcw], s=src, KG=KG:
                   e.dma_start(out=b.rearrange("p (a m) -> p a m", a=KG), in_=s), 16)
        wait(prog, "tensor", swp, w_ld)
        if pre_mm_wait is not None:
            wait(prog, "tensor", pre_mm_wait[0], pre_mm_wait[1])
            pre_mm_wait = None
        for kl in range(KG):
            ka = kg * KG + kl
            for j in range(mg):
                bld = (lambda e, p=ps[:, (psbank0 + j) * 512:(psbank0 + j) * 512 + ncols],
                       w=buf[:, kl * mcw + j * 128: kl * mcw + (j + 1) * 128],
                       r=rhs16[:, ka * ncols:(ka + 1) * ncols],
                       s=(ka == 0), z=(ka == nk - 1):
                       e.matmul(p, w, r, start=s, stop=z))
                if kl == KG - 1 and j == mg - 1:
                    t_last = inc(prog, "tensor", st, bld)
                    sems["_wmm"][kg % 2] = t_last
                else:
                    run_op(prog, "tensor", bld)
    if evac is not None:
        wait(prog, "vector", st, t_last)
        for j in range(mg):
            evac(j, ps[:, (psbank0 + j) * 512:(psbank0 + j) * 512 + ncols])
    return t_last


def mm_streamed_emit(prog, sems, wdram, kdim, mdim, rhs16, ps, wbuf, evac,
                     pre_mm_wait=None, ncols=TT):
    sv = sems["v"]
    nm = mdim // 128
    assert nm % 4 == 0
    ev_done = [None, None]
    for mq in range(nm // 4):
        par = mq % 2
        def evac4(j, psl, mq=mq):
            evac(mq * 4 + j, psl)
        pw = pre_mm_wait if mq <= 1 else ev_done[par]
        mm_group_emit(prog, sems, wdram, kdim, 512 * mq, 4, rhs16, ps, par * 4, wbuf,
                      evac=evac4, pre_mm_wait=pw, ncols=ncols)
        ev_done[par] = (sv, sv.n)


def build_launchA():
    nc = bass.Bass("TRN2", target_bir_lowering=False)
    NT = NTI + 1
    xin = nc.dram_tensor("A_x", [D, WARM + SH], F32, kind="ExternalInput").ap()
    winT = nc.dram_tensor("A_winT", [D, D], F32, kind="ExternalInput").ap()
    woutT = nc.dram_tensor("A_woutT", [D, D], F32, kind="ExternalInput").ap()
    dcol_d = nc.dram_tensor("A_dcol", [128, NCH], F32, kind="ExternalInput").ap()
    bcol_d = nc.dram_tensor("A_bcol", [128, NCH], F32, kind="ExternalInput").ap()
    onesf_d = nc.dram_tensor("A_onesf", [1, 128], F32, kind="ExternalInput").ap()
    onesb_d = nc.dram_tensor("A_onesb", [128, 1], BF16, kind="ExternalInput").ap()
    xout = nc.dram_tensor("A_out", [D, SH], F32, kind="ExternalOutput").ap()

    with ExitStack() as ctx:
        sb = lambda n, s, d: ctx.enter_context(nc.sbuf_tensor(n, s, d))
        xr = sb("A_xr", [128, NCH * TT], F32)
        xb16 = sb("A_xb16", [128, NCH * TT], BF16)
        xln = sb("A_xln", [128, NCH * TT], BF16)
        q = sb("A_q", [128, NCH * TT], F32)
        stk = sb("A_stk", [128, NCH * TT], F32)
        stb = sb("A_stb", [128, NCH * TT], BF16)
        xxsc = sb("A_xxsc", [128, 2 * TT], BF16)
        t1sc = sb("A_t1sc", [128, TT], F32)
        wbuf = [sb(f"A_wb{i}", [128, 4 * 1024], BF16) for i in range(2)]
        dcol = sb("A_dc", [128, NCH], F32)
        bcol = sb("A_bc", [128, NCH], F32)
        dsig = sb("A_ds", [128, NCH], F32)
        omd = sb("A_om", [128, NCH], F32)
        stcol = sb("A_st", [128, NCH], F32)
        ones_b = sb("A_o1", [128, 1], BF16)
        onesf = sb("A_o2", [1, 128], F32)
        rows = sb("A_rows", [1, 5 * 512], F32)
        ps = ctx.enter_context(nc.psum_tensor("A_ps", [128, 4096], F32))

        sems = {k: Sem(ctx, nc, "A_s" + k) for k in ("x", "w", "v", "t", "c", "o", "k")}
        sems["wp"] = [Sem(ctx, nc, "A_swp0"), Sem(ctx, nc, "A_swp1")]
        sems["_wmm"] = [None, None]
        block = ctx.enter_context(nc.Block())
        prog = Prog()
        sx, sw, sv, st, sc, so = (sems[k] for k in ("x", "w", "v", "t", "c", "o"))

        sk = sems["k"]
        inc(prog, "sync", sk, lambda e: e.dma_start(out=ones_b[:], in_=onesb_d[:]), 16)
        inc(prog, "sync", sk, lambda e: e.dma_start(out=onesf[:], in_=onesf_d[:]), 16)
        inc(prog, "sync", sk, lambda e: e.dma_start(out=dcol[:], in_=dcol_d[:]), 16)
        x0 = inc(prog, "sync", sk, lambda e: e.dma_start(out=bcol[:], in_=bcol_d[:]), 16)
        wait(prog, "scalar", sk, x0)
        wait(prog, "tensor", sk, x0)
        wait(prog, "vector", sk, x0)
        c0 = inc(prog, "scalar", sc, lambda e: e.activation(dsig[:], dcol[:], AF.Sigmoid))
        wait(prog, "vector", sc, c0)
        inc(prog, "vector", sv, lambda e: e.tensor_scalar(omd[:], dsig[:], -1.0, 1.0, ALU.mult, ALU.add))

        v_xr_free = None
        for i in range(NT):
            warm = (i == 0)
            c0_ = i * TT if warm else WARM + (i - 1) * TT
            c1_ = c0_ + TT
            if v_xr_free is not None:
                wait(prog, "sync", sv, v_xr_free)
            src = xin.rearrange("(j p) t -> p j t", p=128)[:, :, c0_:c1_]
            x_ld = inc(prog, "sync", sx, lambda e, s=src:
                       e.dma_start(out=xr.ap().rearrange("p (j t) -> p j t", j=NCH), in_=s), 16)
            wait(prog, "vector", sx, x_ld)
            if not warm:
                wait(prog, "vector", so, so.n)
            v_ln = ln_tile_emit(prog, sems, xr, xb16, xln, xxsc, t1sc, ps, ones_b, onesf, rows)

            v_ev = [None]

            def evac_q(j_abs, psl):
                v_ev[0] = inc(prog, "vector", sv,
                              lambda e, o=q[:, j_abs * TT:(j_abs + 1) * TT], p=psl,
                              s=omd[:, j_abs:j_abs + 1]:
                              e.tensor_scalar(o, p, s, None, ALU.mult))
            mm_streamed_emit(prog, sems, winT, D, D, xln, ps, wbuf, evac_q,
                             pre_mm_wait=(sv, v_ln))
            for j in range(NCH):
                init = 0.0 if warm else stcol[:, j:j + 1]
                dbc = dsig[:, j:j + 1].broadcast_to([128, TT])
                inc(prog, "vector", sv,
                    lambda e, o=stk[:, j * TT:(j + 1) * TT], d=dbc,
                    qq=q[:, j * TT:(j + 1) * TT], ii=init:
                    e.tensor_tensor_scan(o, d, qq, ii, ALU.mult, ALU.add))
            v_scan = inc(prog, "vector", sv, lambda e: e.tensor_copy(stcol[:], stk[:, TT - 1::TT]))
            if warm:
                v_xr_free = v_scan
                continue
            v_stb = inc(prog, "vector", sv, lambda e: e.tensor_copy(stb[:], stk[:]))

            def evac_o(j_abs, psl):
                v_ev[0] = inc(prog, "vector", sv,
                              lambda e, o=q[:, j_abs * TT:(j_abs + 1) * TT], p=psl,
                              s=bcol[:, j_abs:j_abs + 1], x=xr[:, j_abs * TT:(j_abs + 1) * TT]:
                              e.scalar_tensor_tensor(o, p, s, x, ALU.add, ALU.add))
            mm_streamed_emit(prog, sems, woutT, D, D, stb, ps, wbuf, evac_o,
                             pre_mm_wait=(sv, v_stb))
            v_xr_free = v_ev[0]
            wait(prog, "sync", sv, v_xr_free)
            dst = xout.rearrange("(j p) t -> p j t", p=128)[:, :, c0_ - WARM:c1_ - WARM]
            inc(prog, "sync", so, lambda e, d=dst:
                e.dma_start(out=d, in_=q.ap().rearrange("p (j t) -> p j t", j=NCH)), 16)
        wait(prog, "sync", so, so.n)
        prog.run(block)
    return nc


def host_prep_A(inputs):
    x = inputs["L0_attn_x"]
    winT = np.ascontiguousarray(inputs["L0_attn_W_in"].T.astype(f32))
    woutT = np.ascontiguousarray(inputs["L0_attn_W_out"].T.astype(f32))
    dcol = np.ascontiguousarray(inputs["L0_attn_decay"].reshape(NCH, 128).T.astype(f32))
    bcol = np.ascontiguousarray(inputs["L0_attn_bias"].reshape(NCH, 128).T.astype(f32))
    onesf_h = np.ones((1, 128), f32)
    onesb_h = np.ones((128, 1), ml_dtypes.bfloat16)
    in_maps = []
    for c in range(8):
        b, h = c // 2, c % 2
        t0 = h * SH
        xs = np.empty((D, WARM + SH), f32)
        if h == 0:
            xs[:, :WARM] = 0.0
        else:
            xs[:, :WARM] = x[b, t0 - WARM:t0, :].T
        xs[:, WARM:] = x[b, t0:t0 + SH, :].T
        in_maps.append({"A_x": np.ascontiguousarray(xs), "A_winT": winT,
                        "A_woutT": woutT, "A_dcol": dcol, "A_bcol": bcol,
                        "A_onesf": onesf_h, "A_onesb": onesb_h})
    return in_maps


def run_launchA(inputs):
    nc = build_launchA()
    in_maps = host_prep_A(inputs)
    res = run_bass_kernel_spmd(nc, in_maps, core_ids=list(range(8)))
    return [r["A_out"] for r in res.results]


MAGIC = 12582912.0


def quant_weight_emit(prog, sems, wsrc, wdst, kdim, mdim, qbuf, qb2, wqbuf, redc, acc,
                      gcol, ones_c, onesf, ps, grows, qcols=2816):
    sx, sw, sv, st = sems["q"], sems["w"], sems["v"], sems["t"]
    sld = [sems["qa"], sems["qb"]]
    sst = [sems["wa"], sems["wb"]]
    nrow = kdim // 128
    csplits = [(c, min(qcols, mdim - c)) for c in range(0, mdim, qcols)]
    nit = nrow * len(csplits)
    assert nit <= redc.shape[1]
    iters = [(r, c0, cw) for r in range(nrow) for (c0, cw) in csplits]
    v0 = sv.n if sv.n > 0 else None
    v_done = [v0, v0]
    for it, (r, c0, cw) in enumerate(iters):
        par = it % 2
        if v_done[par] is not None:
            wait(prog, "sync", sv, v_done[par])
        x_ld = inc(prog, "sync", sld[par], lambda e, s=wsrc[r * 128:(r + 1) * 128, c0:c0 + cw],
                   cw=cw, par=par: e.dma_start(out=qbuf[par][:, 0:cw], in_=s), 16)
        wait(prog, "vector", sld[par], x_ld)
        v_done[par] = inc(prog, "vector", sv,
                          lambda e, cw=cw, it=it, par=par:
                          e.reduce_sum(redc[:, it:it + 1], qbuf[par][:, 0:cw], AX.X,
                                       apply_absolute_value=True))
    v_g = inc(prog, "vector", sv, lambda e: e.reduce_sum(acc[:], redc[:, 0:nit], AX.X))
    wait(prog, "tensor", sv, v_g)
    psg = ps[0:1, 0:1]
    psb = ps[:, 512:513]
    t_g = inc(prog, "tensor", st, lambda e: e.matmul(psg, ones_c[:, 0:1], acc[:], start=True, stop=True))
    wait(prog, "vector", st, t_g)
    v_pre = inc(prog, "vector", sv, lambda e: e.tensor_scalar(grows[0:1, 0:1], psg, 1.0 / (kdim * mdim), 1e-8,
                                                              ALU.mult, ALU.add))
    wait(prog, "vector", sv, v_pre)
    v_gr = inc(prog, "vector", sv, lambda e: e.reciprocal(grows[0:1, 1:2], grows[0:1, 0:1]))
    wait(prog, "tensor", sv, v_gr)
    t_b = inc(prog, "tensor", st, lambda e: e.matmul(psb, onesf[0:1, :], grows[0:1, 1:2], start=True, stop=True))
    wait(prog, "vector", st, t_b)
    v_gc = inc(prog, "vector", sv, lambda e: e.tensor_copy(gcol[:], psb))
    v_q = [v_done[0], v_done[1]]
    st_c = [sst[0].n if sst[0].n > 0 else None, sst[1].n if sst[1].n > 0 else None]
    first = True
    for it, (r, c0, cw) in enumerate(iters):
        par = it % 2
        if v_q[par] is not None:
            wait(prog, "sync", sv, v_q[par])
        x_ld = inc(prog, "sync", sld[par], lambda e, s=wsrc[r * 128:(r + 1) * 128, c0:c0 + cw],
                   cw=cw, par=par: e.dma_start(out=qbuf[par][:, 0:cw], in_=s), 16)
        wait(prog, "vector", sld[par], x_ld)
        if first:
            wait(prog, "vector", sv, v_gc)
            first = False
        if st_c[par] is not None:
            wait(prog, "vector", sst[par], st_c[par])
        inc(prog, "vector", sv, lambda e, cw=cw, par=par: e.tensor_scalar(qb2[:, 0:cw], qbuf[par][:, 0:cw],
                                                                          gcol[:, 0:1], 1.49, ALU.mult, ALU.min))
        inc(prog, "vector", sv, lambda e, cw=cw, par=par: e.tensor_scalar(qbuf[par][:, 0:cw], qb2[:, 0:cw],
                                                                          -1.49, MAGIC, ALU.max, ALU.add))
        v_q[par] = inc(prog, "vector", sv, lambda e, cw=cw, par=par: e.tensor_scalar(wqbuf[par][:, 0:cw],
                                                                                     qbuf[par][:, 0:cw],
                                                                                     MAGIC, None, ALU.subtract))
        wait(prog, "gpsimd", sv, v_q[par])
        st_c[par] = inc(prog, "gpsimd", sst[par], lambda e, d=wdst[r * 128:(r + 1) * 128, c0:c0 + cw],
                        cw=cw, par=par: e.dma_start(out=d, in_=wqbuf[par][:, 0:cw]), 16)
    return (st_c[0], st_c[1])


def build_launchB(debug=False):
    nc = bass.Bass("TRN2", target_bir_lowering=False)
    x1in = nc.dram_tensor("B_x", [D, SH], F32, kind="ExternalInput").ap()
    w1T = nc.dram_tensor("B_w1T", [D, H_FFN], F32, kind="ExternalInput").ap()
    w2T = nc.dram_tensor("B_w2T", [D, H_FFN], F32, kind="ExternalInput").ap()
    w3T = nc.dram_tensor("B_w3T", [H_FFN, D], F32, kind="ExternalInput").ap()
    if debug:
        dbg_g = nc.dram_tensor("B_dbg_g", [128, 44 * TT], BF16, kind="ExternalOutput").ap()
        dbg_xln = nc.dram_tensor("B_dbg_xln", [128, NCH * TT], BF16, kind="ExternalOutput").ap()
        dbg_w1q = nc.dram_tensor("B_dbg_w1q", [128, H_FFN], BF16, kind="ExternalOutput").ap()
        dbg_x2a = nc.dram_tensor("B_dbg_x2a", [128, TT], F32, kind="ExternalOutput").ap()
        dbg_ssc = nc.dram_tensor("B_dbg_ssc", [128, 4 * TT], F32, kind="ExternalOutput").ap()
        dbg_g0 = nc.dram_tensor("B_dbg_g0", [128, 4 * TT], BF16, kind="ExternalOutput").ap()
    onesf_d = nc.dram_tensor("B_onesf", [1, 128], F32, kind="ExternalInput").ap()
    onesb_d = nc.dram_tensor("B_onesb", [128, 1], BF16, kind="ExternalInput").ap()
    onesc_d = nc.dram_tensor("B_onesc", [128, 1], F32, kind="ExternalInput").ap()
    xout = nc.dram_tensor("B_out", [D, SH], F32, kind="ExternalOutput").ap()
    w1q = nc.dram_tensor("B_w1q", [D, H_FFN], BF16).ap()
    w2q = nc.dram_tensor("B_w2q", [D, H_FFN], BF16).ap()
    w3q = nc.dram_tensor("B_w3q", [H_FFN, D], BF16).ap()
    NHC = H_FFN // 128

    with ExitStack() as ctx:
        sb = lambda n, s, d: ctx.enter_context(nc.sbuf_tensor(n, s, d))
        xr = sb("B_xr", [128, NCH * TT], F32)
        xb16 = sb("B_xb16", [128, NCH * TT], BF16)
        xln = sb("B_xln", [128, NCH * TT], BF16)
        gact = sb("B_g", [128, NHC * TT], BF16)
        xo = sb("B_xo", [128, NCH * TT], F32)
        qbuf = [sb(f"B_qb{i}", [128, 2816], F32) for i in range(2)]
        qb2 = sb("B_qb2", [128, 2816], F32)
        wqbuf = [sb(f"B_wqb{i}", [128, 2816], BF16) for i in range(2)]
        ssc = sb("B_ssc", [128, 4 * TT], F32)
        xxsc = sb("B_xxsc", [128, 2 * TT], BF16)
        t1sc = sb("B_t1sc", [128, TT], F32)
        wbuf = [sb(f"B_wb{i}", [128, 4 * 1024], BF16) for i in range(2)]
        redc = sb("B_redc", [128, 96], F32)
        acc = sb("B_acc", [128, 1], F32)
        gcol = sb("B_gcol", [128, 1], F32)
        ones_c = sb("B_oc", [128, 1], F32)
        ones_b = sb("B_o1", [128, 1], BF16)
        onesf = sb("B_o2", [1, 128], F32)
        grows = sb("B_grows", [1, 4], F32)
        rows = sb("B_rows", [1, 5 * 512], F32)
        ps = ctx.enter_context(nc.psum_tensor("B_ps", [128, 4096], F32))

        sems = {k: Sem(ctx, nc, "B_s" + k) for k in ("x", "w", "v", "t", "c", "o", "k", "q",
                                                      "qa", "qb", "wa", "wb")}
        sems["wp"] = [Sem(ctx, nc, "B_swp0"), Sem(ctx, nc, "B_swp1")]
        sems["_wmm"] = [None, None]
        block = ctx.enter_context(nc.Block())
        prog = Prog()
        sx, sw, sv, st, sc, so = (sems[k] for k in ("x", "w", "v", "t", "c", "o"))

        sk = sems["k"]
        inc(prog, "sync", sk, lambda e: e.dma_start(out=ones_b[:], in_=onesb_d[:]), 16)
        inc(prog, "sync", sk, lambda e: e.dma_start(out=onesf[:], in_=onesf_d[:]), 16)
        v_c = inc(prog, "sync", sk, lambda e: e.dma_start(out=ones_c[:], in_=onesc_d[:]), 16)
        wait(prog, "tensor", sk, v_c)
        wait(prog, "vector", sk, v_c)

        quant_weight_emit(prog, sems, w1T, w1q, D, H_FFN, qbuf, qb2, wqbuf, redc, acc,
                          gcol, ones_c, onesf, ps, grows)
        quant_weight_emit(prog, sems, w2T, w2q, D, H_FFN, qbuf, qb2, wqbuf, redc, acc,
                          gcol, ones_c, onesf, ps, grows)
        quant_weight_emit(prog, sems, w3T, w3q, H_FFN, D, qbuf, qb2, wqbuf, redc, acc,
                          gcol, ones_c, onesf, ps, grows)
        wait(prog, "gpsimd", sems["wa"], sems["wa"].n)
        wait(prog, "gpsimd", sems["wb"], sems["wb"].n)

        v_xr_free = None
        for i in range(NTI):
            c0_, c1_ = i * TT, (i + 1) * TT
            if v_xr_free is not None:
                wait(prog, "sync", sv, v_xr_free)
            src = x1in.rearrange("(j p) t -> p j t", p=128)[:, :, c0_:c1_]
            x_ld = inc(prog, "sync", sx, lambda e, s=src:
                       e.dma_start(out=xr.ap().rearrange("p (j t) -> p j t", j=NCH), in_=s), 16)
            wait(prog, "vector", sx, x_ld)
            if i > 0:
                wait(prog, "vector", so, so.n)
            v_ln = ln_tile_emit(prog, sems, xr, xb16, xln, xxsc, t1sc, ps, ones_b, onesf, rows)

            pre = (sv, v_ln)
            for gi in range(NHC // 4):
                t1 = mm_group_emit(prog, sems, w1q, D, gi * 512, 4, xln, ps, 0, wbuf,
                                   pre_mm_wait=pre)
                t2 = mm_group_emit(prog, sems, w2q, D, gi * 512, 4, xln, ps, 4, wbuf)
                wait(prog, "scalar", st, t1)
                wait(prog, "scalar", sv, sv.n)
                c_sig = None
                for j in range(4):
                    c_sig = inc(prog, "scalar", sems["c"],
                                lambda e, o=ssc[:, j * TT:(j + 1) * TT], p=ps[:, j * 512:j * 512 + TT]:
                                e.activation(o, p, AF.Sigmoid))
                wait(prog, "vector", st, t2)
                wait(prog, "vector", sems["c"], c_sig)
                v_h = None
                for j in range(4):
                    hh = gi * 4 + j
                    v_h = inc(prog, "vector", sv,
                              lambda e, o=gact[:, hh * TT:(hh + 1) * TT],
                              a=ssc[:, j * TT:(j + 1) * TT], p=ps[:, (4 + j) * 512:(4 + j) * 512 + TT]:
                              e.tensor_tensor(o, a, p, ALU.mult))
                if debug and i == 0 and gi == 0:
                    v_dbg = inc(prog, "vector", sv, lambda e:
                                e.tensor_copy(t1sc[:], ps[:, 4 * 512:4 * 512 + TT]))
                    wait(prog, "sync", sv, v_dbg)
                    inc(prog, "sync", so, lambda e: e.dma_start(out=dbg_x2a[:], in_=t1sc[:]), 16)
                    inc(prog, "sync", so, lambda e: e.dma_start(out=dbg_ssc[:], in_=ssc[:]), 16)
                    inc(prog, "sync", so, lambda e: e.dma_start(out=dbg_g0[:], in_=gact[:, 0:4 * TT]), 16)
                pre = (sv, sv.n)

            v_ev = [None]

            def evac_o(j_abs, psl):
                v_ev[0] = inc(prog, "vector", sv,
                              lambda e, o=xo[:, j_abs * TT:(j_abs + 1) * TT], p=psl,
                              x=xr[:, j_abs * TT:(j_abs + 1) * TT]:
                              e.tensor_tensor(o, x, p, ALU.add))
            mm_streamed_emit(prog, sems, w3q, H_FFN, D, gact, ps, wbuf, evac_o,
                             pre_mm_wait=pre)
            v_xr_free = v_ev[0]
            wait(prog, "sync", sv, v_xr_free)
            dst = xout.rearrange("(j p) t -> p j t", p=128)[:, :, c0_:c1_]
            inc(prog, "sync", so, lambda e, d=dst:
                e.dma_start(out=d, in_=xo.ap().rearrange("p (j t) -> p j t", j=NCH)), 16)
            if debug and i == 0:
                inc(prog, "sync", so, lambda e: e.dma_start(out=dbg_g[:], in_=gact[:]), 16)
                inc(prog, "sync", so, lambda e: e.dma_start(out=dbg_xln[:], in_=xln[:]), 16)
                inc(prog, "sync", so, lambda e: e.dma_start(out=dbg_w1q[:], in_=w2q[0:128, :]), 16)
        wait(prog, "sync", so, so.n)
        prog.run(block)
    return nc


def host_prep_B(inputs, x1_shards):
    w1Th = np.ascontiguousarray(inputs["L0_ffn_w1"].T.astype(f32))
    w2Th = np.ascontiguousarray(inputs["L0_ffn_w2"].T.astype(f32))
    w3Th = np.ascontiguousarray(inputs["L0_ffn_w3"].T.astype(f32))
    onesf_h = np.ones((1, 128), f32)
    onesb_h = np.ones((128, 1), ml_dtypes.bfloat16)
    onesc_h = np.ones((128, 1), f32)
    return [{"B_x": np.ascontiguousarray(x1_shards[c]), "B_w1T": w1Th,
             "B_w2T": w2Th, "B_w3T": w3Th, "B_onesf": onesf_h,
             "B_onesb": onesb_h, "B_onesc": onesc_h} for c in range(8)]


def run_launchB(inputs, x1_shards):
    nc = build_launchB()
    in_maps = host_prep_B(inputs, x1_shards)
    res = run_bass_kernel_spmd(nc, in_maps, core_ids=list(range(8)))
    return [r["B_out"] for r in res.results]


def _slopes16():
    start = 2.0 ** (-2.0 ** (-(np.log2(16) - 3)))
    return np.array([start ** (i + 1) for i in range(16)], dtype=np.float32)


def build_launchC(debug=False, nck=None, no_o1=False, no_o2=False):
    nc = bass.Bass("TRN2", target_bir_lowering=False)
    NCKE = nck if nck is not None else (S // CHUNK)
    if debug:
        dbg = {nm: nc.dram_tensor("C_dbg_" + nm, sh, dt, kind="ExternalOutput").ap()
               for nm, sh, dt in (("xln", [128, 2048], BF16), ("qplo", [128, 1024], BF16),
                                  ("qphi", [128, 1024], BF16), ("kplo", [128, 1024], BF16),
                                  ("vt", [128, 1024], BF16), ("smask", [128, 128], BF16),
                                  ("wq", [128, 3072], BF16), ("cos", [128, 1024], F32),
                                  ("st16", [128, 2048], BF16), ("kdlo", [128, 1024], BF16),
                                  ("obuf0", [128, 1024], BF16), ("qqlo", [128, 1024], BF16),
                                  ("st16b", [128, 2048], BF16), ("ops", [128, 128], F32),
                                  ("stf1", [128, 2048], F32))}
    NCK = S // CHUNK
    x2in = nc.dram_tensor("C_x", [D, S], F32, kind="ExternalInput").ap()
    wqkvT = nc.dram_tensor("C_wqkvT", [D, 3072], F32, kind="ExternalInput").ap()
    theta_d = nc.dram_tensor("C_theta", [1, 1024], F32, kind="ExternalInput").ap()
    pos_d = nc.dram_tensor("C_pos", [1, S], F32, kind="ExternalInput").ap()
    maskT_d = nc.dram_tensor("C_maskT", [128, 1024], F32, kind="ExternalInput").ap()
    qdbc_d = nc.dram_tensor("C_qdbc", [128, 1024], F32, kind="ExternalInput").ap()
    kdbc_d = nc.dram_tensor("C_kdbc", [128, 1024], F32, kind="ExternalInput").ap()
    ident_d = nc.dram_tensor("C_ident", [128, 128], BF16, kind="ExternalInput").ap()
    onesf_d = nc.dram_tensor("C_onesf", [1, 128], F32, kind="ExternalInput").ap()
    onesb_d = nc.dram_tensor("C_onesb", [128, 1], BF16, kind="ExternalInput").ap()
    bdcol_d = nc.dram_tensor("C_bdcol", [128, 8], F32, kind="ExternalInput").ap()
    zf32_d = nc.dram_tensor("C_zf32", [128, 2048], F32, kind="ExternalInput").ap()
    ofm = nc.dram_tensor("C_o", [1024, S], BF16, kind="ExternalOutput").ap()

    with ExitStack() as ctx:
        sb = lambda n, s, d: ctx.enter_context(nc.sbuf_tensor(n, s, d))
        wq16 = sb("C_wq16", [128, NCH * 3072], BF16)
        xr = sb("C_xr", [128, NCH * CHUNK], F32)
        xln = sb("C_xln", [128, NCH * CHUNK], BF16)
        xxsc = sb("C_xxsc", [128, 2 * CHUNK], BF16)
        t1sc = sb("C_t1sc", [128, CHUNK], F32)
        rows = sb("C_rows", [1, 5 * CHUNK], F32)
        cosf = sb("C_cosf", [128, 1024], F32)
        sinf = sb("C_sinf", [128, 1024], F32)
        trg = sb("C_trg", [128, 7 * 512], F32)
        cosq = sb("C_cosq", [128, 1024], F32)
        sinq = sb("C_sinq", [128, 1024], F32)
        cosk = sb("C_cosk", [128, 1024], F32)
        sink = sb("C_sink", [128, 1024], F32)
        qp_lo = sb("C_qplo", [128, 1024], BF16)
        qp_hi = sb("C_qphi", [128, 1024], BF16)
        qq_lo = sb("C_qqlo", [128, 1024], BF16)
        qq_hi = sb("C_qqhi", [128, 1024], BF16)
        kp_lo = sb("C_kplo", [128, 1024], BF16)
        kp_hi = sb("C_kphi", [128, 1024], BF16)
        kd_lo = sb("C_kdlo", [128, 1024], BF16)
        kd_hi = sb("C_kdhi", [128, 1024], BF16)
        vt = sb("C_vt", [128, 1024], BF16)
        stf = [sb(f"C_stf{i}", [128, 8 * 256], F32) for i in range(2)]
        st16 = sb("C_st16", [128, 8 * 256], BF16)
        smask = sb("C_smask", [128, 128], BF16)
        ktm = sb("C_ktm", [128, 256], BF16)
        obuf = sb("C_obuf", [128, 1024], BF16)
        theta = sb("C_th", [1, 1024], F32)
        pos = sb("C_pos_s", [1, CHUNK], F32)
        maskT = sb("C_maskT_s", [128, 1024], F32)
        qdbc = sb("C_qdbc_s", [128, 1024], F32)
        kdbc = sb("C_kdbc_s", [128, 1024], F32)
        idb = sb("C_idb", [128, 128], BF16)
        onesf = sb("C_onesf_s", [1, 128], F32)
        ones_b = sb("C_onesb_s", [128, 1], BF16)
        bdcol = sb("C_bdcol_s", [128, 8], F32)
        ps = ctx.enter_context(nc.psum_tensor("C_ps", [128, 3584], F32))
        psB = ctx.enter_context(nc.psum_tensor("C_psB", [128, 256], BF16))

        LNOFF = (3072, 3200, 3328, 3456)

        sems = {k: Sem(ctx, nc, "C_s" + k) for k in ("x", "w", "v", "t", "c", "o", "k", "p")}
        sems["wp"] = [Sem(ctx, nc, "C_swp0"), Sem(ctx, nc, "C_swp1")]
        sems["_wmm"] = [None, None]
        block = ctx.enter_context(nc.Block())
        prog = Prog()
        sx, sw, sv, st, sc, so = (sems[k] for k in ("x", "w", "v", "t", "c", "o"))

        sk = sems["k"]
        sp = sems["p"]
        for (dst, src) in ((theta, theta_d), (maskT, maskT_d),
                           (qdbc, qdbc_d), (kdbc, kdbc_d), (idb, ident_d),
                           (onesf, onesf_d), (ones_b, onesb_d), (bdcol, bdcol_d)):
            inc(prog, "sync", sk, lambda e, d=dst, s=src: e.dma_start(out=d[:], in_=s[:]), 16)
        inc(prog, "sync", sk, lambda e: e.dma_start(out=stf[0][:], in_=zf32_d[:]), 16)
        ld_consts = sk.n
        for kc in range(NCH):
            inc(prog, "gpsimd", sw, lambda e, kc=kc:
                e.dma_start(out=wq16[:, kc * 3072:(kc + 1) * 3072],
                            in_=wqkvT[kc * 128:(kc + 1) * 128, :]), 16)
        inc(prog, "gpsimd", sw, lambda e: e.dma_start(out=st16[:], in_=zf32_d[:]), 16)
        ld_w = sw.n
        wait(prog, "tensor", sk, ld_consts)
        wait(prog, "vector", sk, ld_consts)
        wait(prog, "scalar", sk, ld_consts)
        wait(prog, "tensor", sw, ld_w)
        wait(prog, "vector", sw, ld_w)

        v_xr_free = None
        v_phase_c_last = None
        v_state_upd = {i: None for i in range(8)}
        c_oevac_last = None
        c_ktm_last = None
        v_smask_last = None
        so_last = 0
        for c in range(NCKE):
            cur, nxt = c % 2, (c + 1) % 2
            if v_xr_free is not None:
                wait(prog, "sync", sv, v_xr_free)
            src = x2in.rearrange("(j p) t -> p j t", p=128)[:, :, c * CHUNK:(c + 1) * CHUNK]
            x_ld = inc(prog, "sync", sx, lambda e, s=src:
                       e.dma_start(out=xr.ap().rearrange("p (j t) -> p j t", j=NCH), in_=s), 16)
            if sems.get("_thguard") is not None:
                wait(prog, "sync", st, sems["_tlast"])
            pos_ld = inc(prog, "sync", sp, lambda e, c=c:
                         e.dma_start(out=pos[:], in_=pos_d[0:1, c * CHUNK:(c + 1) * CHUNK]), 16)
            wait(prog, "vector", sx, x_ld)
            v_ln = ln_tile_emit(prog, sems, xr, xln, xln, xxsc, t1sc, ps, ones_b, onesf,
                                rows, ncols=CHUNK, off=LNOFF)
            v_xr_free = v_ln

            for hh in range(2):
                if hh == 0 and c > 0:
                    wait(prog, "tensor", sc, c_v)
                if hh == 0:
                    wait(prog, "tensor", sp, pos_ld)
                if sems.get("_thguard") is not None:
                    wait(prog, "tensor", sv, sems["_thguard"])
                t_th = None
                for i in range(4):
                    gi = hh * 4 + i
                    t_th = inc(prog, "tensor", st, lambda e, gi=gi, i=i:
                               e.matmul(ps[:, 2048 + i * 128:2048 + (i + 1) * 128],
                                        theta[0:1, gi * 128:(gi + 1) * 128],
                                        pos[0:1, :],
                                        start=True, stop=True))
                sems["_tlast"] = t_th
                wait(prog, "vector", st, t_th)
                wait(prog, "vector", sems["c"], sems["c"].n)
                thp = ps[:, 2048:2560]
                PI2I = float(1.0 / (2 * np.pi))
                PI2 = float(2 * np.pi)
                rsl = trg[:, (3 + 2 * hh) * 512:(4 + 2 * hh) * 512]
                r2sl = trg[:, (4 + 2 * hh) * 512:(5 + 2 * hh) * 512]
                inc(prog, "vector", sv, lambda e: e.tensor_scalar(trg[:, 0:512], thp, PI2I, None, ALU.mult))
                inc(prog, "vector", sv, lambda e: e.tensor_scalar(trg[:, 512:1024], trg[:, 0:512], MAGIC, MAGIC, ALU.add, ALU.subtract))
                inc(prog, "vector", sv, lambda e, rsl=rsl: e.scalar_tensor_tensor(rsl, trg[:, 512:1024], -PI2, thp, ALU.mult, ALU.add))
                inc(prog, "vector", sv, lambda e: e.tensor_scalar(trg[:, 512:1024], trg[:, 0:512], 0.25, MAGIC, ALU.add, ALU.add))
                inc(prog, "vector", sv, lambda e: e.tensor_scalar(trg[:, 1024:1536], trg[:, 512:1024], MAGIC, None, ALU.subtract))
                v_trg = inc(prog, "vector", sv, lambda e, r2sl=r2sl: e.scalar_tensor_tensor(r2sl, trg[:, 1024:1536], -PI2, thp, ALU.mult, ALU.add))
                sems["_thguard"] = v_trg
                PIC = float(np.nextafter(np.float32(np.pi), np.float32(0.0)))
                inc(prog, "vector", sv, lambda e, rsl=rsl: e.tensor_scalar(trg[:, 0:512], rsl, PIC, -PIC, ALU.min, ALU.max))
                inc(prog, "vector", sv, lambda e, r2sl=r2sl: e.tensor_scalar(trg[:, 1024:1536], r2sl, float(np.pi / 2), PIC, ALU.add, ALU.min))
                v_trg = inc(prog, "vector", sv, lambda e: e.tensor_scalar(trg[:, 512:1024], trg[:, 1024:1536], -PIC, None, ALU.max))
                wait(prog, "scalar", sv, v_trg)
                inc(prog, "scalar", sc, lambda e, hh=hh: e.activation(sinf[:, hh * 512:(hh + 1) * 512], trg[:, 0:512], AF.Sin))
                c_trig = inc(prog, "scalar", sc, lambda e, hh=hh: e.activation(cosf[:, hh * 512:(hh + 1) * 512], trg[:, 512:1024], AF.Sin))
            wait(prog, "vector", sc, c_trig)
            inc(prog, "vector", sv, lambda e: e.tensor_tensor(cosq[:], cosf[:], qdbc[:], ALU.mult))
            inc(prog, "vector", sv, lambda e: e.tensor_tensor(sinq[:], sinf[:], qdbc[:], ALU.mult))
            inc(prog, "vector", sv, lambda e: e.tensor_tensor(cosk[:], cosf[:], kdbc[:], ALU.mult))
            v_trig_all = inc(prog, "vector", sv, lambda e: e.tensor_tensor(sink[:], sinf[:], kdbc[:], ALU.mult))

            wait(prog, "tensor", sv, v_trig_all)
            if c_oevac_last is not None:
                wait(prog, "tensor", sc, c_oevac_last)
            for i in range(8):
                for kc in range(NCH):
                    run_op(prog, "tensor", lambda e, kc=kc, i=i:
                           e.matmul(ps[:, i * 128:(i + 1) * 128],
                                    wq16[:, kc * 3072 + i * 128: kc * 3072 + (i + 1) * 128],
                                    xln[:, kc * CHUNK:(kc + 1) * CHUNK],
                                    start=(kc == 0), stop=(kc == NCH - 1)))
            for i in range(8):
                for kc in range(NCH):
                    run_op(prog, "tensor", lambda e, kc=kc, i=i:
                           e.matmul(ps[:, 1024 + i * 128:1024 + (i + 1) * 128],
                                    wq16[:, kc * 3072 + 1024 + i * 128: kc * 3072 + 1024 + (i + 1) * 128],
                                    xln[:, kc * CHUNK:(kc + 1) * CHUNK],
                                    start=(kc == 0), stop=(kc == NCH - 1)))
            for i in range(8):
                for kc in range(NCH):
                    is_last = (kc == NCH - 1) and (i == 7)
                    bld = (lambda e, kc=kc, i=i:
                           e.matmul(ps[:, 2048 + i * 128:2048 + (i + 1) * 128],
                                    xln[:, kc * CHUNK:(kc + 1) * CHUNK],
                                    wq16[:, kc * 3072 + 2048 + i * 128: kc * 3072 + 2048 + (i + 1) * 128],
                                    start=(kc == 0), stop=(kc == NCH - 1)))
                    if is_last:
                        t_qkv = inc(prog, "tensor", st, bld)
                    else:
                        run_op(prog, "tensor", bld)

            wait(prog, "vector", st, t_qkv)
            qps = ps[:, 0:1024]
            kps = ps[:, 1024:2048]
            inc(prog, "vector", sv, lambda e: e.tensor_tensor(qp_lo[:], qps, cosf[:], ALU.mult))
            inc(prog, "vector", sv, lambda e: e.tensor_tensor(qp_hi[:], qps, sinf[:], ALU.mult))
            inc(prog, "vector", sv, lambda e: e.tensor_tensor(qq_lo[:], qps, cosq[:], ALU.mult))
            inc(prog, "vector", sv, lambda e: e.tensor_tensor(qq_hi[:], qps, sinq[:], ALU.mult))
            inc(prog, "vector", sv, lambda e: e.tensor_tensor(kp_lo[:], kps, cosf[:], ALU.mult))
            inc(prog, "vector", sv, lambda e: e.tensor_tensor(kp_hi[:], kps, sinf[:], ALU.mult))
            inc(prog, "vector", sv, lambda e: e.tensor_tensor(kd_lo[:], kps, cosk[:], ALU.mult))
            inc(prog, "vector", sv, lambda e: e.tensor_tensor(kd_hi[:], kps, sink[:], ALU.mult))
            v_st16 = sv.n
            wait(prog, "scalar", st, t_qkv)
            c_v = inc(prog, "scalar", sc, lambda e: e.activation(vt[:], ps[:, 2048:3072], AF.Copy))
            v_phase_c_last = v_st16
            if debug and c == 2:
                wait(prog, "sync", sv, v_st16)
                inc(prog, "sync", so, lambda e: e.dma_start(out=dbg["st16"][:], in_=st16[:]), 16)
                wait(prog, "vector", so, so.n)
            if debug and c == 0:
                wait(prog, "sync", sv, v_phase_c_last)
                inc(prog, "sync", so, lambda e: e.dma_start(out=dbg["kdlo"][:], in_=kd_lo[:]), 16)
                wait(prog, "vector", so, so.n)

            for i in range(8):
                if v_smask_last is not None:
                    wait(prog, "tensor", sv, v_smask_last)
                wait(prog, "tensor", sv, v_phase_c_last)
                wait(prog, "tensor", sc, c_v)
                run_op(prog, "tensor", lambda e, i=i:
                       e.matmul(ps[:, 0:128], kp_lo[:, i * 128:(i + 1) * 128],
                                qp_lo[:, i * 128:(i + 1) * 128], start=True, stop=False))
                t_s = inc(prog, "tensor", st, lambda e, i=i:
                          e.matmul(ps[:, 0:128], kp_hi[:, i * 128:(i + 1) * 128],
                                   qp_hi[:, i * 128:(i + 1) * 128], start=False, stop=True))
                wait(prog, "vector", st, t_s)
                v_smask_last = inc(prog, "vector", sv, lambda e, i=i:
                                   e.tensor_tensor(smask[:], ps[:, 0:128], maskT[:, i * 128:(i + 1) * 128], ALU.mult))
                if debug and c == 0 and i == 0:
                    for nm, srcb in (("xln", xln), ("qplo", qp_lo), ("qphi", qp_hi),
                                     ("kplo", kp_lo), ("vt", vt), ("smask", smask),
                                     ("wq", None), ("cos", cosf)):
                        if nm == "wq":
                            inc(prog, "sync", so, lambda e, d=dbg["wq"]:
                                e.dma_start(out=d[:], in_=wq16[:, 0:3072]), 16)
                            continue
                        wait(prog, "sync", sv, v_smask_last)
                        inc(prog, "sync", so, lambda e, d=dbg[nm], s=srcb: e.dma_start(out=d[:], in_=s[:]), 16)
                    wait(prog, "vector", so, so.n)
                if c_oevac_last is not None:
                    wait(prog, "tensor", sc, c_oevac_last)
                wait(prog, "tensor", sv, v_smask_last)
                o_ops = []
                if not no_o1:
                    o_ops.append((vt[:, i * 128:(i + 1) * 128], smask[:]))
                if not no_o2:
                    v_c16 = inc(prog, "vector", sv, lambda e, i=i, cur=cur:
                                e.tensor_copy(st16[:, i * 256:(i + 1) * 256],
                                              stf[cur][:, i * 256:(i + 1) * 256]))
                    wait(prog, "tensor", sv, v_c16)
                    o_ops.append((st16[:, i * 256:i * 256 + 128], qq_lo[:, i * 128:(i + 1) * 128]))
                    o_ops.append((st16[:, i * 256 + 128:i * 256 + 256], qq_hi[:, i * 128:(i + 1) * 128]))
                t_o = None
                for pi, (lh, rh) in enumerate(o_ops):
                    bld = (lambda e, lh=lh, rh=rh, s=(pi == 0), z=(pi == len(o_ops) - 1):
                           e.matmul(ps[:, 512:640], lh, rh, start=s, stop=z))
                    if pi == len(o_ops) - 1:
                        t_o = inc(prog, "tensor", st, bld)
                    else:
                        run_op(prog, "tensor", bld)
                wait(prog, "scalar", st, t_o)
                if debug and c == 1 and i == 0:
                    wait(prog, "sync", st, t_o)
                    inc(prog, "sync", so, lambda e: e.dma_start(out=dbg["qqlo"][:], in_=qq_lo[:]), 16)
                    inc(prog, "sync", so, lambda e: e.dma_start(out=dbg["st16b"][:], in_=st16[:]), 16)
                    wait(prog, "scalar", so, so.n)
                c_oevac_last = inc(prog, "scalar", sc, lambda e, i=i:
                                   e.activation(obuf[:, i * 128:(i + 1) * 128], ps[:, 512:640], AF.Copy))
                if c_ktm_last is not None:
                    wait(prog, "tensor", sc, c_ktm_last)
                run_op(prog, "tensor", lambda e, i=i:
                       e.transpose(psB[:, 0:128], kd_lo[:, i * 128:(i + 1) * 128], idb[:]))
                t_tp = inc(prog, "tensor", st, lambda e, i=i:
                           e.transpose(psB[:, 128:256], kd_hi[:, i * 128:(i + 1) * 128], idb[:]))
                wait(prog, "scalar", st, t_tp)
                c_ktm_last = inc(prog, "scalar", sc, lambda e: e.activation(ktm[:], psB[:], AF.Copy))
                wait(prog, "tensor", sv, sv.n)
                wait(prog, "tensor", sc, c_ktm_last)
                run_op(prog, "tensor", lambda e, i=i:
                       e.matmul(ps[:, 1024:1152], ktm[:, 0:128], vt[:, i * 128:(i + 1) * 128],
                                start=True, stop=True))
                t_c2 = inc(prog, "tensor", st, lambda e, i=i:
                           e.matmul(ps[:, 1536:1664], ktm[:, 128:256], vt[:, i * 128:(i + 1) * 128],
                                    start=True, stop=True))
                wait(prog, "vector", st, t_c2)
                inc(prog, "vector", sv, lambda e, i=i, cur=cur, nxt=nxt:
                    e.scalar_tensor_tensor(stf[nxt][:, i * 256:i * 256 + 128],
                                           stf[cur][:, i * 256:i * 256 + 128], bdcol[:, i:i + 1],
                                           ps[:, 1024:1152], ALU.mult, ALU.add))
                v_state_upd[i] = inc(prog, "vector", sv, lambda e, i=i, cur=cur, nxt=nxt:
                                     e.scalar_tensor_tensor(stf[nxt][:, i * 256 + 128:i * 256 + 256],
                                                            stf[cur][:, i * 256 + 128:i * 256 + 256], bdcol[:, i:i + 1],
                                                            ps[:, 1536:1664], ALU.mult, ALU.add))
            if debug and c == 0:
                wait(prog, "sync", sc, c_oevac_last)
                inc(prog, "sync", so, lambda e: e.dma_start(out=dbg["obuf0"][:], in_=obuf[:]), 16)
                wait(prog, "sync", sv, v_state_upd[7])
                inc(prog, "sync", so, lambda e, nxt=nxt: e.dma_start(out=dbg["stf1"][:], in_=stf[nxt][:]), 16)
                wait(prog, "vector", so, so.n)
            wait(prog, "sync", sc, c_oevac_last)
            if so_last:
                pass
            dst = ofm.rearrange("(i p) t -> p i t", p=128)[:, :, c * CHUNK:(c + 1) * CHUNK]
            so_last = inc(prog, "sync", so, lambda e, d=dst:
                          e.dma_start(out=d, in_=obuf.ap().rearrange("p (i t) -> p i t", t=CHUNK)), 16)
            wait(prog, "scalar", so, so_last)
        wait(prog, "sync", so, so.n)
        prog.run(block)
    return nc


def host_prep_C(inputs, x2_shards):
    w_qkv = inputs["L1_attn_w_qkv"].astype(f32)
    theta = inputs["L1_attn_theta"].astype(f32).reshape(NH, EH)
    slopes = _slopes16()
    idx = np.arange(CHUNK, dtype=np.float64)
    diff = idx[:, None] - idx[None, :]
    in_maps = []
    onesf_h = np.ones((1, 128), f32)
    onesb_h = np.ones((128, 1), ml_dtypes.bfloat16)
    ident_h = np.eye(128, dtype=ml_dtypes.bfloat16)
    pos_h = np.arange(S, dtype=f32).reshape(1, S)
    for core in range(8):
        b, g = core // 2, core % 2
        hsl = slice(g * 8, (g + 1) * 8)
        wg = np.concatenate([w_qkv[g * 1024:(g + 1) * 1024],
                             w_qkv[2048 + g * 1024:2048 + (g + 1) * 1024],
                             w_qkv[4096 + g * 1024:4096 + (g + 1) * 1024]], axis=0)
        wqkvT_h = np.ascontiguousarray(wg.T)
        theta_h = np.ascontiguousarray(theta[hsl].reshape(1, 1024))
        maskT_h = np.empty((128, 1024), f32)
        qdbc_h = np.empty((128, 1024), f32)
        kdbc_h = np.empty((128, 1024), f32)
        for i in range(8):
            s = float(slopes[g * 8 + i])
            m = np.where(diff >= 0, np.exp(np.float32(-s) * diff.astype(f32)), 0.0).astype(f32)
            maskT_h[:, i * 128:(i + 1) * 128] = m.T
            qdbc_h[:, i * 128:(i + 1) * 128] = np.exp(np.float32(-s) * (idx + 1.0).astype(f32))[None, :]
            kdbc_h[:, i * 128:(i + 1) * 128] = np.exp(np.float32(-s) * (CHUNK - 1.0 - idx).astype(f32))[None, :]
        x2full = np.concatenate([x2_shards[2 * b], x2_shards[2 * b + 1]], axis=1)
        bdcol_h = np.broadcast_to(
            np.exp(-slopes[g * 8:(g + 1) * 8].astype(f32) * np.float32(CHUNK))[None, :],
            (128, 8)).astype(f32)
        in_maps.append({"C_zf32": np.zeros((128, 2048), f32),
                        "C_x": np.ascontiguousarray(x2full), "C_wqkvT": wqkvT_h,
                        "C_theta": theta_h, "C_pos": pos_h, "C_maskT": maskT_h,
                        "C_qdbc": qdbc_h, "C_kdbc": kdbc_h, "C_ident": ident_h,
                        "C_onesf": onesf_h, "C_onesb": onesb_h,
                        "C_bdcol": np.ascontiguousarray(bdcol_h)})
    return in_maps


def run_launchC(inputs, x2_shards):
    nc = build_launchC()
    in_maps = host_prep_C(inputs, x2_shards)
    res = run_bass_kernel_spmd(nc, in_maps, core_ids=list(range(8)))
    return [r["C_o"] for r in res.results]


def build_launchD():
    nc = bass.Bass("TRN2", target_bir_lowering=False)
    x2in = nc.dram_tensor("D_x", [D, SH], F32, kind="ExternalInput").ap()
    oin = nc.dram_tensor("D_o", [D, SH], BF16, kind="ExternalInput").ap()
    wprojT = nc.dram_tensor("D_wprojT", [D, D], F32, kind="ExternalInput").ap()
    wr_d = nc.dram_tensor("D_wr", [D, 8], F32, kind="ExternalInput").ap()
    wg_d = nc.dram_tensor("D_wg", [8, D, D], F32, kind="ExternalInput").ap()
    wu_d = nc.dram_tensor("D_wu", [8, D, D], F32, kind="ExternalInput").ap()
    wd_d = nc.dram_tensor("D_wd", [8, D, D], F32, kind="ExternalInput").ap()
    oh8_d = nc.dram_tensor("D_oh8", [8, 8 * 128], F32, kind="ExternalInput").ap()
    ident_d = nc.dram_tensor("D_ident", [128, 128], F32, kind="ExternalInput").ap()
    onesf_d = nc.dram_tensor("D_onesf", [1, 128], F32, kind="ExternalInput").ap()
    onesb_d = nc.dram_tensor("D_onesb", [128, 1], BF16, kind="ExternalInput").ap()
    xout = nc.dram_tensor("D_out", [D, SH], F32, kind="ExternalOutput").ap()

    with ExitStack() as ctx:
        sb = lambda n, s, d: ctx.enter_context(nc.sbuf_tensor(n, s, d))
        xr = sb("D_xr", [128, NCH * TT], F32)
        ob = sb("D_ob", [128, NCH * TT], BF16)
        x3r = sb("D_x3r", [128, NCH * TT], F32)
        xln = sb("D_xln", [128, NCH * TT], BF16)
        hbuf = sb("D_h", [128, NCH * TT], BF16)
        ssc = sb("D_ssc", [128, 4 * TT], F32)
        xxsc = sb("D_xxsc", [128, 2 * TT], BF16)
        t1sc = sb("D_t1sc", [128, TT], F32)
        wbuf = [sb(f"D_wb{i}", [128, 4 * 1024], BF16) for i in range(2)]
        wr16 = sb("D_wr16", [128, NCH * 8], BF16)
        lgt = sb("D_lgt", [128, 14 * 8], F32)
        wgt8 = sb("D_wgt8", [8, 4 * 128], F32)
        wbc16 = sb("D_wbc", [128, TT], BF16)
        idb = sb("D_idb", [128, 128], F32)
        oh8 = sb("D_oh8_s", [8, 8 * 128], F32)
        onesf = sb("D_onesf_s", [1, 128], F32)
        ones_b = sb("D_onesb_s", [128, 1], BF16)
        rows = sb("D_rows", [1, 5 * 512], F32)
        ps = ctx.enter_context(nc.psum_tensor("D_ps", [128, 4096], F32))

        sems = {k: Sem(ctx, nc, "D_s" + k) for k in ("x", "w", "v", "t", "c", "o", "k")}
        sems["wp"] = [Sem(ctx, nc, "D_swp0"), Sem(ctx, nc, "D_swp1")]
        sems["_wmm"] = [None, None]
        block = ctx.enter_context(nc.Block())
        prog = Prog()
        sx, sw, sv, st, sc, so = (sems[k] for k in ("x", "w", "v", "t", "c", "o"))

        sk = sems["k"]
        for (dst, src) in ((idb, ident_d), (oh8, oh8_d), (onesf, onesf_d), (ones_b, onesb_d)):
            inc(prog, "sync", sk, lambda e, d=dst, s=src: e.dma_start(out=d[:], in_=s[:]), 16)
        inc(prog, "gpsimd", sw, lambda e:
            e.dma_start(out=wr16.ap().rearrange("p (k e) -> p k e", e=8),
                        in_=wr_d.rearrange("(k p) e -> p k e", p=128)), 16)
        ld_consts = sk.n
        ld_wr = sw.n
        wait(prog, "tensor", sk, ld_consts)
        wait(prog, "vector", sk, ld_consts)
        wait(prog, "scalar", sk, ld_consts)
        wait(prog, "tensor", sw, ld_wr)

        v_tile_free = None
        for i in range(NTI):
            c0_, c1_ = i * TT, (i + 1) * TT
            if v_tile_free is not None:
                wait(prog, "sync", sv, v_tile_free)
            srcx = x2in.rearrange("(j p) t -> p j t", p=128)[:, :, c0_:c1_]
            x_ld = inc(prog, "sync", sx, lambda e, s=srcx:
                       e.dma_start(out=xr.ap().rearrange("p (j t) -> p j t", j=NCH), in_=s), 16)
            srco = oin.rearrange("(j p) t -> p j t", p=128)[:, :, c0_:c1_]
            o_ld = inc(prog, "sync", sx, lambda e, s=srco:
                       e.dma_start(out=ob.ap().rearrange("p (j t) -> p j t", j=NCH), in_=s), 16)
            wait(prog, "vector", sx, o_ld)
            if i > 0:
                wait(prog, "vector", so, so.n)

            v_ev = [None]

            def evac_x3(j_abs, psl):
                v_ev[0] = inc(prog, "vector", sv,
                              lambda e, o=x3r[:, j_abs * TT:(j_abs + 1) * TT], p=psl,
                              x=xr[:, j_abs * TT:(j_abs + 1) * TT]:
                              e.tensor_tensor(o, x, p, ALU.add))
            wait(prog, "tensor", sx, o_ld)
            mm_streamed_emit(prog, sems, wprojT, D, D, ob, ps, wbuf, evac_x3,
                             pre_mm_wait=(sv, sv.n))
            v_ln = ln_tile_emit(prog, sems, x3r, xln, xln, xxsc, t1sc, ps, ones_b, onesf, rows)

            wait(prog, "tensor", sv, v_ln)
            for tsub in range(4):
                wait(prog, "tensor", sv, sv.n)
                wait(prog, "tensor", sems["c"], sems["c"].n)
                wait(prog, "vector", st, st.n)
                lgp = ps[0:128, 2048 + tsub * 8: 2056 + tsub * 8]
                t_lg = None
                for kc in range(NCH):
                    t_lg = inc(prog, "tensor", st, lambda e, kc=kc, tsub=tsub, lgp=lgp:
                               e.matmul(lgp, xln[:, kc * TT + tsub * 128: kc * TT + (tsub + 1) * 128],
                                        wr16[:, kc * 8:(kc + 1) * 8],
                                        start=(kc == 0), stop=(kc == NCH - 1)))
                wait(prog, "vector", st, t_lg)
                L = lambda k: lgt[:, k * 8:(k + 1) * 8]
                C1 = lambda k: lgt[:, 96 + k:96 + k + 1]
                inc(prog, "vector", sv, lambda e: e.tensor_copy(L(0), lgp))
                inc(prog, "vector", sv, lambda e: e.reduce_max(C1(0), L(0), AX.X))
                v_sm = inc(prog, "vector", sv, lambda e: e.tensor_scalar(L(1), L(0), C1(0), None, ALU.subtract))
                wait(prog, "scalar", sv, v_sm)
                c_ex = inc(prog, "scalar", sc, lambda e: e.activation(L(2), L(1), AF.Exp))
                wait(prog, "vector", sc, c_ex)
                inc(prog, "vector", sv, lambda e: e.reduce_sum(C1(1), L(2), AX.X))
                inc(prog, "vector", sv, lambda e: e.reciprocal(C1(2), C1(1)))
                inc(prog, "vector", sv, lambda e: e.tensor_scalar(L(3), L(2), C1(2), None, ALU.mult))
                inc(prog, "vector", sv, lambda e: e.reduce_max(C1(3), L(3), AX.X))
                inc(prog, "vector", sv, lambda e: e.tensor_scalar(L(4), L(3), C1(3), None, ALU.is_ge))
                inc(prog, "vector", sv, lambda e: e.tensor_scalar(L(5), L(4), -1.0, 1.0, ALU.mult, ALU.add))
                inc(prog, "vector", sv, lambda e: e.tensor_tensor(L(6), L(3), L(5), ALU.mult))
                inc(prog, "vector", sv, lambda e: e.reduce_max(C1(4), L(6), AX.X))
                inc(prog, "vector", sv, lambda e: e.tensor_scalar(L(7), L(3), C1(4), None, ALU.is_ge))
                inc(prog, "vector", sv, lambda e: e.tensor_tensor(L(8), L(3), L(7), ALU.mult))
                inc(prog, "vector", sv, lambda e: e.tensor_tensor(C1(5), C1(3), C1(4), ALU.add))
                inc(prog, "vector", sv, lambda e: e.reciprocal(C1(6), C1(5)))
                v_wn = inc(prog, "vector", sv, lambda e: e.tensor_scalar(L(9), L(8), C1(6), None, ALU.mult))
                wait(prog, "tensor", sv, v_wn)
                wnp = ps[:, 2176:2304]
                t_tr = inc(prog, "tensor", st, lambda e: e.transpose(wnp[0:8, 0:128], L(9), idb[:]))
                wait(prog, "scalar", st, t_tr)
                c_w = inc(prog, "scalar", sc, lambda e, tsub=tsub:
                          e.activation(wgt8[0:8, tsub * 128:(tsub + 1) * 128], wnp[0:8, 0:128], AF.Copy))
            wait(prog, "vector", sems["c"], sems["c"].n)

            for ex in range(8):
                wait(prog, "tensor", sems["c"], sems["c"].n)
                wait(prog, "tensor", sv, sv.n)
                wbp = ps[:, 2560:2560 + TT]
                t_wb = inc(prog, "tensor", st, lambda e, ex=ex:
                           e.matmul(wbp, oh8[0:8, ex * 128:(ex + 1) * 128], wgt8[0:8, :], start=True, stop=True))
                wait(prog, "scalar", st, t_wb)
                c_wb = inc(prog, "scalar", sc, lambda e: e.activation(wbc16[:], wbp, AF.Copy))
                wait(prog, "vector", sems["c"], c_wb)
                pre = (sv, sv.n)
                for gi in range(4):
                    t1 = mm_group_emit(prog, sems, wg_d[ex], D, gi * 512, 4, xln, ps, 0, wbuf,
                                       pre_mm_wait=pre)
                    t2 = mm_group_emit(prog, sems, wu_d[ex], D, gi * 512, 4, xln, ps, 4, wbuf,
                                       pre_mm_wait=(sems["c"], c_wb) if gi == 0 else None)
                    wait(prog, "scalar", st, t1)
                    wait(prog, "scalar", sv, sv.n)
                    c_sig = None
                    for j in range(4):
                        c_sig = inc(prog, "scalar", sems["c"],
                                    lambda e, o=ssc[:, j * TT:(j + 1) * TT], p=ps[:, j * 512:j * 512 + TT]:
                                    e.activation(o, p, AF.Silu))
                    wait(prog, "vector", st, t2)
                    wait(prog, "vector", sems["c"], c_sig)
                    v_h = None
                    for j in range(4):
                        hh = gi * 4 + j
                        inc(prog, "vector", sv,
                            lambda e, o=t1sc[:], a=ssc[:, j * TT:(j + 1) * TT],
                            p=ps[:, (4 + j) * 512:(4 + j) * 512 + TT]:
                            e.tensor_tensor(o, a, p, ALU.mult))
                        v_h = inc(prog, "vector", sv,
                                  lambda e, o=hbuf[:, hh * TT:(hh + 1) * TT], a=t1sc[:]:
                                  e.tensor_tensor(o, a, wbc16[:], ALU.mult))
                    pre = (sv, v_h)
                src_y = x3r if ex == 0 else (xr if ex % 2 == 1 else x3r)
                dst_y = xr if ex % 2 == 0 else x3r
                v_ev2 = [None]

                def evac_y(j_abs, psl, src_y=src_y, dst_y=dst_y):
                    v_ev2[0] = inc(prog, "vector", sv,
                                   lambda e, o=dst_y[:, j_abs * TT:(j_abs + 1) * TT], p=psl,
                                   x=src_y[:, j_abs * TT:(j_abs + 1) * TT]:
                                   e.tensor_tensor(o, x, p, ALU.add))
                mm_streamed_emit(prog, sems, wd_d[ex], D, D, hbuf, ps, wbuf, evac_y,
                                 pre_mm_wait=pre)
            v_tile_free = v_ev2[0]
            wait(prog, "sync", sv, v_tile_free)
            dst = xout.rearrange("(j p) t -> p j t", p=128)[:, :, c0_:c1_]
            inc(prog, "sync", so, lambda e, d=dst:
                e.dma_start(out=d, in_=x3r.ap().rearrange("p (j t) -> p j t", j=NCH)), 16)
        wait(prog, "sync", so, so.n)
        prog.run(block)
    return nc


def host_prep_D(inputs, x2_shards, o_cores):
    wprojT_h = np.ascontiguousarray(inputs["L1_attn_w_proj"].T.astype(f32))
    wr_h = np.ascontiguousarray(inputs["L1_ffn_Wr"].astype(f32))
    wg_h = np.ascontiguousarray(inputs["L1_ffn_Wg"].astype(f32))
    wu_h = np.ascontiguousarray(inputs["L1_ffn_Wu"].astype(f32))
    wd_h = np.ascontiguousarray(inputs["L1_ffn_Wd"].astype(f32))
    oh8_h = np.zeros((8, 8 * 128), f32)
    for ex in range(8):
        oh8_h[ex, ex * 128:(ex + 1) * 128] = 1.0
    ident_h = np.eye(128, dtype=f32)
    onesf_h = np.ones((1, 128), f32)
    onesb_h = np.ones((128, 1), ml_dtypes.bfloat16)
    in_maps = []
    for core in range(8):
        b, h = core // 2, core % 2
        ofull = np.concatenate([o_cores[2 * b], o_cores[2 * b + 1]], axis=0)
        osh = np.ascontiguousarray(ofull[:, h * SH:(h + 1) * SH])
        in_maps.append({"D_x": np.ascontiguousarray(x2_shards[core]), "D_o": osh,
                        "D_wprojT": wprojT_h, "D_wr": wr_h, "D_wg": wg_h,
                        "D_wu": wu_h, "D_wd": wd_h, "D_oh8": oh8_h,
                        "D_ident": ident_h, "D_onesf": onesf_h, "D_onesb": onesb_h})
    return in_maps


def run_launchD(inputs, x2_shards, o_cores):
    nc = build_launchD()
    in_maps = host_prep_D(inputs, x2_shards, o_cores)
    res = run_bass_kernel_spmd(nc, in_maps, core_ids=list(range(8)))
    return [r["D_out"] for r in res.results]


def build_fused(debug=False):
    nc = bass.Bass("TRN2", target_bir_lowering=False)
    if debug:
        dbg_x2 = nc.dram_tensor("F_dbg_x2", [D, SH], F32, kind="ExternalOutput").ap()
        dbg_cc = nc.dram_tensor("F_dbg_cc", [2 * D, SH], BF16, kind="ExternalOutput").ap()
        dbg_oc = nc.dram_tensor("F_dbg_oc", [2048, S], BF16, kind="ExternalOutput").ap()
        dbg_ci = nc.dram_tensor("F_dbg_ci", [D, SH], BF16, kind="ExternalOutput").ap()
        dbg_xl = nc.dram_tensor("F_dbg_xl", [128, NCH * TT], BF16, kind="ExternalOutput").ap()
        dbg_rw = nc.dram_tensor("F_dbg_rw", [1, 5 * 512], F32, kind="ExternalOutput").ap()
        dbg_rw1 = nc.dram_tensor("F_dbg_rw1", [1, 5 * 512], F32, kind="ExternalOutput").ap()
        dbg_rw2 = nc.dram_tensor("F_dbg_rw2", [1, 5 * 512], F32, kind="ExternalOutput").ap()
    NT = NTI + 1
    NHC = H_FFN // 128
    NCK = S // CHUNK
    QC = 1408

    xin = nc.dram_tensor("F_x", [D, WARM + SH], F32, kind="ExternalInput").ap()
    winT = nc.dram_tensor("F_winT", [D, D], F32, kind="ExternalInput").ap()
    woutT = nc.dram_tensor("F_woutT", [D, D], F32, kind="ExternalInput").ap()
    dcol_d = nc.dram_tensor("F_dcol", [128, NCH], F32, kind="ExternalInput").ap()
    bcol_d = nc.dram_tensor("F_bcol", [128, NCH], F32, kind="ExternalInput").ap()
    w1T = nc.dram_tensor("F_w1T", [D, H_FFN], F32, kind="ExternalInput").ap()
    w2T = nc.dram_tensor("F_w2T", [D, H_FFN], F32, kind="ExternalInput").ap()
    w3T = nc.dram_tensor("F_w3T", [H_FFN, D], F32, kind="ExternalInput").ap()
    wqkvT = nc.dram_tensor("F_wqkvT", [D, 3072], F32, kind="ExternalInput").ap()
    theta_d = nc.dram_tensor("F_theta", [1, 1024], F32, kind="ExternalInput").ap()
    pos_d = nc.dram_tensor("F_pos", [1, S], F32, kind="ExternalInput").ap()
    maskT_d = nc.dram_tensor("F_maskT", [128, 1024], F32, kind="ExternalInput").ap()
    qdbc_d = nc.dram_tensor("F_qdbc", [128, 1024], F32, kind="ExternalInput").ap()
    kdbc_d = nc.dram_tensor("F_kdbc", [128, 1024], F32, kind="ExternalInput").ap()
    identb_d = nc.dram_tensor("F_identb", [128, 128], BF16, kind="ExternalInput").ap()
    identf_d = nc.dram_tensor("F_identf", [128, 128], F32, kind="ExternalInput").ap()
    bdcol_d = nc.dram_tensor("F_bdcol", [128, 8], F32, kind="ExternalInput").ap()
    zf32_d = nc.dram_tensor("F_zf32", [128, 2048], F32, kind="ExternalInput").ap()
    wprojT = nc.dram_tensor("F_wprojT", [D, D], F32, kind="ExternalInput").ap()
    wr_d = nc.dram_tensor("F_wr", [D, 8], F32, kind="ExternalInput").ap()
    wg_d = nc.dram_tensor("F_wg", [8, D, D], F32, kind="ExternalInput").ap()
    wu_d = nc.dram_tensor("F_wu", [8, D, D], F32, kind="ExternalInput").ap()
    wd_d = nc.dram_tensor("F_wd", [8, D, D], F32, kind="ExternalInput").ap()
    oh8_d = nc.dram_tensor("F_oh8", [8, 8 * 128], F32, kind="ExternalInput").ap()
    onesf_d = nc.dram_tensor("F_onesf", [1, 128], F32, kind="ExternalInput").ap()
    onesb_d = nc.dram_tensor("F_onesb", [128, 1], BF16, kind="ExternalInput").ap()
    onesc_d = nc.dram_tensor("F_onesc", [128, 1], F32, kind="ExternalInput").ap()
    msel_d = nc.dram_tensor("F_msel", [128, 2], F32, kind="ExternalInput").ap()
    xout = nc.dram_tensor("F_out", [D, SH], F32, kind="ExternalOutput").ap()
    w1q = nc.dram_tensor("F_w1q", [D, H_FFN], BF16).ap()
    win16 = nc.dram_tensor("F_win16", [D, D], BF16).ap()
    wout16 = nc.dram_tensor("F_wout16", [D, D], BF16).ap()
    wproj16 = nc.dram_tensor("F_wproj16", [D, D], BF16).ap()
    wg16 = nc.dram_tensor("F_wg16", [8, D, D], BF16).ap()
    wu16 = nc.dram_tensor("F_wu16", [8, D, D], BF16).ap()
    wd16 = nc.dram_tensor("F_wd16", [8, D, D], BF16).ap()
    w2q = nc.dram_tensor("F_w2q", [D, H_FFN], BF16).ap()
    w3q = nc.dram_tensor("F_w3q", [H_FFN, D], BF16).ap()
    x2sc = nc.dram_tensor("F_x2", [D, SH], F32).ap()
    ccin = nc.dram_tensor("F_ccin", [D, SH], BF16).ap()
    ccout = nc.dram_tensor("F_ccout", [2 * D, SH], BF16).ap()
    ocin = nc.dram_tensor("F_ocin", [1024, S], BF16).ap()
    ocout = nc.dram_tensor("F_ocout", [2048, S], BF16).ap()
    RG = [[0, 1], [2, 3], [4, 5], [6, 7]]

    with ExitStack() as ctx:
        sb = lambda n, s, d: ctx.enter_context(nc.sbuf_tensor(n, s, d))
        sems = {k: Sem(ctx, nc, "F_s" + k) for k in ("x", "w", "v", "t", "c", "o", "k", "p", "q", "g", "d",
                                                      "qa", "qb", "wa", "wb")}
        sems["wp"] = [Sem(ctx, nc, "F_swp0"), Sem(ctx, nc, "F_swp1")]
        sems["_wmm"] = [None, None]
        block = ctx.enter_context(nc.Block())
        prog = Prog()
        sx, sw, sv, st, sc, so = (sems[k] for k in ("x", "w", "v", "t", "c", "o"))
        sk, sp, sq, sg = sems["k"], sems["p"], sems["q"], sems["g"]
        ones_b = sb("F_o1", [128, 1], BF16)
        onesf = sb("F_o2", [1, 128], F32)

        with ExitStack() as cab:
            sba = lambda n, s, d: cab.enter_context(nc.sbuf_tensor(n, s, d))
            wbuf = [sba(f"F_wbA{i}", [128, 4 * 1024], BF16) for i in range(2)]
            xr = sba("F_xr", [128, NCH * TT], F32)
            xln = sba("F_xln", [128, NCH * TT], BF16)
            q = sba("F_q", [128, NCH * TT], F32)
            stk = sba("F_stk", [128, NCH * TT], BF16)
            gact = sba("F_g", [128, NHC * TT], BF16)
            xxsc = sba("F_xxsc", [128, 2 * TT], BF16)
            t1sc = sba("F_t1sc", [128, TT], F32)
            qbuf = [sba(f"F_qb{i}", [128, QC], F32) for i in range(2)]
            qb2 = sba("F_qb2", [128, QC], F32)
            wqbuf = [sba(f"F_wqb{i}", [128, QC], BF16) for i in range(2)]
            ssc = sba("F_ssc", [128, 4 * TT], F32)
            redc = sba("F_redc", [128, 96], F32)
            acc = sba("F_acc", [128, 1], F32)
            gcol = sba("F_gcol", [128, 1], F32)
            ones_c = sba("F_oc", [128, 1], F32)
            grows = sba("F_grows", [1, 4], F32)
            dcol = sba("F_dc", [128, NCH], F32)
            bcol = sba("F_bc", [128, NCH], F32)
            dsig = sba("F_ds", [128, NCH], F32)
            omd = sba("F_om", [128, NCH], F32)
            stcol = sba("F_st", [128, NCH], F32)
            rows = sba("F_rows", [1, 5 * 512], F32)
            ps = cab.enter_context(nc.psum_tensor("F_psAB", [128, 4096], F32))

            inc(prog, "sync", sk, lambda e: e.dma_start(out=ones_b[:], in_=onesb_d[:]), 16)
            inc(prog, "sync", sk, lambda e: e.dma_start(out=onesf[:], in_=onesf_d[:]), 16)
            inc(prog, "sync", sk, lambda e: e.dma_start(out=ones_c[:], in_=onesc_d[:]), 16)
            inc(prog, "sync", sk, lambda e: e.dma_start(out=dcol[:], in_=dcol_d[:]), 16)
            k0 = inc(prog, "sync", sk, lambda e: e.dma_start(out=bcol[:], in_=bcol_d[:]), 16)
            wait(prog, "scalar", sk, k0)
            wait(prog, "tensor", sk, k0)
            wait(prog, "vector", sk, k0)
            c0s = inc(prog, "scalar", sc, lambda e: e.activation(dsig[:], dcol[:], AF.Sigmoid))
            wait(prog, "vector", sc, c0s)
            inc(prog, "vector", sv, lambda e: e.tensor_scalar(omd[:], dsig[:], -1.0, 1.0, ALU.mult, ALU.add))
            sd = sems["d"]
            inc(prog, "gpsimd", sd, lambda e: e.dma_start(out=win16[:, :], in_=winT[:, :]), 16)
            sd_win = inc(prog, "gpsimd", sd, lambda e: e.dma_start(out=wout16[:, :], in_=woutT[:, :]), 16)
            wait(prog, "gpsimd", sd, sd_win)
            conv_jobs = [(wproj16, wprojT)]
            for ex in range(8):
                conv_jobs += [(wg16[ex], wg_d[ex]), (wu16[ex], wu_d[ex]), (wd16[ex], wd_d[ex])]

            quant_weight_emit(prog, sems, w1T, w1q, D, H_FFN, qbuf, qb2, wqbuf, redc, acc,
                              gcol, ones_c, onesf, ps, grows, qcols=QC)
            quant_weight_emit(prog, sems, w2T, w2q, D, H_FFN, qbuf, qb2, wqbuf, redc, acc,
                              gcol, ones_c, onesf, ps, grows, qcols=QC)
            quant_weight_emit(prog, sems, w3T, w3q, H_FFN, D, qbuf, qb2, wqbuf, redc, acc,
                              gcol, ones_c, onesf, ps, grows, qcols=QC)
            wait(prog, "gpsimd", sems["wa"], sems["wa"].n)
            wait(prog, "gpsimd", sems["wb"], sems["wb"].n)

            v_xr_free = None
            for i in range(NT):
                warm = (i == 0)
                c0_ = i * TT if warm else WARM + (i - 1) * TT
                c1_ = c0_ + TT
                if v_xr_free is not None:
                    wait(prog, "sync", sv, v_xr_free)
                src = xin.rearrange("(j p) t -> p j t", p=128)[:, :, c0_:c1_]
                x_ld = inc(prog, "sync", sx, lambda e, s=src:
                           e.dma_start(out=xr.ap().rearrange("p (j t) -> p j t", j=NCH), in_=s), 16)
                wait(prog, "vector", sx, x_ld)
                if not warm:
                    wait(prog, "vector", so, so.n)
                v_ln = ln_tile_emit(prog, sems, xr, xln, xln, xxsc, t1sc, ps, ones_b, onesf, rows)
                if debug and i == 1:
                    wait(prog, "sync", sv, v_ln)
                    inc(prog, "sync", so, lambda e: e.dma_start(out=dbg_rw1[:], in_=rows[:]), 16)
                v_ev = [None]

                def evac_q(j_abs, psl):
                    v_ev[0] = inc(prog, "vector", sv,
                                  lambda e, o=q[:, j_abs * TT:(j_abs + 1) * TT], p=psl,
                                  s=omd[:, j_abs:j_abs + 1]:
                                  e.tensor_scalar(o, p, s, None, ALU.mult))
                mm_streamed_emit(prog, sems, win16, D, D, xln, ps, wbuf, evac_q,
                                 pre_mm_wait=(sv, v_ln))
                for j in range(NCH):
                    init = 0.0 if warm else stcol[:, j:j + 1]
                    dbc = dsig[:, j:j + 1].broadcast_to([128, TT])
                    inc(prog, "vector", sv,
                        lambda e, o=stk[:, j * TT:(j + 1) * TT], d=dbc,
                        qq=q[:, j * TT:(j + 1) * TT], ii=init:
                        e.tensor_tensor_scan(o, d, qq, ii, ALU.mult, ALU.add))
                v_scan = inc(prog, "vector", sv, lambda e: e.tensor_copy(stcol[:], stk[:, TT - 1::TT]))
                if warm:
                    v_xr_free = v_scan
                    continue

                def evac_x1(j_abs, psl):
                    v_ev[0] = inc(prog, "vector", sv,
                                  lambda e, o=q[:, j_abs * TT:(j_abs + 1) * TT], p=psl,
                                  s=bcol[:, j_abs:j_abs + 1], x=xr[:, j_abs * TT:(j_abs + 1) * TT]:
                                  e.scalar_tensor_tensor(o, p, s, x, ALU.add, ALU.add))
                mm_streamed_emit(prog, sems, wout16, D, D, stk, ps, wbuf, evac_x1,
                                 pre_mm_wait=(sv, v_scan))
                v_ln2 = ln_tile_emit(prog, sems, q, xln, xln, xxsc, t1sc, ps, ones_b, onesf, rows)
                if debug and i == 1:
                    wait(prog, "sync", sv, v_ln2)
                    inc(prog, "sync", so, lambda e: e.dma_start(out=dbg_rw2[:], in_=rows[:]), 16)
                pre = (sv, v_ln2)
                ev2 = [pre, pre]
                for gi in range(NHC // 2):
                    par = gi % 2
                    b0 = par * 4
                    t1 = mm_group_emit(prog, sems, w1q, D, gi * 256, 2, xln, ps, b0, wbuf,
                                       pre_mm_wait=ev2[par])
                    t2 = mm_group_emit(prog, sems, w2q, D, gi * 256, 2, xln, ps, b0 + 2, wbuf)
                    wait(prog, "scalar", st, t1)
                    wait(prog, "scalar", sv, sv.n)
                    c_sig = None
                    for j in range(2):
                        c_sig = inc(prog, "scalar", sems["c"],
                                    lambda e, o=ssc[:, (par * 2 + j) * TT:(par * 2 + j + 1) * TT],
                                    p=ps[:, (b0 + j) * 512:(b0 + j) * 512 + TT]:
                                    e.activation(o, p, AF.Sigmoid))
                    wait(prog, "vector", st, t2)
                    wait(prog, "vector", sems["c"], c_sig)
                    v_h = None
                    for j in range(2):
                        hh = gi * 2 + j
                        v_h = inc(prog, "vector", sv,
                                  lambda e, o=gact[:, hh * TT:(hh + 1) * TT],
                                  a=ssc[:, (par * 2 + j) * TT:(par * 2 + j + 1) * TT],
                                  p=ps[:, (b0 + 2 + j) * 512:(b0 + 2 + j) * 512 + TT]:
                                  e.tensor_tensor(o, a, p, ALU.mult))
                    ev2[par] = (sv, v_h)
                pre = (sv, v_h)

                def evac_x2(j_abs, psl):
                    v_ev[0] = inc(prog, "vector", sv,
                                  lambda e, o=xr[:, j_abs * TT:(j_abs + 1) * TT], p=psl,
                                  x=q[:, j_abs * TT:(j_abs + 1) * TT]:
                                  e.tensor_tensor(o, x, p, ALU.add))
                mm_streamed_emit(prog, sems, w3q, H_FFN, D, gact, ps, wbuf, evac_x2,
                                 pre_mm_wait=pre)
                v_x2 = v_ev[0]
                wait(prog, "sync", sv, v_x2)
                dx2 = x2sc.rearrange("(j p) t -> p j t", p=128)[:, :, c0_ - WARM:c1_ - WARM]
                inc(prog, "sync", so, lambda e, d=dx2:
                    e.dma_start(out=d, in_=xr.ap().rearrange("p (j t) -> p j t", j=NCH)), 16)
                v_ln3 = ln_tile_emit(prog, sems, xr, xln, gact, xxsc, t1sc, ps, ones_b, onesf, rows,
                                     off=(2048, 2560, 3072, 3584))
                v_xr_free = v_ln3
                wait(prog, "sync", sv, v_ln3)
                dln = ccin.rearrange("(j p) t -> p j t", p=128)[:, :, c0_ - WARM:c1_ - WARM]
                inc(prog, "sync", so, lambda e, d=dln:
                    e.dma_start(out=d, in_=gact.ap().rearrange("p (j t) -> p j t", t=TT)[:, 0:NCH, :]), 16)
                if debug and i == 1:
                    inc(prog, "sync", so, lambda e: e.dma_start(out=dbg_xl[:], in_=xln[:]), 16)
                    inc(prog, "sync", so, lambda e: e.dma_start(out=dbg_rw[:], in_=rows[:]), 16)
            V_ab = so.n

        wait(prog, "gpsimd", so, V_ab)
        inc(prog, "gpsimd", sg, lambda e: e.collective_compute(
            "AllGather", ALU.bypass, replica_groups=RG,
            ins=[ccin.opt()], outs=[ccout.opt()]))
        CC1 = sg.n

        with ExitStack() as ccx:
            sbc = lambda n, s, d: ccx.enter_context(nc.sbuf_tensor(n, s, d))
            wq16 = sbc("F_wq16", [128, NCH * 3072], BF16)
            xlnC = sbc("F_xlnC", [128, NCH * CHUNK], BF16)
            cosf = sbc("F_cosf", [128, 1024], F32)
            sinf = sbc("F_sinf", [128, 1024], F32)
            trg = sbc("F_trg", [128, 7 * 512], F32)
            cosq = sbc("F_cosq", [128, 1024], F32)
            sinq = sbc("F_sinq", [128, 1024], F32)
            cosk = sbc("F_cosk", [128, 1024], F32)
            sink = sbc("F_sink", [128, 1024], F32)
            qp_lo = sbc("F_qplo", [128, 1024], BF16)
            qp_hi = sbc("F_qphi", [128, 1024], BF16)
            qq_lo = sbc("F_qqlo", [128, 1024], BF16)
            qq_hi = sbc("F_qqhi", [128, 1024], BF16)
            kp_lo = sbc("F_kplo", [128, 1024], BF16)
            kp_hi = sbc("F_kphi", [128, 1024], BF16)
            kd_lo = sbc("F_kdlo", [128, 1024], BF16)
            kd_hi = sbc("F_kdhi", [128, 1024], BF16)
            vt = sbc("F_vt", [128, 1024], BF16)
            stf = [sbc(f"F_stf{i}", [128, 8 * 256], F32) for i in range(2)]
            st16 = sbc("F_st16", [128, 8 * 256], BF16)
            smask = sbc("F_smask", [128, 128], BF16)
            ktm = sbc("F_ktm", [128, 256], BF16)
            obuf = sbc("F_obuf", [128, 1024], BF16)
            theta = sbc("F_th", [1, 1024], F32)
            pos = sbc("F_pos_s", [1, CHUNK], F32)
            maskT = sbc("F_maskT_s", [128, 1024], F32)
            qdbc = sbc("F_qdbc_s", [128, 1024], F32)
            kdbc = sbc("F_kdbc_s", [128, 1024], F32)
            idb = sbc("F_idb", [128, 128], BF16)
            bdcol = sbc("F_bdcol_s", [128, 8], F32)
            psC = ccx.enter_context(nc.psum_tensor("F_psC", [128, 3584], F32))
            psB = ccx.enter_context(nc.psum_tensor("F_psB", [128, 256], BF16))

            for eng in ("sync", "gpsimd", "vector", "scalar", "tensor"):
                wait(prog, eng, so, V_ab)
            for (dst, src2) in ((theta, theta_d), (maskT, maskT_d),
                                (qdbc, qdbc_d), (kdbc, kdbc_d), (idb, identb_d),
                                (bdcol, bdcol_d)):
                inc(prog, "sync", sk, lambda e, d=dst, s=src2: e.dma_start(out=d[:], in_=s[:]), 16)
            inc(prog, "sync", sk, lambda e: e.dma_start(out=stf[0][:], in_=zf32_d[:]), 16)
            ldC = sk.n
            for kc in range(NCH):
                inc(prog, "gpsimd", sw, lambda e, kc=kc:
                    e.dma_start(out=wq16[:, kc * 3072:(kc + 1) * 3072],
                                in_=wqkvT[kc * 128:(kc + 1) * 128, :]), 16)
            inc(prog, "gpsimd", sw, lambda e: e.dma_start(out=st16[:], in_=zf32_d[:]), 16)
            ldCw = sw.n
            wait(prog, "tensor", sk, ldC)
            wait(prog, "vector", sk, ldC)
            wait(prog, "scalar", sk, ldC)
            wait(prog, "tensor", sw, ldCw)
            wait(prog, "vector", sw, ldCw)
            wait(prog, "sync", sg, CC1)

            sems["_thguard"] = None
            v_xln_free = None
            v_phase_c_last = None
            c_oevac_last = None
            c_ktm_last = None
            v_smask_last = None
            c_v = None
            for c in range(NCK):
                if c % 2 == 0 and conv_jobs:
                    cdst, csrc = conv_jobs.pop(0)
                    inc(prog, "gpsimd", sems["d"], lambda e, d=cdst, s=csrc:
                        e.dma_start(out=d[:, :], in_=s[:, :]), 16)
                cur, nxt = c % 2, (c + 1) % 2
                half, tl = c // 32, c % 32
                if v_xln_free is not None:
                    wait(prog, "sync", st, v_xln_free)
                src2 = ccout[half * D:(half + 1) * D, :].rearrange(
                    "(j p) t -> p j t", p=128)[:, :, tl * CHUNK:(tl + 1) * CHUNK]
                x_ld = inc(prog, "sync", sx, lambda e, s=src2:
                           e.dma_start(out=xlnC.ap().rearrange("p (j t) -> p j t", j=NCH), in_=s), 16)
                if sems.get("_thguard") is not None:
                    wait(prog, "sync", st, sems["_tlast"])
                pos_ld = inc(prog, "sync", sp, lambda e, c=c:
                             e.dma_start(out=pos[:], in_=pos_d[0:1, c * CHUNK:(c + 1) * CHUNK]), 16)
                for hh in range(2):
                    if hh == 0 and c > 0:
                        wait(prog, "tensor", sc, c_v)
                    if hh == 0:
                        wait(prog, "tensor", sp, pos_ld)
                    if sems.get("_thguard") is not None:
                        wait(prog, "tensor", sv, sems["_thguard"])
                    t_th = None
                    for i in range(4):
                        gi = hh * 4 + i
                        t_th = inc(prog, "tensor", st, lambda e, gi=gi, i=i:
                                   e.matmul(psC[:, 2048 + i * 128:2048 + (i + 1) * 128],
                                            theta[0:1, gi * 128:(gi + 1) * 128],
                                            pos[0:1, :], start=True, stop=True))
                    sems["_tlast"] = t_th
                    wait(prog, "vector", st, t_th)
                    wait(prog, "vector", sems["c"], sems["c"].n)
                    thp = psC[:, 2048:2560]
                    PI2I = float(1.0 / (2 * np.pi))
                    PI2 = float(2 * np.pi)
                    rsl = trg[:, (3 + 2 * hh) * 512:(4 + 2 * hh) * 512]
                    r2sl = trg[:, (4 + 2 * hh) * 512:(5 + 2 * hh) * 512]
                    inc(prog, "vector", sv, lambda e: e.tensor_scalar(trg[:, 0:512], thp, PI2I, None, ALU.mult))
                    inc(prog, "vector", sv, lambda e: e.tensor_scalar(trg[:, 512:1024], trg[:, 0:512], MAGIC, MAGIC, ALU.add, ALU.subtract))
                    inc(prog, "vector", sv, lambda e, rsl=rsl: e.scalar_tensor_tensor(rsl, trg[:, 512:1024], -PI2, thp, ALU.mult, ALU.add))
                    inc(prog, "vector", sv, lambda e: e.tensor_scalar(trg[:, 512:1024], trg[:, 0:512], 0.25, MAGIC, ALU.add, ALU.add))
                    inc(prog, "vector", sv, lambda e: e.tensor_scalar(trg[:, 1024:1536], trg[:, 512:1024], MAGIC, None, ALU.subtract))
                    v_trg = inc(prog, "vector", sv, lambda e, r2sl=r2sl: e.scalar_tensor_tensor(r2sl, trg[:, 1024:1536], -PI2, thp, ALU.mult, ALU.add))
                    sems["_thguard"] = v_trg
                    PIC = float(np.nextafter(np.float32(np.pi), np.float32(0.0)))
                    inc(prog, "vector", sv, lambda e, rsl=rsl: e.tensor_scalar(trg[:, 0:512], rsl, PIC, -PIC, ALU.min, ALU.max))
                    inc(prog, "vector", sv, lambda e, r2sl=r2sl: e.tensor_scalar(trg[:, 1024:1536], r2sl, float(np.pi / 2), PIC, ALU.add, ALU.min))
                    v_trg = inc(prog, "vector", sv, lambda e: e.tensor_scalar(trg[:, 512:1024], trg[:, 1024:1536], -PIC, None, ALU.max))
                    wait(prog, "scalar", sv, v_trg)
                    inc(prog, "scalar", sc, lambda e, hh=hh: e.activation(sinf[:, hh * 512:(hh + 1) * 512], trg[:, 0:512], AF.Sin))
                    c_trig = inc(prog, "scalar", sc, lambda e, hh=hh: e.activation(cosf[:, hh * 512:(hh + 1) * 512], trg[:, 512:1024], AF.Sin))
                wait(prog, "vector", sc, c_trig)
                inc(prog, "vector", sv, lambda e: e.tensor_tensor(cosq[:], cosf[:], qdbc[:], ALU.mult))
                inc(prog, "vector", sv, lambda e: e.tensor_tensor(sinq[:], sinf[:], qdbc[:], ALU.mult))
                inc(prog, "vector", sv, lambda e: e.tensor_tensor(cosk[:], cosf[:], kdbc[:], ALU.mult))
                v_trig_all = inc(prog, "vector", sv, lambda e: e.tensor_tensor(sink[:], sinf[:], kdbc[:], ALU.mult))

                wait(prog, "tensor", sv, v_trig_all)
                wait(prog, "tensor", sx, x_ld)
                if c_oevac_last is not None:
                    wait(prog, "tensor", sc, c_oevac_last)
                for i in range(8):
                    for kc in range(NCH):
                        run_op(prog, "tensor", lambda e, kc=kc, i=i:
                               e.matmul(psC[:, i * 128:(i + 1) * 128],
                                        wq16[:, kc * 3072 + i * 128: kc * 3072 + (i + 1) * 128],
                                        xlnC[:, kc * CHUNK:(kc + 1) * CHUNK],
                                        start=(kc == 0), stop=(kc == NCH - 1)))
                for i in range(8):
                    for kc in range(NCH):
                        run_op(prog, "tensor", lambda e, kc=kc, i=i:
                               e.matmul(psC[:, 1024 + i * 128:1024 + (i + 1) * 128],
                                        wq16[:, kc * 3072 + 1024 + i * 128: kc * 3072 + 1024 + (i + 1) * 128],
                                        xlnC[:, kc * CHUNK:(kc + 1) * CHUNK],
                                        start=(kc == 0), stop=(kc == NCH - 1)))
                t_qkv = None
                for i in range(8):
                    for kc in range(NCH):
                        is_last = (kc == NCH - 1) and (i == 7)
                        bld = (lambda e, kc=kc, i=i:
                               e.matmul(psC[:, 2048 + i * 128:2048 + (i + 1) * 128],
                                        xlnC[:, kc * CHUNK:(kc + 1) * CHUNK],
                                        wq16[:, kc * 3072 + 2048 + i * 128: kc * 3072 + 2048 + (i + 1) * 128],
                                        start=(kc == 0), stop=(kc == NCH - 1)))
                        if is_last:
                            t_qkv = inc(prog, "tensor", st, bld)
                        else:
                            run_op(prog, "tensor", bld)
                v_xln_free = t_qkv

                wait(prog, "vector", st, t_qkv)
                qps = psC[:, 0:1024]
                kps = psC[:, 1024:2048]
                inc(prog, "vector", sv, lambda e: e.tensor_tensor(qp_lo[:], qps, cosf[:], ALU.mult))
                inc(prog, "vector", sv, lambda e: e.tensor_tensor(qp_hi[:], qps, sinf[:], ALU.mult))
                inc(prog, "vector", sv, lambda e: e.tensor_tensor(qq_lo[:], qps, cosq[:], ALU.mult))
                inc(prog, "vector", sv, lambda e: e.tensor_tensor(qq_hi[:], qps, sinq[:], ALU.mult))
                inc(prog, "vector", sv, lambda e: e.tensor_tensor(kp_lo[:], kps, cosf[:], ALU.mult))
                inc(prog, "vector", sv, lambda e: e.tensor_tensor(kp_hi[:], kps, sinf[:], ALU.mult))
                inc(prog, "vector", sv, lambda e: e.tensor_tensor(kd_lo[:], kps, cosk[:], ALU.mult))
                v_phase_c_last = inc(prog, "vector", sv, lambda e: e.tensor_tensor(kd_hi[:], kps, sink[:], ALU.mult))
                wait(prog, "scalar", st, t_qkv)
                c_v = inc(prog, "scalar", sc, lambda e: e.activation(vt[:], psC[:, 2048:3072], AF.Copy))

                for i in range(8):
                    if v_smask_last is not None:
                        wait(prog, "tensor", sv, v_smask_last)
                    wait(prog, "tensor", sv, v_phase_c_last)
                    wait(prog, "tensor", sc, c_v)
                    run_op(prog, "tensor", lambda e, i=i:
                           e.matmul(psC[:, 0:128], kp_lo[:, i * 128:(i + 1) * 128],
                                    qp_lo[:, i * 128:(i + 1) * 128], start=True, stop=False))
                    t_s = inc(prog, "tensor", st, lambda e, i=i:
                              e.matmul(psC[:, 0:128], kp_hi[:, i * 128:(i + 1) * 128],
                                       qp_hi[:, i * 128:(i + 1) * 128], start=False, stop=True))
                    wait(prog, "vector", st, t_s)
                    v_smask_last = inc(prog, "vector", sv, lambda e, i=i:
                                       e.tensor_tensor(smask[:], psC[:, 0:128], maskT[:, i * 128:(i + 1) * 128], ALU.mult))
                    if c_oevac_last is not None:
                        wait(prog, "tensor", sc, c_oevac_last)
                    wait(prog, "tensor", sv, v_smask_last)
                    v_c16 = inc(prog, "vector", sv, lambda e, i=i, cur=cur:
                                e.tensor_copy(st16[:, i * 256:(i + 1) * 256],
                                              stf[cur][:, i * 256:(i + 1) * 256]))
                    wait(prog, "tensor", sv, v_c16)
                    run_op(prog, "tensor", lambda e, i=i:
                           e.matmul(psC[:, 512:640], vt[:, i * 128:(i + 1) * 128], smask[:], start=True, stop=False))
                    run_op(prog, "tensor", lambda e, i=i:
                           e.matmul(psC[:, 512:640], st16[:, i * 256:i * 256 + 128],
                                    qq_lo[:, i * 128:(i + 1) * 128], start=False, stop=False))
                    t_o = inc(prog, "tensor", st, lambda e, i=i:
                              e.matmul(psC[:, 512:640], st16[:, i * 256 + 128:i * 256 + 256],
                                       qq_hi[:, i * 128:(i + 1) * 128], start=False, stop=True))
                    wait(prog, "scalar", st, t_o)
                    c_oevac_last = inc(prog, "scalar", sc, lambda e, i=i:
                                       e.activation(obuf[:, i * 128:(i + 1) * 128], psC[:, 512:640], AF.Copy))
                    if c_ktm_last is not None:
                        wait(prog, "tensor", sc, c_ktm_last)
                    run_op(prog, "tensor", lambda e, i=i:
                           e.transpose(psB[:, 0:128], kd_lo[:, i * 128:(i + 1) * 128], idb[:]))
                    t_tp = inc(prog, "tensor", st, lambda e, i=i:
                               e.transpose(psB[:, 128:256], kd_hi[:, i * 128:(i + 1) * 128], idb[:]))
                    wait(prog, "scalar", st, t_tp)
                    c_ktm_last = inc(prog, "scalar", sc, lambda e: e.activation(ktm[:], psB[:], AF.Copy))
                    wait(prog, "tensor", sv, sv.n)
                    wait(prog, "tensor", sc, c_ktm_last)
                    run_op(prog, "tensor", lambda e, i=i:
                           e.matmul(psC[:, 1024:1152], ktm[:, 0:128], vt[:, i * 128:(i + 1) * 128],
                                    start=True, stop=True))
                    t_c2 = inc(prog, "tensor", st, lambda e, i=i:
                               e.matmul(psC[:, 1536:1664], ktm[:, 128:256], vt[:, i * 128:(i + 1) * 128],
                                        start=True, stop=True))
                    wait(prog, "vector", st, t_c2)
                    inc(prog, "vector", sv, lambda e, i=i, cur=cur, nxt=nxt:
                        e.scalar_tensor_tensor(stf[nxt][:, i * 256:i * 256 + 128],
                                               stf[cur][:, i * 256:i * 256 + 128], bdcol[:, i:i + 1],
                                               psC[:, 1024:1152], ALU.mult, ALU.add))
                    inc(prog, "vector", sv, lambda e, i=i, cur=cur, nxt=nxt:
                        e.scalar_tensor_tensor(stf[nxt][:, i * 256 + 128:i * 256 + 256],
                                               stf[cur][:, i * 256 + 128:i * 256 + 256], bdcol[:, i:i + 1],
                                               psC[:, 1536:1664], ALU.mult, ALU.add))
                wait(prog, "sync", sc, c_oevac_last)
                dsto = ocin.rearrange("(i p) t -> p i t", p=128)[:, :, c * CHUNK:(c + 1) * CHUNK]
                so_last = inc(prog, "sync", so, lambda e, d=dsto:
                              e.dma_start(out=d, in_=obuf.ap().rearrange("p (i t) -> p i t", t=CHUNK)), 16)
                wait(prog, "scalar", so, so_last)
            assert not conv_jobs
            SD_ALL = sems["d"].n
            V_c = so.n

        wait(prog, "gpsimd", so, V_c)
        inc(prog, "gpsimd", sg, lambda e: e.collective_compute(
            "AllGather", ALU.bypass, replica_groups=RG,
            ins=[ocin.opt()], outs=[ocout.opt()]))
        CC2 = sg.n

        with ExitStack() as cex:
            sbe = lambda n, s, d: cex.enter_context(nc.sbuf_tensor(n, s, d))
            wbuf = [sbe(f"F_wbE{i}", [128, 4 * 1024], BF16) for i in range(2)]
            xr = sbe("F_xrE", [128, NCH * TT], F32)
            ob = sbe("F_ob", [128, NCH * TT], BF16)
            ob0 = sbe("F_ob0", [128, NCH * TT], BF16)
            ob1 = sbe("F_ob1", [128, NCH * TT], BF16)
            msel = sbe("F_msel_s", [128, 2], F32)
            x3r = sbe("F_x3r", [128, NCH * TT], F32)
            xln = sbe("F_xlnE", [128, NCH * TT], BF16)
            hbuf = sbe("F_h", [128, NCH * TT], BF16)
            ssc = sbe("F_sscE", [128, 4 * TT], F32)
            xxsc = sbe("F_xxscE", [128, 2 * TT], BF16)
            t1sc = sbe("F_t1scE", [128, TT], F32)
            wr16 = sbe("F_wr16", [128, NCH * 8], BF16)
            lgt = sbe("F_lgt", [128, 14 * 8], F32)
            wgt8 = sbe("F_wgt8", [8, 4 * 128], F32)
            wbc16 = sbe("F_wbc", [128, TT], BF16)
            idf = sbe("F_idf", [128, 128], F32)
            oh8 = sbe("F_oh8_s", [8, 8 * 128], F32)
            rows = sbe("F_rowsE", [1, 5 * 512], F32)
            ps = cex.enter_context(nc.psum_tensor("F_psE", [128, 4096], F32))

            for eng in ("sync", "gpsimd", "vector", "scalar", "tensor"):
                wait(prog, eng, so, V_c)
            for (dst, src2) in ((idf, identf_d), (oh8, oh8_d), (msel, msel_d)):
                inc(prog, "sync", sk, lambda e, d=dst, s=src2: e.dma_start(out=d[:], in_=s[:]), 16)
            ldE = sk.n
            inc(prog, "gpsimd", sw, lambda e:
                e.dma_start(out=wr16.ap().rearrange("p (k e) -> p k e", e=8),
                            in_=wr_d.rearrange("(k p) e -> p k e", p=128)), 16)
            ldEw = sw.n
            wait(prog, "tensor", sk, ldE)
            wait(prog, "vector", sk, ldE)
            wait(prog, "scalar", sk, ldE)
            wait(prog, "tensor", sw, ldEw)
            wait(prog, "sync", sg, CC2)
            wait(prog, "gpsimd", sems["d"], SD_ALL)

            v_tile_free = None
            for i in range(NTI):
                c0_, c1_ = i * TT, (i + 1) * TT
                if v_tile_free is not None:
                    wait(prog, "sync", sv, v_tile_free)
                srcx = x2sc.rearrange("(j p) t -> p j t", p=128)[:, :, c0_:c1_]
                x_ld = inc(prog, "sync", sx, lambda e, s=srcx:
                           e.dma_start(out=xr.ap().rearrange("p (j t) -> p j t", j=NCH), in_=s), 16)
                srco = ocout.rearrange("(j p) t -> p j t", p=128)
                inc(prog, "sync", sx, lambda e, s=srco, c0_=c0_, c1_=c1_:
                    e.dma_start(out=ob0.ap().rearrange("p (j t) -> p j t", j=NCH),
                                in_=s[:, :, c0_:c1_]), 16)
                o_ld = inc(prog, "sync", sx, lambda e, s=srco, c0_=c0_, c1_=c1_:
                           e.dma_start(out=ob1.ap().rearrange("p (j t) -> p j t", j=NCH),
                                       in_=s[:, :, SH + c0_:SH + c1_]), 16)
                wait(prog, "vector", sx, o_ld)
                if i > 0:
                    wait(prog, "vector", so, so.n)
                inc(prog, "vector", sv, lambda e: e.tensor_scalar(ob[:], ob0[:], msel[:, 0:1], None, ALU.mult))
                inc(prog, "vector", sv, lambda e: e.scalar_tensor_tensor(ob0[:], ob1[:], msel[:, 1:2], ob[:], ALU.mult, ALU.add))
                v_ev = [None]

                def evac_x3(j_abs, psl):
                    v_ev[0] = inc(prog, "vector", sv,
                                  lambda e, o=x3r[:, j_abs * TT:(j_abs + 1) * TT], p=psl,
                                  x=xr[:, j_abs * TT:(j_abs + 1) * TT]:
                                  e.tensor_tensor(o, x, p, ALU.add))
                wait(prog, "tensor", sx, o_ld)
                mm_streamed_emit(prog, sems, wproj16, D, D, ob0, ps, wbuf, evac_x3,
                                 pre_mm_wait=(sv, sv.n))
                v_ln4 = ln_tile_emit(prog, sems, x3r, xln, xln, xxsc, t1sc, ps, ones_b, onesf, rows)
                wait(prog, "tensor", sv, v_ln4)
                c_w = None
                t_tr = None
                for tsub in range(4):
                    wait(prog, "tensor", sv, sv.n)
                    wait(prog, "tensor", sems["c"], sems["c"].n)
                    if t_tr is not None:
                        wait(prog, "vector", st, t_tr)
                    lgp = ps[0:128, 2048 + tsub * 8: 2056 + tsub * 8]
                    t_lg = None
                    for kc in range(NCH):
                        t_lg = inc(prog, "tensor", st, lambda e, kc=kc, tsub=tsub, lgp=lgp:
                                   e.matmul(lgp, xln[:, kc * TT + tsub * 128: kc * TT + (tsub + 1) * 128],
                                            wr16[:, kc * 8:(kc + 1) * 8],
                                            start=(kc == 0), stop=(kc == NCH - 1)))
                    wait(prog, "vector", st, t_lg)
                    L = lambda k: lgt[:, k * 8:(k + 1) * 8]
                    C1 = lambda k: lgt[:, 96 + k:96 + k + 1]
                    inc(prog, "vector", sv, lambda e, lgp=lgp: e.tensor_copy(L(0), lgp))
                    inc(prog, "vector", sv, lambda e: e.reduce_max(C1(0), L(0), AX.X))
                    v_sm = inc(prog, "vector", sv, lambda e: e.tensor_scalar(L(1), L(0), C1(0), None, ALU.subtract))
                    wait(prog, "scalar", sv, v_sm)
                    c_ex = inc(prog, "scalar", sc, lambda e: e.activation(L(2), L(1), AF.Exp))
                    wait(prog, "vector", sc, c_ex)
                    inc(prog, "vector", sv, lambda e: e.reduce_sum(C1(1), L(2), AX.X))
                    inc(prog, "vector", sv, lambda e: e.reciprocal(C1(2), C1(1)))
                    inc(prog, "vector", sv, lambda e: e.tensor_scalar(L(3), L(2), C1(2), None, ALU.mult))
                    inc(prog, "vector", sv, lambda e: e.reduce_max(C1(3), L(3), AX.X))
                    inc(prog, "vector", sv, lambda e: e.tensor_scalar(L(4), L(3), C1(3), None, ALU.is_ge))
                    inc(prog, "vector", sv, lambda e: e.tensor_scalar(L(5), L(4), -1.0, 1.0, ALU.mult, ALU.add))
                    inc(prog, "vector", sv, lambda e: e.tensor_tensor(L(6), L(3), L(5), ALU.mult))
                    inc(prog, "vector", sv, lambda e: e.reduce_max(C1(4), L(6), AX.X))
                    inc(prog, "vector", sv, lambda e: e.tensor_scalar(L(7), L(3), C1(4), None, ALU.is_ge))
                    inc(prog, "vector", sv, lambda e: e.tensor_tensor(L(8), L(3), L(7), ALU.mult))
                    inc(prog, "vector", sv, lambda e: e.tensor_tensor(C1(5), C1(3), C1(4), ALU.add))
                    inc(prog, "vector", sv, lambda e: e.reciprocal(C1(6), C1(5)))
                    v_wn = inc(prog, "vector", sv, lambda e: e.tensor_scalar(L(9), L(8), C1(6), None, ALU.mult))
                    wait(prog, "tensor", sv, v_wn)
                    wnp = ps[:, 2176:2304]
                    t_tr = inc(prog, "tensor", st, lambda e: e.transpose(wnp[0:8, 0:128], L(9), idf[:]))
                    wait(prog, "scalar", st, t_tr)
                    c_w = inc(prog, "scalar", sc, lambda e, tsub=tsub:
                              e.activation(wgt8[0:8, tsub * 128:(tsub + 1) * 128], wnp[0:8, 0:128], AF.Copy))
                wait(prog, "vector", sems["c"], sems["c"].n)

                for ex in range(8):
                    wait(prog, "tensor", sems["c"], sems["c"].n)
                    wait(prog, "tensor", sv, sv.n)
                    wbp = ps[:, 2560:2560 + TT]
                    t_wb = inc(prog, "tensor", st, lambda e, ex=ex:
                               e.matmul(wbp, oh8[0:8, ex * 128:(ex + 1) * 128], wgt8[0:8, :], start=True, stop=True))
                    wait(prog, "scalar", st, t_wb)
                    c_wb = inc(prog, "scalar", sc, lambda e: e.activation(wbc16[:], wbp, AF.Copy))
                    wait(prog, "vector", sems["c"], c_wb)
                    pre = (sv, sv.n)
                    ev2 = [pre, pre]
                    for gi in range(8):
                        par = gi % 2
                        b0 = par * 4
                        t1 = mm_group_emit(prog, sems, wg16[ex], D, gi * 256, 2, xln, ps, b0, wbuf,
                                           pre_mm_wait=ev2[par])
                        t2 = mm_group_emit(prog, sems, wu16[ex], D, gi * 256, 2, xln, ps, b0 + 2, wbuf,
                                           pre_mm_wait=(sems["c"], c_wb) if gi <= 1 else None)
                        wait(prog, "scalar", st, t1)
                        wait(prog, "scalar", sv, sv.n)
                        c_sig = None
                        for j in range(2):
                            c_sig = inc(prog, "scalar", sems["c"],
                                        lambda e, o=ssc[:, (par * 2 + j) * TT:(par * 2 + j + 1) * TT],
                                        p=ps[:, (b0 + j) * 512:(b0 + j) * 512 + TT]:
                                        e.activation(o, p, AF.Silu))
                        wait(prog, "vector", st, t2)
                        wait(prog, "vector", sems["c"], c_sig)
                        v_h = None
                        for j in range(2):
                            hh = gi * 2 + j
                            inc(prog, "vector", sv,
                                lambda e, o=t1sc[:], a=ssc[:, (par * 2 + j) * TT:(par * 2 + j + 1) * TT],
                                p=ps[:, (b0 + 2 + j) * 512:(b0 + 2 + j) * 512 + TT]:
                                e.tensor_tensor(o, a, p, ALU.mult))
                            v_h = inc(prog, "vector", sv,
                                      lambda e, o=hbuf[:, hh * TT:(hh + 1) * TT], a=t1sc[:]:
                                      e.tensor_tensor(o, a, wbc16[:], ALU.mult))
                        ev2[par] = (sv, v_h)
                    pre = (sv, v_h)
                    src_y = x3r if ex == 0 else (xr if ex % 2 == 1 else x3r)
                    dst_y = xr if ex % 2 == 0 else x3r
                    v_ev2 = [None]

                    def evac_y(j_abs, psl, src_y=src_y, dst_y=dst_y):
                        v_ev2[0] = inc(prog, "vector", sv,
                                       lambda e, o=dst_y[:, j_abs * TT:(j_abs + 1) * TT], p=psl,
                                       x=src_y[:, j_abs * TT:(j_abs + 1) * TT]:
                                       e.tensor_tensor(o, x, p, ALU.add))
                    mm_streamed_emit(prog, sems, wd16[ex], D, D, hbuf, ps, wbuf, evac_y,
                                     pre_mm_wait=pre)
                v_tile_free = v_ev2[0]
                wait(prog, "sync", sv, v_tile_free)
                dst = xout.rearrange("(j p) t -> p j t", p=128)[:, :, c0_:c1_]
                inc(prog, "sync", so, lambda e, d=dst:
                    e.dma_start(out=d, in_=x3r.ap().rearrange("p (j t) -> p j t", j=NCH)), 16)
            wait(prog, "sync", so, so.n)
            if debug:
                inc(prog, "sync", so, lambda e: e.dma_start(out=dbg_ci[:, :], in_=ccin[:, :]), 16)
                inc(prog, "sync", so, lambda e: e.dma_start(out=dbg_x2[:, :], in_=x2sc[:, :]), 16)
                inc(prog, "sync", so, lambda e: e.dma_start(out=dbg_cc[:, :], in_=ccout[:, :]), 16)
                inc(prog, "sync", so, lambda e: e.dma_start(out=dbg_oc[:, :], in_=ocout[:, :]), 16)
                wait(prog, "sync", so, so.n)
        prog.run(block)
    return nc


def host_prep_fused(inputs):
    x = inputs["L0_attn_x"]
    winT_h = np.ascontiguousarray(inputs["L0_attn_W_in"].T.astype(f32))
    woutT_h = np.ascontiguousarray(inputs["L0_attn_W_out"].T.astype(f32))
    dcol_h = np.ascontiguousarray(inputs["L0_attn_decay"].reshape(NCH, 128).T.astype(f32))
    bcol_h = np.ascontiguousarray(inputs["L0_attn_bias"].reshape(NCH, 128).T.astype(f32))
    w1Th = np.ascontiguousarray(inputs["L0_ffn_w1"].T.astype(f32))
    w2Th = np.ascontiguousarray(inputs["L0_ffn_w2"].T.astype(f32))
    w3Th = np.ascontiguousarray(inputs["L0_ffn_w3"].T.astype(f32))
    w_qkv = inputs["L1_attn_w_qkv"].astype(f32)
    theta = inputs["L1_attn_theta"].astype(f32).reshape(NH, EH)
    wprojT_h = np.ascontiguousarray(inputs["L1_attn_w_proj"].T.astype(f32))
    wr_h = np.ascontiguousarray(inputs["L1_ffn_Wr"].astype(f32))
    wg_h = np.ascontiguousarray(inputs["L1_ffn_Wg"].astype(f32))
    wu_h = np.ascontiguousarray(inputs["L1_ffn_Wu"].astype(f32))
    wd_h = np.ascontiguousarray(inputs["L1_ffn_Wd"].astype(f32))
    slopes = _slopes16()
    idx = np.arange(CHUNK, dtype=np.float64)
    diff = idx[:, None] - idx[None, :]
    oh8_h = np.zeros((8, 8 * 128), f32)
    for ex in range(8):
        oh8_h[ex, ex * 128:(ex + 1) * 128] = 1.0
    onesf_h = np.ones((1, 128), f32)
    onesb_h = np.ones((128, 1), ml_dtypes.bfloat16)
    onesc_h = np.ones((128, 1), f32)
    identb_h = np.eye(128, dtype=ml_dtypes.bfloat16)
    identf_h = np.eye(128, dtype=f32)
    pos_h = np.arange(S, dtype=f32).reshape(1, S)
    zf32_h = np.zeros((128, 2048), f32)
    in_maps = []
    for core in range(8):
        b, g = core // 2, core % 2
        t0 = g * SH
        xs = np.empty((D, WARM + SH), f32)
        if g == 0:
            xs[:, :WARM] = 0.0
        else:
            xs[:, :WARM] = x[b, t0 - WARM:t0, :].T
        xs[:, WARM:] = x[b, t0:t0 + SH, :].T
        wgq = np.concatenate([w_qkv[g * 1024:(g + 1) * 1024],
                              w_qkv[2048 + g * 1024:2048 + (g + 1) * 1024],
                              w_qkv[4096 + g * 1024:4096 + (g + 1) * 1024]], axis=0)
        wqkvT_h = np.ascontiguousarray(wgq.T)
        theta_h = np.ascontiguousarray(theta[g * 8:(g + 1) * 8].reshape(1, 1024))
        maskT_h = np.empty((128, 1024), f32)
        qdbc_h = np.empty((128, 1024), f32)
        kdbc_h = np.empty((128, 1024), f32)
        for i in range(8):
            s = float(slopes[g * 8 + i])
            m = np.where(diff >= 0, np.exp(np.float32(-s) * diff.astype(f32)), 0.0).astype(f32)
            maskT_h[:, i * 128:(i + 1) * 128] = m.T
            qdbc_h[:, i * 128:(i + 1) * 128] = np.exp(np.float32(-s) * (idx + 1.0).astype(f32))[None, :]
            kdbc_h[:, i * 128:(i + 1) * 128] = np.exp(np.float32(-s) * (CHUNK - 1.0 - idx).astype(f32))[None, :]
        bdcol_h = np.broadcast_to(
            np.exp(-slopes[g * 8:(g + 1) * 8].astype(f32) * np.float32(CHUNK))[None, :],
            (128, 8)).astype(f32)
        msel_h = np.zeros((128, 2), f32)
        msel_h[:, g] = 1.0
        in_maps.append({
            "F_msel": msel_h,
            "F_x": np.ascontiguousarray(xs), "F_winT": winT_h, "F_woutT": woutT_h,
            "F_dcol": dcol_h, "F_bcol": bcol_h, "F_w1T": w1Th, "F_w2T": w2Th,
            "F_w3T": w3Th, "F_wqkvT": wqkvT_h, "F_theta": theta_h, "F_pos": pos_h,
            "F_maskT": maskT_h, "F_qdbc": qdbc_h, "F_kdbc": kdbc_h,
            "F_identb": identb_h, "F_identf": identf_h,
            "F_bdcol": np.ascontiguousarray(bdcol_h), "F_zf32": zf32_h,
            "F_wprojT": wprojT_h, "F_wr": wr_h, "F_wg": wg_h, "F_wu": wu_h,
            "F_wd": wd_h, "F_oh8": oh8_h, "F_onesf": onesf_h, "F_onesb": onesb_h,
            "F_onesc": onesc_h})
    return in_maps


def kernel(**inputs):
    nc = build_fused()
    in_maps = host_prep_fused(inputs)
    res = run_bass_kernel_spmd(nc, in_maps, core_ids=list(range(8)))
    full = np.empty((B, S, D), f32)
    for c in range(8):
        b, h = c // 2, c % 2
        full[b, h * SH:(h + 1) * SH, :] = res.results[c]["F_out"].T
    return full


def kernel_unfused(**inputs):
    x1 = run_launchA(inputs)
    x2 = run_launchB(inputs, x1)
    oc = run_launchC(inputs, x2)
    out = run_launchD(inputs, x2, oc)
    full = np.empty((B, S, D), f32)
    for c in range(8):
        b, h = c // 2, c % 2
        full[b, h * SH:(h + 1) * SH, :] = out[c].T
    return full
